# version 01
# baseline (speedup 1.0000x reference)
import numpy as np
import ml_dtypes
import concourse.bass as bass
import concourse.mybir as mybir
from concourse.bass_utils import run_bass_kernel_spmd
from concourse.alu_op_type import AluOpType as OP

F32 = mybir.dt.float32
BF16 = mybir.dt.bfloat16
I32 = mybir.dt.int32
AF = mybir.ActivationFunctionType
f32 = np.float32

ABF_N = 50176
AF32_N = 8192
SLOT = 512
MAGIC1 = 0x12AB34CD
MAGIC2 = 0x56EF78A1
INV_SQRT_HD = f32(1.0 / np.sqrt(128.0))
EXP_SHIFT = f32(-20.0)

PHASES = (1, 2, 3, 4, 5)
SPARSIFY = True
DBG = None

ENGINES = ("sync", "gpsimd", "scalar", "vector", "tensor")

_seq = [0]


class Ev:
    __slots__ = ("eng", "val", "waited", "seq")

    def __init__(self, eng):
        self.eng = eng
        self.val = None
        self.waited = False
        _seq[0] += 1
        self.seq = _seq[0]


class Res:
    __slots__ = ("name", "w", "rs")

    def __init__(self, name=""):
        self.name = name
        self.w = None
        self.rs = []


class Step:
    __slots__ = ("eng", "emit", "deps", "ev", "ndma", "tag", "signal")

    def __init__(self, eng, emit, deps, ev, ndma, tag):
        self.eng = eng
        self.emit = emit
        self.deps = deps
        self.ev = ev
        self.ndma = ndma
        self.tag = tag
        self.signal = False


class Prog:
    def __init__(self):
        self.steps = {e: [] for e in ENGINES}

    def add(self, eng, emit, reads=(), writes=(), ndma=0, tag=""):
        ev = Ev(eng)
        deps = {}
        for r in reads:
            if r.w is not None:
                deps[r.w.seq] = r.w
        for r in writes:
            if r.w is not None:
                deps[r.w.seq] = r.w
            for e in r.rs:
                deps[e.seq] = e
        deps.pop(ev.seq, None)
        for r in reads:
            r.rs.append(ev)
        for r in writes:
            r.w = ev
            r.rs = []
        final = []
        for d in sorted(deps.values(), key=lambda d: d.seq):
            if d.eng == eng == "tensor":
                continue
            final.append(d)
            d.waited = True
        self.steps[eng].append(Step(eng, emit, final, ev, ndma, tag))
        return ev

    def plan(self):
        for eng in ENGINES:
            cnt = 0
            for st in self.steps[eng]:
                if st.ev.waited or st.ndma:
                    st.signal = True
                    cnt += 16 * st.ndma if st.ndma else 1
                    st.ev.val = cnt

    def emit_engine(self, eng_name, eng_obj, sems):
        water = {}
        for st in self.steps[eng_name]:
            need = {}
            for d in st.deps:
                assert d.val is not None, f"dep of {st.tag} not planned"
                need[d.eng] = max(need.get(d.eng, 0), d.val)
            for deng, val in sorted(need.items()):
                if water.get(deng, 0) >= val:
                    continue
                eng_obj.wait_ge(sems[deng], val)
                water[deng] = val
            out = st.emit(eng_obj) if st.emit is not None else None
            if st.signal:
                assert out is not None, f"step {st.tag} must emit to signal"
                instrs = out if isinstance(out, (list, tuple)) else [out]
                if st.ndma:
                    assert len(instrs) == st.ndma, \
                        f"{st.tag}: ndma={st.ndma} got {len(instrs)}"
                    for i in instrs:
                        i.then_inc(sems[eng_name], 16)
                else:
                    instrs[-1].then_inc(sems[eng_name], 1)



def _own_blocks(par):
    keep = (0, 3) if par == 0 else (1, 2)
    return [g for g in range(32) if g % 4 in keep]


def _prep_core(inputs, core):
    seq, par = core // 2, core % 2
    bf = ml_dtypes.bfloat16
    x = np.asarray(inputs["L0_attn_a_x"][seq])
    xT = np.ascontiguousarray(x.T)
    own = _own_blocks(par)
    own_tok = np.concatenate([np.arange(128 * g, 128 * g + 128) for g in own])
    x_own = np.ascontiguousarray(xT[:, own_tok])

    def lhs_stream(W):
        return np.ascontiguousarray(
            W.reshape(16, 128, 16, 128).transpose(0, 3, 2, 1))

    def mov_stream(W):
        return np.ascontiguousarray(W.T.reshape(16, 128, 2048))

    Wqkv = np.asarray(inputs["L0_attn_a_W_qkv"])
    Wq, Wk, Wv = Wqkv[0:2048], Wqkv[2048:4096], Wqkv[4096:6144]
    Wo = np.asarray(inputs["L0_attn_a_W_out"])
    Wqkvb = np.asarray(inputs["L0_attn_b_w_qkv"])
    Wqb, Wkb, Wvb = Wqkvb[0:2048], Wqkvb[2048:4096], Wqkvb[4096:6144]
    Wpb = np.asarray(inputs["L0_attn_b_w_proj"])
    bpb = np.asarray(inputs["L0_attn_b_b_proj"])
    Wg = np.asarray(inputs["L0_ffn_w_gate"])
    Wu = np.asarray(inputs["L0_ffn_w_up"])
    Wd = np.asarray(inputs["L0_ffn_w_down"])

    def gate_stream(W):
        return np.ascontiguousarray(
            W.reshape(16, 128, 16, 128, 4).transpose(2, 4, 1, 0, 3))

    wd_h = np.ascontiguousarray(
        Wd.reshape(16, 128, 4, 16, 128).transpose(3, 1, 0, 2, 4)
    ).reshape(16, 128, 64, 128)

    masks = np.zeros((4, 8, 128, 512), dtype=np.float32)
    for t in range(4):
        qg = [own[4 * t + i] for i in range(4)]
        qpos = np.concatenate([128 * g + np.arange(128) for g in qg])
        for j in range(8):
            kpos = 128 * (8 * t + j) + np.arange(128)
            masks[t, j] = (kpos[:, None] <= qpos[None, :]).astype(np.float32)

    bd = np.zeros((128, 128), dtype=np.float32)
    for i in range(4):
        bd[32 * i:32 * i + 32, 32 * i:32 * i + 32] = 1.0

    return {
        "x_own": x_own,
        "x_full": xT,
        "wq": lhs_stream(Wq), "wk": lhs_stream(Wk), "wo": lhs_stream(Wo),
        "wv": mov_stream(Wv),
        "wqb": lhs_stream(Wqb), "wpb": lhs_stream(Wpb),
        "wkb": mov_stream(Wkb), "wvb": mov_stream(Wvb),
        "wg": gate_stream(Wg), "wu": gate_stream(Wu), "wd": wd_h,
        "biasb": np.ascontiguousarray(bpb.reshape(16, 128).T),
        "masks": masks.astype(bf),
        "bdones_f": bd,
        "bdones_b": bd.astype(bf),
        "parity": np.array([[par]], dtype=np.int32),
    }



class Ctx:
    pass


def _dap(t, off, dims):
    return bass.AP(t, off, dims)


def build_nc(phases=PHASES):
    nc = bass.Bass("TRN2", target_bir_lowering=False)
    C = Ctx()
    C.nc = nc
    P = Prog()
    C.P = P

    di = {}
    di["x_own"] = nc.dram_tensor("x_own", [2048, 2048], F32, kind="ExternalInput")
    di["x_full"] = nc.dram_tensor("x_full", [2048, 4096], F32, kind="ExternalInput")
    for n in ("wq", "wk", "wo", "wqb", "wpb"):
        di[n] = nc.dram_tensor(n, [16, 128, 16, 128], F32, kind="ExternalInput")
    for n in ("wv", "wkb", "wvb"):
        di[n] = nc.dram_tensor(n, [16, 128, 2048], F32, kind="ExternalInput")
    di["wg"] = nc.dram_tensor("wg", [16, 4, 128, 16, 128], F32, kind="ExternalInput")
    di["wu"] = nc.dram_tensor("wu", [16, 4, 128, 16, 128], F32, kind="ExternalInput")
    di["wd"] = nc.dram_tensor("wd", [16, 128, 64, 128], F32, kind="ExternalInput")
    di["biasb"] = nc.dram_tensor("biasb", [128, 16], F32, kind="ExternalInput")
    di["masks"] = nc.dram_tensor("masks", [4, 8, 128, 512], BF16, kind="ExternalInput")
    di["bdones_f"] = nc.dram_tensor("bdones_f", [128, 128], F32, kind="ExternalInput")
    di["bdones_b"] = nc.dram_tensor("bdones_b", [128, 128], BF16, kind="ExternalInput")
    di["parity"] = nc.dram_tensor("parity", [1, 1], I32, kind="ExternalInput")
    di["y"] = nc.dram_tensor("y", [2048, 2048], F32, kind="ExternalOutput")
    C.di = di

    C.wsize = {
        "wq": 16 * 128 * 16 * 128, "wk": 16 * 128 * 16 * 128,
        "wo": 16 * 128 * 16 * 128, "wqb": 16 * 128 * 16 * 128,
        "wpb": 16 * 128 * 16 * 128,
        "wv": 16 * 128 * 2048, "wkb": 16 * 128 * 2048, "wvb": 16 * 128 * 2048,
        "wg": 16 * 4 * 128 * 16 * 128, "wu": 16 * 4 * 128 * 16 * 128,
        "wd": 16 * 128 * 64 * 128,
    }
    C.wB = {n: nc.dram_tensor(n + "_bf", [C.wsize[n] // 128, 128], BF16)
            for n in C.wsize}
    C.RwB = {n: Res("R_" + n) for n in C.wB}

    C.Kscr = nc.dram_tensor("Kscr", [16, 128, 4096], BF16)
    C.Vscr = nc.dram_tensor("Vscr", [16, 4096, 128], BF16)
    C.RK = Res("RK")
    C.RV = Res("RV")

    C.sh_ex = nc.dram_tensor("sh_ex", [2, 128, 2064], F32, addr_space="Shared")
    C.sh_fl = nc.dram_tensor("sh_fl", [2, 1, 16], I32, addr_space="Shared")

    from contextlib import ExitStack
    with ExitStack() as _st:
        en = _st.enter_context
        x_res = en(nc.sbuf_tensor("x_res", [128, 16, 2048], BF16))
        abf = en(nc.sbuf_tensor("abf", [128, ABF_N], BF16))
        af32 = en(nc.sbuf_tensor("af32", [128, AF32_N], F32))
        rows = en(nc.sbuf_tensor("rows", [1, 2048], F32))
        rows_bf = en(nc.sbuf_tensor("rows_bf", [1, 2048], BF16))
        ones_row = en(nc.sbuf_tensor("ones_row", [1, 128], BF16))
        ones_col = en(nc.sbuf_tensor("ones_col", [128, 1], BF16))
        bdb_sb = en(nc.sbuf_tensor("bdb_sb", [128, 128], BF16))
        bdf_sb = en(nc.sbuf_tensor("bdf_sb", [128, 128], F32))
        biasb_sb = en(nc.sbuf_tensor("biasb_sb", [128, 16], F32))
        par_sb = en(nc.sbuf_tensor("par_sb", [1, 1], I32))
        flag_sb = en(nc.sbuf_tensor("flag_sb", [1, 16], I32))
        pfl_sb = en(nc.sbuf_tensor("pfl_sb", [1, 16], I32))
        pbs = [en(nc.psum_tensor(f"pb{i}", [128, 512], F32)) for i in range(8)]
        s_sync = en(nc.semaphore("s_sync"))
        s_gp = en(nc.semaphore("s_gp"))
        s_sc = en(nc.semaphore("s_sc"))
        s_ve = en(nc.semaphore("s_ve"))
        s_pe = en(nc.semaphore("s_pe"))
        s_poll = en(nc.semaphore("s_poll"))
        block = en(nc.Block())

        C.x_res = x_res
        C.abf = abf
        C.af32 = af32
        C.rows = rows
        C.rows_bf = rows_bf
        C.ones_row = ones_row
        C.ones_col = ones_col
        C.bdb_sb = bdb_sb
        C.bdf_sb = bdf_sb
        C.biasb_sb = biasb_sb
        C.par_sb = par_sb
        C.flag_sb = flag_sb
        C.pfl_sb = pfl_sb
        C.pb = pbs
        C.Rpb = [Res(f"pb{i}") for i in range(8)]
        C.Rxres = [Res(f"xres{t}") for t in range(4)]
        C.Rrows = Res("rows")
        C.Rden = Res("rden")
        C.Rconst = Res("const")
        C.s_poll = s_poll
        C.sems = {"sync": s_sync, "gpsimd": s_gp, "scalar": s_sc,
                  "vector": s_ve, "tensor": s_pe}
        C.bslots = [Res(f"b{i}") for i in range(ABF_N // SLOT)]
        C.fslots = [Res(f"f{i}") for i in range(AF32_N // SLOT)]

        emit_P0(C)
        if DBG is not None:
            emit_dbg(C, DBG)
            emit_P6_skip = True
            phases = ()
        else:
            emit_P6_skip = False
        if 1 in phases:
            emit_P1(C)
        if 2 in phases:
            emit_P2(C)
        if 3 in phases:
            emit_ffn(C, "P3")
        if 4 in phases:
            emit_P4(C)
        if 5 in phases:
            emit_ffn(C, "P5")
        if not emit_P6_skip:
            emit_P6(C)

        P.plan()

        @block.sync
        def _(e):
            P.emit_engine("sync", e, C.sems)

        @block.gpsimd
        def _(e):
            P.emit_engine("gpsimd", e, C.sems)

        @block.scalar
        def _(e):
            P.emit_engine("scalar", e, C.sems)

        @block.vector
        def _(e):
            P.emit_engine("vector", e, C.sems)

        @block.tensor
        def _(e):
            P.emit_engine("tensor", e, C.sems)

    return nc



def A(C, off, n):
    assert off + n <= ABF_N, (off, n)
    return C.abf[:, off:off + n]


def AF_(C, off, n):
    assert off + n <= AF32_N, (off, n)
    return C.af32[:, off:off + n]


def BS(C, off, n):
    return C.bslots[off // SLOT:(off + n + SLOT - 1) // SLOT]


def FS(C, off, n):
    return C.fslots[off // SLOT:(off + n + SLOT - 1) // SLOT]



def emit_P0(C):
    P, di = C.P, C.di

    P.add("vector", lambda e: e.memset(C.ones_row[:, :], 1.0),
          writes=[C.Rconst], tag="c_or")
    P.add("vector", lambda e: e.memset(C.ones_col[:, :], 1.0),
          writes=[C.Rconst], tag="c_oc")
    P.add("vector", lambda e: e.memset(C.flag_sb[:, 0:1], MAGIC1),
          writes=[C.Rconst], tag="c_f1")
    P.add("vector", lambda e: e.memset(C.flag_sb[:, 1:2], MAGIC2),
          writes=[C.Rconst], tag="c_f2")
    P.add("sync", lambda e: [
        e.dma_start(out=C.bdb_sb[:, :], in_=_dap(di["bdones_b"], 0, [[128, 128], [1, 128]])),
        e.dma_start(out=C.bdf_sb[:, :], in_=_dap(di["bdones_f"], 0, [[128, 128], [1, 128]])),
        e.dma_start(out=C.biasb_sb[:, :], in_=_dap(di["biasb"], 0, [[16, 128], [1, 16]])),
        e.dma_start(out=C.par_sb[:, :], in_=_dap(di["parity"], 0, [[1, 1], [1, 1]])),
    ], writes=[C.Rconst], ndma=4, tag="c_ld")

    for c in range(16):
        def em(e, c=c):
            return e.dma_start(
                out=C.x_res[:, c, :],
                in_=_dap(di["x_own"], c * 128 * 2048, [[2048, 128], [1, 2048]]))
        P.add("gpsimd", em, writes=list(C.Rxres), ndma=1, tag=f"xi{c}")

    order = ["wk", "wv", "wq", "wo", "wg", "wu", "wd", "wqb", "wkb", "wvb", "wpb"]
    bi = 0
    for n in order:
        cols = C.wsize[n] // 128
        CH = 2048
        for i in range(0, cols, CH):
            w = min(CH, cols - i)
            slot = bi % 2
            bi += 1
            boff = slot * 2048

            def eml(e, n=n, i=i, w=w, cols=cols, boff=boff):
                return e.dma_start(
                    out=AF_(C, boff, w),
                    in_=_dap(di[n], i, [[cols, 128], [1, w]]))

            def ems(e, n=n, i=i, w=w, cols=cols, boff=boff):
                return e.dma_start(
                    out=_dap(C.wB[n], i, [[cols, 128], [1, w]]),
                    in_=AF_(C, boff, w))
            P.add("sync", eml, writes=FS(C, boff, w), ndma=1, tag=f"pl{n}{i}")
            P.add("gpsimd", ems, reads=FS(C, boff, w), writes=[C.RwB[n]],
                  ndma=1, tag=f"ps{n}{i}")



def emit_ln(C, get_x, reads_x, lnx_off, tmp_off, banks, tag):
    P = C.P
    W = 512
    b0, b1 = banks
    x2o = [tmp_off, tmp_off + W]
    rbco, mbco = tmp_off + 2 * W, tmp_off + 3 * W

    def em_sum(e):
        out = []
        for c in range(16):
            out.append(e.matmul(C.pb[b0][0:1, 0:W], C.ones_col[:, :], get_x(c),
                                start=(c == 0), stop=(c == 15)))
        return out
    P.add("tensor", em_sum, reads=list(reads_x) + [C.Rconst],
          writes=[C.Rpb[b0]], tag=tag + "sum")

    for c in range(16):
        o = x2o[c % 2]

        def em_sq(e, c=c, o=o):
            return e.activation(A(C, o, W), get_x(c), AF.Square)
        P.add("scalar", em_sq, reads=list(reads_x), writes=BS(C, o, W),
              tag=f"{tag}sq{c}")

        def em_mm(e, c=c, o=o):
            return e.matmul(C.pb[b1][0:1, 0:W], C.ones_col[:, :], A(C, o, W),
                            start=(c == 0), stop=(c == 15))
        P.add("tensor", em_mm, reads=BS(C, o, W) + [C.Rconst],
              writes=[C.Rpb[b1]], tag=f"{tag}sm{c}")

    mean = C.rows[0:1, 0:W]
    m2 = C.rows[0:1, 512:512 + W]
    var = C.rows[0:1, 1024:1024 + W]
    rstd = C.rows[0:1, 1536:1536 + W]
    rstd_bf = C.rows_bf[0:1, 0:W]
    nmr_bf = C.rows_bf[0:1, 512:512 + W]

    P.add("vector", lambda e: e.tensor_scalar(mean, C.pb[b0][0:1, 0:W],
                                              float(1.0 / 2048), None, OP.mult),
          reads=[C.Rpb[b0]], writes=[C.Rrows], tag=tag + "mn")
    P.add("vector", lambda e: e.tensor_tensor(m2, mean, mean, OP.mult),
          reads=[C.Rrows], writes=[C.Rrows], tag=tag + "m2")
    P.add("vector", lambda e: e.scalar_tensor_tensor(var, C.pb[b1][0:1, 0:W],
                                                     float(1.0 / 2048), m2,
                                                     OP.mult, OP.subtract),
          reads=[C.Rpb[b1], C.Rrows], writes=[C.Rrows], tag=tag + "vr")
    P.add("scalar", lambda e: e.activation(m2, var, AF.Ln, bias=f32(1e-5)),
          reads=[C.Rrows], writes=[C.Rrows], tag=tag + "ln")
    P.add("scalar", lambda e: e.activation(rstd, m2, AF.Exp, scale=f32(-0.5)),
          reads=[C.Rrows], writes=[C.Rrows], tag=tag + "rs")
    P.add("vector", lambda e: e.scalar_tensor_tensor(var, mean, float(-1.0),
                                                     rstd, OP.mult, OP.mult),
          reads=[C.Rrows], writes=[C.Rrows], tag=tag + "nm")
    P.add("vector", lambda e: e.tensor_copy(rstd_bf, rstd),
          reads=[C.Rrows], writes=[C.Rrows], tag=tag + "cr")
    P.add("vector", lambda e: e.tensor_copy(nmr_bf, var),
          reads=[C.Rrows], writes=[C.Rrows], tag=tag + "cm")

    P.add("tensor", lambda e: e.matmul(C.pb[b0][:, 0:W], C.ones_row[:, :],
                                       rstd_bf, start=True, stop=True),
          reads=[C.Rrows, C.Rconst], writes=[C.Rpb[b0]], tag=tag + "br")
    P.add("tensor", lambda e: e.matmul(C.pb[b1][:, 0:W], C.ones_row[:, :],
                                       nmr_bf, start=True, stop=True),
          reads=[C.Rrows, C.Rconst], writes=[C.Rpb[b1]], tag=tag + "bm")
    P.add("scalar", lambda e: e.copy(A(C, rbco, W), C.pb[b0][:, 0:W]),
          reads=[C.Rpb[b0]], writes=BS(C, rbco, W), tag=tag + "er")
    P.add("scalar", lambda e: e.copy(A(C, mbco, W), C.pb[b1][:, 0:W]),
          reads=[C.Rpb[b1]], writes=BS(C, mbco, W), tag=tag + "em")

    for c in range(16):
        o = x2o[c % 2]

        def em_t1(e, c=c, o=o):
            return e.tensor_tensor(A(C, o, W), get_x(c), A(C, rbco, W), OP.mult)
        P.add("vector", em_t1, reads=list(reads_x) + BS(C, rbco, W),
              writes=BS(C, o, W), tag=f"{tag}t1{c}")

        def em_t2(e, c=c, o=o):
            return e.tensor_tensor(A(C, lnx_off + 512 * c, W), A(C, o, W),
                                   A(C, mbco, W), OP.add)
        P.add("vector", em_t2, reads=BS(C, o, W) + BS(C, mbco, W),
              writes=BS(C, lnx_off + 512 * c, W), tag=f"{tag}t2{c}")



def emit_P1(C):
    P, di = C.P, C.di
    O_XBF, O_LNX, O_TMP, O_WK, O_KEV, O_WV, O_VEV = \
        0, 8192, 24576, 26624, 30720, 31744, 39936

    for sl in range(8):
        lnoff = O_LNX + (sl % 2) * 8192

        def em_lx(e, sl=sl):
            out = []
            for c in range(16):
                out.append(e.dma_start(
                    out=A(C, O_XBF + 512 * c, 512),
                    in_=_dap(di["x_full"], c * 128 * 4096 + 512 * sl,
                             [[4096, 128], [1, 512]])))
            return out
        P.add("gpsimd", em_lx, writes=BS(C, O_XBF, 8192), ndma=16,
              tag=f"1lx{sl}")

        emit_ln(C, lambda c: A(C, O_XBF + 512 * c, 512), BS(C, O_XBF, 8192),
                lnoff, O_TMP, (6, 7), f"1n{sl}")

        for oc in range(16):
            wo = O_WK + 2048 * (oc % 2)

            def em_wk(e, oc=oc, wo=wo):
                return e.dma_start(
                    out=A(C, wo, 2048),
                    in_=_dap(C.wB["wk"], oc * 262144, [[2048, 128], [1, 2048]]))
            P.add("sync", em_wk, reads=[C.RwB["wk"]], writes=BS(C, wo, 2048),
                  ndma=1, tag=f"1wk{sl}_{oc}")

            bank = oc % 4

            def em_mm(e, wo=wo, bank=bank, lnoff=lnoff):
                out = []
                for k in range(16):
                    out.append(e.matmul(C.pb[bank][:, :],
                                        A(C, wo + 128 * k, 128),
                                        A(C, lnoff + 512 * k, 512),
                                        start=(k == 0), stop=(k == 15)))
                return out
            P.add("tensor", em_mm,
                  reads=BS(C, wo, 2048) + BS(C, lnoff, 8192),
                  writes=[C.Rpb[bank]], tag=f"1km{sl}_{oc}")

            eo = O_KEV + 512 * (oc % 2)

            def em_ev(e, bank=bank, eo=eo):
                return e.tensor_copy(A(C, eo, 512), C.pb[bank][:, :])
            P.add("vector", em_ev, reads=[C.Rpb[bank]], writes=BS(C, eo, 512),
                  tag=f"1ke{sl}_{oc}")

            def em_st(e, oc=oc, eo=eo, sl=sl):
                return e.dma_start(
                    out=_dap(C.Kscr, oc * 128 * 4096 + 512 * sl,
                             [[4096, 128], [1, 512]]),
                    in_=A(C, eo, 512))
            P.add("gpsimd", em_st, reads=BS(C, eo, 512), writes=[C.RK],
                  ndma=1, tag=f"1ks{sl}_{oc}")

        for nt in range(4):
            for k in range(16):
                def em_wv(e, nt=nt, k=k):
                    return e.dma_start(
                        out=A(C, O_WV + 512 * k, 512),
                        in_=_dap(C.wB["wv"], k * 262144 + 512 * nt,
                                 [[2048, 128], [1, 512]]))
                P.add("sync", em_wv, reads=[C.RwB["wv"]],
                      writes=BS(C, O_WV + 512 * k, 512), ndma=1,
                      tag=f"1wv{sl}_{nt}_{k}")
            for blk in range(4):
                bank = 4 + blk % 2

                def em_vmm(e, blk=blk, bank=bank, lnoff=lnoff):
                    out = []
                    for k in range(16):
                        out.append(e.matmul(C.pb[bank][:, :],
                                            A(C, lnoff + 512 * k + 128 * blk, 128),
                                            A(C, O_WV + 512 * k, 512),
                                            start=(k == 0), stop=(k == 15)))
                    return out
                P.add("tensor", em_vmm,
                      reads=BS(C, O_WV, 8192) + BS(C, lnoff, 8192),
                      writes=[C.Rpb[bank]], tag=f"1vm{sl}_{nt}_{blk}")

                eo = O_VEV + 512 * (blk % 2)

                def em_vev(e, bank=bank, eo=eo):
                    return e.tensor_copy(A(C, eo, 512), C.pb[bank][:, :])
                P.add("vector", em_vev, reads=[C.Rpb[bank]],
                      writes=BS(C, eo, 512), tag=f"1ve{sl}_{nt}_{blk}")

                def em_vst(e, nt=nt, blk=blk, eo=eo, sl=sl):
                    off = (4 * nt) * 4096 * 128 + (512 * sl + 128 * blk) * 128
                    return e.dma_start(
                        out=_dap(C.Vscr, off,
                                 [[128, 128], [4096 * 128, 4], [1, 128]]),
                        in_=A(C, eo, 512))
                P.add("gpsimd", em_vst, reads=BS(C, eo, 512), writes=[C.RV],
                      ndma=1, tag=f"1vs{sl}_{nt}_{blk}")



def emit_P2(C):
    P, di = C.P, C.di
    O_MASK, O_LNX, O_QH, O_KB, O_VB, O_EXP, O_ATT, O_WQ, O_WO, O_TMP, \
        O_OUN, O_RBC, O_YT = (0, 4096, 12288, 13312, 21504, 29696, 31744,
                              39936, 41984, 44032, 46080, 47104, 48128)
    rden = C.af32[0:1, 4096:4608]
    rden_bf = C.rows_bf[0:1, 1024:1536]

    for t in range(4):
        PT = 8 * (t + 1)

        def em_mask(e, t=t):
            return e.dma_start(
                out=A(C, O_MASK, 4096),
                in_=_dap(di["masks"], t * 8 * 128 * 512,
                         [[512, 128], [65536, 8], [1, 512]]))
        P.add("sync", em_mask, writes=BS(C, O_MASK, 4096), ndma=1,
              tag=f"2mk{t}")

        emit_ln(C, lambda c, t=t: C.x_res[:, c, 512 * t:512 * t + 512],
                [C.Rxres[t]], O_LNX, O_TMP, (6, 7), f"2n{t}")

        for h in range(16):
            def em_wq(e, h=h):
                return e.dma_start(
                    out=A(C, O_WQ, 2048),
                    in_=_dap(C.wB["wq"], h * 262144, [[2048, 128], [1, 2048]]))
            P.add("sync", em_wq, reads=[C.RwB["wq"]],
                  writes=BS(C, O_WQ, 2048), ndma=1, tag=f"2wq{t}_{h}")

            qb = h % 2

            def em_qmm(e, qb=qb):
                out = []
                for k in range(16):
                    out.append(e.matmul(C.pb[qb][:, :],
                                        A(C, O_WQ + 128 * k, 128),
                                        A(C, O_LNX + 512 * k, 512),
                                        start=(k == 0), stop=(k == 15)))
                return out
            P.add("tensor", em_qmm,
                  reads=BS(C, O_WQ, 2048) + BS(C, O_LNX, 8192),
                  writes=[C.Rpb[qb]], tag=f"2qm{t}_{h}")

            qo = O_QH + 512 * (h % 2)

            def em_qev(e, qb=qb, qo=qo):
                return e.tensor_copy(A(C, qo, 512), C.pb[qb][:, :])
            P.add("vector", em_qev, reads=[C.Rpb[qb]], writes=BS(C, qo, 512),
                  tag=f"2qe{t}_{h}")

            ko = O_KB + 4096 * (h % 2)
            vo = O_VB + 4096 * (h % 2)

            def em_k(e, h=h, ko=ko, PT=PT):
                return e.dma_start(
                    out=A(C, ko, 128 * PT),
                    in_=_dap(C.Kscr, h * 128 * 4096, [[4096, 128], [1, 128 * PT]]))
            P.add("sync", em_k, reads=[C.RK], writes=BS(C, ko, 4096), ndma=1,
                  tag=f"2kl{t}_{h}")

            def em_v(e, h=h, vo=vo, PT=PT):
                return e.dma_start(
                    out=A(C, vo, 128 * PT),
                    in_=_dap(C.Vscr, h * 4096 * 128,
                             [[128, 128], [128 * 128, PT], [1, 128]]))
            P.add("sync", em_v, reads=[C.RV], writes=BS(C, vo, 4096), ndma=1,
                  tag=f"2vl{t}_{h}")

            for kc in range(PT):
                sb = 2 + kc % 2

                def em_s(e, kc=kc, sb=sb, ko=ko, qo=qo):
                    return e.matmul(C.pb[sb][:, :], A(C, ko + 128 * kc, 128),
                                    A(C, qo, 512), start=True, stop=True)
                P.add("tensor", em_s,
                      reads=BS(C, ko + 128 * kc, 128) + BS(C, qo, 512),
                      writes=[C.Rpb[sb]], tag=f"2s{t}_{h}_{kc}")

                eo = O_EXP + 512 * (kc % 2)

                def em_exp(e, sb=sb, eo=eo):
                    return e.activation(A(C, eo, 512), C.pb[sb][:, :], AF.Exp,
                                        bias=EXP_SHIFT, scale=INV_SQRT_HD)
                P.add("scalar", em_exp, reads=[C.Rpb[sb]],
                      writes=BS(C, eo, 512), tag=f"2e{t}_{h}_{kc}")

                if kc >= 8 * t:
                    j = kc - 8 * t
                    mo = O_EXP + 1024 + 512 * (kc % 2)

                    def em_msk(e, j=j, eo=eo, mo=mo):
                        return e.tensor_tensor(
                            A(C, mo, 512), A(C, eo, 512),
                            C.abf[:, O_MASK + 512 * j:O_MASK + 512 * j + 512],
                            OP.mult)
                    P.add("vector", em_msk,
                          reads=BS(C, eo, 512) + BS(C, O_MASK, 4096),
                          writes=BS(C, mo, 512), tag=f"2x{t}_{h}_{kc}")
                    eo = mo
                erd = BS(C, eo, 512)

                def em_den(e, kc=kc, eo=eo, PT=PT):
                    return e.matmul(C.pb[4][0:1, :], C.ones_col[:, :],
                                    A(C, eo, 512),
                                    start=(kc == 0), stop=(kc == PT - 1))
                P.add("tensor", em_den, reads=erd + [C.Rconst],
                      writes=[C.Rpb[4]], tag=f"2d{t}_{h}_{kc}")

                def em_pv(e, kc=kc, eo=eo, vo=vo, PT=PT):
                    return e.matmul(C.pb[5][:, :], A(C, vo + 128 * kc, 128),
                                    A(C, eo, 512),
                                    start=(kc == 0), stop=(kc == PT - 1))
                P.add("tensor", em_pv, reads=erd + BS(C, vo + 128 * kc, 128),
                      writes=[C.Rpb[5]], tag=f"2p{t}_{h}_{kc}")

            P.add("vector", lambda e: e.reciprocal(rden, C.pb[4][0:1, :]),
                  reads=[C.Rpb[4]], writes=[C.Rden] + FS(C, 4096, 512),
                  tag=f"2r{t}_{h}")
            P.add("vector", lambda e: e.tensor_copy(rden_bf, rden),
                  reads=[C.Rden], writes=[C.Rden], tag=f"2rc{t}_{h}")
            P.add("tensor", lambda e: e.matmul(C.pb[6][:, :], C.ones_row[:, :],
                                               rden_bf, start=True, stop=True),
                  reads=[C.Rden, C.Rconst], writes=[C.Rpb[6]], tag=f"2b{t}_{h}")

            ro = O_RBC + 512 * (h % 2)
            oo = O_OUN + 512 * (h % 2)
            ao = O_ATT + 512 * h

            def em_rbc(e, ro=ro):
                return e.tensor_copy(A(C, ro, 512), C.pb[6][:, :])
            P.add("vector", em_rbc, reads=[C.Rpb[6]], writes=BS(C, ro, 512),
                  tag=f"2rb{t}_{h}")

            def em_oun(e, oo=oo):
                return e.tensor_copy(A(C, oo, 512), C.pb[5][:, :])
            P.add("vector", em_oun, reads=[C.Rpb[5]], writes=BS(C, oo, 512),
                  tag=f"2ou{t}_{h}")

            def em_att(e, ao=ao, oo=oo, ro=ro):
                return e.tensor_tensor(A(C, ao, 512), A(C, oo, 512),
                                       A(C, ro, 512), OP.mult)
            P.add("vector", em_att, reads=BS(C, oo, 512) + BS(C, ro, 512),
                  writes=BS(C, ao, 512), tag=f"2a{t}_{h}")

        for oc in range(16):
            def em_wo(e, oc=oc):
                return e.dma_start(
                    out=A(C, O_WO, 2048),
                    in_=_dap(C.wB["wo"], oc * 262144, [[2048, 128], [1, 2048]]))
            P.add("sync", em_wo, reads=[C.RwB["wo"]],
                  writes=BS(C, O_WO, 2048), ndma=1, tag=f"2wo{t}_{oc}")

            ob = oc % 2

            def em_omm(e, ob=ob):
                out = []
                for k in range(16):
                    out.append(e.matmul(C.pb[ob][:, :],
                                        A(C, O_WO + 128 * k, 128),
                                        A(C, O_ATT + 512 * k, 512),
                                        start=(k == 0), stop=(k == 15)))
                return out
            P.add("tensor", em_omm,
                  reads=BS(C, O_WO, 2048) + BS(C, O_ATT, 8192),
                  writes=[C.Rpb[ob]], tag=f"2om{t}_{oc}")

            yo = O_YT + 512 * (oc % 2)

            def em_yev(e, ob=ob, yo=yo):
                return e.tensor_copy(A(C, yo, 512), C.pb[ob][:, :])
            P.add("vector", em_yev, reads=[C.Rpb[ob]], writes=BS(C, yo, 512),
                  tag=f"2ye{t}_{oc}")

            def em_res(e, oc=oc, yo=yo, t=t):
                xs = C.x_res[:, oc, 512 * t:512 * t + 512]
                return e.tensor_tensor(xs, A(C, yo, 512), xs, OP.add)
            P.add("vector", em_res, reads=BS(C, yo, 512),
                  writes=[C.Rxres[t]], tag=f"2rs{t}_{oc}")



def emit_ffn(C, ph):
    P = C.P
    O_LNX, O_WG, O_WU, O_HSP, O_TMP, O_YT = 0, 8192, 12288, 14336, 47104, 49152
    O_WD = 0

    for tt in range(4):
        emit_ln(C, lambda c, tt=tt: C.x_res[:, c, 512 * tt:512 * tt + 512],
                [C.Rxres[tt]], O_LNX, O_TMP, (6, 7), f"{ph}n{tt}")

        for gc in range(16):
            for s in range(4):
                wgo = O_WG + 2048 * (s % 2)

                def em_wg(e, gc=gc, s=s, wgo=wgo):
                    return e.dma_start(
                        out=A(C, wgo, 2048),
                        in_=_dap(C.wB["wg"], (gc * 4 + s) * 262144,
                                 [[2048, 128], [1, 2048]]))
                P.add("sync", em_wg, reads=[C.RwB["wg"]],
                      writes=BS(C, wgo, 2048), ndma=1, tag=f"{ph}wg{tt}_{gc}_{s}")

                def em_wu(e, gc=gc, s=s):
                    return e.dma_start(
                        out=A(C, O_WU, 2048),
                        in_=_dap(C.wB["wu"], (gc * 4 + s) * 262144,
                                 [[2048, 128], [1, 2048]]))
                P.add("sync", em_wu, reads=[C.RwB["wu"]],
                      writes=BS(C, O_WU, 2048), ndma=1, tag=f"{ph}wu{tt}_{gc}_{s}")

                gb = s % 2
                ub = 2 + s % 2

                def em_gmm(e, wgo=wgo, gb=gb):
                    out = []
                    for k in range(16):
                        out.append(e.matmul(C.pb[gb][:, :],
                                            A(C, wgo + 128 * k, 128),
                                            A(C, O_LNX + 512 * k, 512),
                                            start=(k == 0), stop=(k == 15)))
                    return out
                P.add("tensor", em_gmm,
                      reads=BS(C, wgo, 2048) + BS(C, O_LNX, 8192),
                      writes=[C.Rpb[gb]], tag=f"{ph}gm{tt}_{gc}_{s}")

                def em_umm(e, ub=ub):
                    out = []
                    for k in range(16):
                        out.append(e.matmul(C.pb[ub][:, :],
                                            A(C, O_WU + 128 * k, 128),
                                            A(C, O_LNX + 512 * k, 512),
                                            start=(k == 0), stop=(k == 15)))
                    return out
                P.add("tensor", em_umm,
                      reads=BS(C, O_WU, 2048) + BS(C, O_LNX, 8192),
                      writes=[C.Rpb[ub]], tag=f"{ph}um{tt}_{gc}_{s}")

                so = 512 * (s % 2)

                def em_sil(e, gb=gb, so=so):
                    return e.activation(AF_(C, so, 512), C.pb[gb][:, :], AF.Silu)
                P.add("scalar", em_sil, reads=[C.Rpb[gb]],
                      writes=FS(C, so, 512), tag=f"{ph}si{tt}_{gc}_{s}")

                ho = 1024 + 512 * s

                def em_h(e, ub=ub, so=so, ho=ho):
                    return e.tensor_tensor(AF_(C, ho, 512), AF_(C, so, 512),
                                           C.pb[ub][:, :], OP.mult)
                P.add("vector", em_h, reads=FS(C, so, 512) + [C.Rpb[ub]],
                      writes=FS(C, ho, 512), tag=f"{ph}h{tt}_{gc}_{s}")

            for s in range(4):
                def em_abs(e, s=s):
                    return e.tensor_tensor(AF_(C, 3072 + 512 * s, 512),
                                           AF_(C, 1024 + 512 * s, 512),
                                           AF_(C, 1024 + 512 * s, 512),
                                           OP.mult)
                P.add("vector", em_abs, reads=FS(C, 1024 + 512 * s, 512),
                      writes=FS(C, 3072 + 512 * s, 512), tag=f"{ph}ab{tt}_{gc}_{s}")
            QO = 5120
            spec = [(QO, 3072, 3584, OP.max), (QO + 512, 3072, 3584, OP.min),
                    (QO + 1024, 4096, 4608, OP.max),
                    (QO + 1536, 4096, 4608, OP.min)]
            for do, o1, o2, op in spec:
                def em_p(e, do=do, o1=o1, o2=o2, op=op):
                    return e.tensor_tensor(AF_(C, do, 512), AF_(C, o1, 512),
                                           AF_(C, o2, 512), op)
                P.add("vector", em_p,
                      reads=FS(C, o1, 512) + FS(C, o2, 512),
                      writes=FS(C, do, 512), tag=f"{ph}mx{tt}_{gc}_{do}")
            P.add("vector", lambda e: e.tensor_tensor(
                AF_(C, QO + 2048, 512), AF_(C, QO, 512), AF_(C, QO + 1024, 512),
                OP.min),
                reads=FS(C, QO, 512) + FS(C, QO + 1024, 512),
                writes=FS(C, QO + 2048, 512), tag=f"{ph}t1{tt}_{gc}")
            P.add("vector", lambda e: e.tensor_tensor(
                AF_(C, QO, 512), AF_(C, QO + 512, 512), AF_(C, QO + 1536, 512),
                OP.max),
                reads=FS(C, QO + 512, 512) + FS(C, QO + 1536, 512),
                writes=FS(C, QO, 512), tag=f"{ph}t2{tt}_{gc}")
            P.add("vector", lambda e: e.tensor_tensor(
                AF_(C, QO + 2560, 512), AF_(C, QO + 2048, 512), AF_(C, QO, 512),
                OP.max),
                reads=FS(C, QO + 2048, 512) + FS(C, QO, 512),
                writes=FS(C, QO + 2560, 512), tag=f"{ph}m2{tt}_{gc}")
            for s in range(4):
                def em_ge(e, s=s):
                    return e.tensor_tensor(AF_(C, QO + 512, 512),
                                           AF_(C, 3072 + 512 * s, 512),
                                           AF_(C, QO + 2560, 512), OP.is_ge)
                P.add("vector", em_ge,
                      reads=FS(C, 3072 + 512 * s, 512) + FS(C, QO + 2560, 512),
                      writes=FS(C, QO + 512, 512), tag=f"{ph}ge{tt}_{gc}_{s}")

                hso = O_HSP + 512 * (4 * gc + s)

                if SPARSIFY:
                    def em_ap(e, s=s, hso=hso):
                        return e.tensor_tensor(A(C, hso, 512),
                                               AF_(C, 1024 + 512 * s, 512),
                                               AF_(C, QO + 512, 512), OP.mult)
                    P.add("vector", em_ap,
                          reads=FS(C, 1024 + 512 * s, 512) + FS(C, QO + 512, 512),
                          writes=BS(C, hso, 512), tag=f"{ph}sp{tt}_{gc}_{s}")
                else:
                    def em_ap(e, s=s, hso=hso):
                        return e.tensor_copy(A(C, hso, 512),
                                             AF_(C, 1024 + 512 * s, 512))
                    P.add("vector", em_ap,
                          reads=FS(C, 1024 + 512 * s, 512),
                          writes=BS(C, hso, 512), tag=f"{ph}sp{tt}_{gc}_{s}")

        for m in range(16):
            db = 4 + m % 2
            for half in range(2):
                wdo = O_WD + 4096 * ((2 * m + half) % 3)

                def em_wd(e, m=m, half=half, wdo=wdo):
                    return e.dma_start(
                        out=A(C, wdo, 4096),
                        in_=_dap(C.wB["wd"], m * 1048576 + 4096 * half,
                                 [[8192, 128], [1, 4096]]))
                P.add("sync", em_wd, reads=[C.RwB["wd"]],
                      writes=BS(C, wdo, 4096), ndma=1,
                      tag=f"{ph}wd{tt}_{m}_{half}")

                def em_dmm(e, wdo=wdo, db=db, half=half):
                    out = []
                    for i in range(32):
                        out.append(e.matmul(
                            C.pb[db][:, :],
                            A(C, wdo + 128 * i, 128),
                            A(C, O_HSP + 512 * (32 * half + i), 512),
                            start=(half == 0 and i == 0),
                            stop=(half == 1 and i == 31)))
                    return out
                P.add("tensor", em_dmm,
                      reads=BS(C, wdo, 4096) + BS(C, O_HSP + 16384 * half, 16384),
                      writes=[C.Rpb[db]], tag=f"{ph}dm{tt}_{m}_{half}")

            yo = O_YT + 512 * (m % 2)

            def em_ye(e, db=db, yo=yo):
                return e.tensor_copy(A(C, yo, 512), C.pb[db][:, :])
            P.add("vector", em_ye, reads=[C.Rpb[db]], writes=BS(C, yo, 512),
                  tag=f"{ph}ye{tt}_{m}")

            def em_res(e, m=m, yo=yo, tt=tt):
                xs = C.x_res[:, m, 512 * tt:512 * tt + 512]
                return e.tensor_tensor(xs, A(C, yo, 512), xs, OP.add)
            P.add("vector", em_res, reads=BS(C, yo, 512),
                  writes=[C.Rxres[tt]], tag=f"{ph}rs{tt}_{m}")



def emit_P4(C):
    P, di = C.P, C.di
    O_LNX, O_KT, O_VT, O_WM, O_TMP = 0, 8192, 16384, 24576, 26624

    P.add("vector", lambda e: e.memset(AF_(C, 0, 2048), 0.0),
          writes=FS(C, 0, 2048), tag="4bd0")
    P.add("vector", lambda e: e.memset(AF_(C, 2048, 16), 0.0),
          writes=FS(C, 2048, 512), tag="4ks0")

    for tt in range(4):
        emit_ln(C, lambda c, tt=tt: C.x_res[:, c, 512 * tt:512 * tt + 512],
                [C.Rxres[tt]], O_LNX, O_TMP, (6, 7), f"4n{tt}")

        for nt in range(4):
            for k in range(16):
                wko = O_WM + 512 * (k % 2)
                wvo = O_WM + 1024 + 512 * (k % 2)

                def em_wk(e, k=k, nt=nt, wko=wko):
                    return e.dma_start(
                        out=A(C, wko, 512),
                        in_=_dap(C.wB["wkb"], k * 262144 + 512 * nt,
                                 [[2048, 128], [1, 512]]))
                P.add("sync", em_wk, reads=[C.RwB["wkb"]],
                      writes=BS(C, wko, 512), ndma=1, tag=f"4wk{tt}_{nt}_{k}")

                def em_wv(e, k=k, nt=nt, wvo=wvo):
                    return e.dma_start(
                        out=A(C, wvo, 512),
                        in_=_dap(C.wB["wvb"], k * 262144 + 512 * nt,
                                 [[2048, 128], [1, 512]]))
                P.add("sync", em_wv, reads=[C.RwB["wvb"]],
                      writes=BS(C, wvo, 512), ndma=1, tag=f"4wv{tt}_{nt}_{k}")

                def em_kvmm(e, k=k, wko=wko, wvo=wvo):
                    out = []
                    for blk in range(4):
                        lhs = A(C, O_LNX + 512 * k + 128 * blk, 128)
                        out.append(e.matmul(C.pb[blk][:, :], lhs, A(C, wko, 512),
                                            start=(k == 0), stop=(k == 15)))
                        out.append(e.matmul(C.pb[4 + blk][:, :], lhs, A(C, wvo, 512),
                                            start=(k == 0), stop=(k == 15)))
                    return out
                P.add("tensor", em_kvmm,
                      reads=BS(C, wko, 512) + BS(C, wvo, 512) + BS(C, O_LNX, 8192),
                      writes=[C.Rpb[i] for i in range(8)],
                      tag=f"4kv{tt}_{nt}_{k}")

            for blk in range(4):
                kto = O_KT + 2048 * blk + 512 * nt
                vto = O_VT + 2048 * blk + 512 * nt

                def em_kev(e, blk=blk, kto=kto):
                    return e.tensor_relu(A(C, kto, 512), C.pb[blk][:, :])
                P.add("vector", em_kev, reads=[C.Rpb[blk]],
                      writes=BS(C, kto, 512), tag=f"4ke{tt}_{nt}_{blk}")

                def em_vev(e, blk=blk, vto=vto):
                    return e.tensor_copy(A(C, vto, 512), C.pb[4 + blk][:, :])
                P.add("vector", em_vev, reads=[C.Rpb[4 + blk]],
                      writes=BS(C, vto, 512), tag=f"4ve{tt}_{nt}_{blk}")

        for grp in range(16):
            vb = grp % 2

            def em_vk(e, grp=grp, vb=vb):
                out = []
                for blk in range(4):
                    out.append(e.matmul(
                        C.pb[vb][:, 0:128],
                        A(C, O_KT + 2048 * blk + 128 * grp, 128),
                        A(C, O_VT + 2048 * blk + 128 * grp, 128),
                        start=(blk == 0), stop=(blk == 3)))
                return out
            P.add("tensor", em_vk,
                  reads=BS(C, O_KT, 8192) + BS(C, O_VT, 8192),
                  writes=[C.Rpb[vb]], tag=f"4vk{tt}_{grp}")

            def em_vka(e, grp=grp, vb=vb):
                dst = AF_(C, 128 * grp, 128)
                return e.tensor_tensor(dst, C.pb[vb][:, 0:128], dst, OP.add)
            P.add("vector", em_vka, reads=[C.Rpb[vb]] + FS(C, 128 * grp, 128),
                  writes=FS(C, 128 * grp, 128), tag=f"4va{tt}_{grp}")

            def em_ks(e, grp=grp):
                out = []
                for blk in range(4):
                    out.append(e.matmul(
                        C.pb[2][:, grp:grp + 1],
                        A(C, O_KT + 2048 * blk + 128 * grp, 128),
                        C.ones_col[:, :],
                        start=(blk == 0), stop=(blk == 3)))
                return out
            P.add("tensor", em_ks, reads=BS(C, O_KT, 8192) + [C.Rconst],
                  writes=[C.Rpb[2]], tag=f"4ks{tt}_{grp}")

        def em_ksa(e):
            dst = AF_(C, 2048, 16)
            return e.tensor_tensor(dst, C.pb[2][:, 0:16], dst, OP.add)
        P.add("vector", em_ksa, reads=[C.Rpb[2]] + FS(C, 2048, 512),
              writes=FS(C, 2048, 512), tag=f"4ka{tt}")

    for grp in range(16):
        def em_m(e, grp=grp):
            dst = AF_(C, 128 * grp, 128)
            return e.tensor_tensor(dst, dst, C.bdf_sb[:, :], OP.mult)
        P.add("vector", em_m, reads=[C.Rconst] + FS(C, 128 * grp, 128),
              writes=FS(C, 128 * grp, 128), tag=f"4bm{grp}")

    def em_ex(g):
        with (
            g.register("p") as p,
            g.register("off") as off,
            g.register("cnt") as cnt,
            g.register("cond") as cond,
            g.register("fl") as fl,
        ):
            g.reg_load(p, C.par_sb[:1, :1])
            g.reg_mov(off, 0)
            g.reg_add(off, off, p)
            g.reg_mul(off, off, 128 * 2064)
            g.dma_start(bass.AP(C.sh_ex, off, [[2064, 128], [1, 2048]]),
                        AF_(C, 0, 2048)).then_inc(C.s_poll, 16)
            g.reg_add(off, off, 2048)
            g.dma_start(bass.AP(C.sh_ex, off, [[2064, 128], [1, 16]]),
                        AF_(C, 2048, 16)).then_inc(C.s_poll, 16)
            g.wait_ge(C.s_poll, 32)
            g.reg_mov(off, 0)
            g.reg_add(off, off, p)
            g.reg_mul(off, off, 16)
            g.dma_start(bass.AP(C.sh_fl, off, [[16, 1], [1, 16]]),
                        C.flag_sb[:, :]).then_inc(C.s_poll, 16)
            g.wait_ge(C.s_poll, 48)
            g.reg_mov(off, 1)
            g.reg_sub(off, off, p)
            g.reg_mul(off, off, 16)
            g.reg_mov(cnt, 48)
            g.reg_mov(cond, 1)
            with g.While(cond):
                g.reg_add(cnt, cnt, 16)
                g.dma_start(C.pfl_sb[:, :],
                            bass.AP(C.sh_fl, off, [[16, 1], [1, 16]])
                            ).then_inc(C.s_poll, 16)
                g.wait_ge(C.s_poll, cnt)
                g.reg_load(fl, C.pfl_sb[:1, 0:1])
                g.reg_mov(cond, fl)
                g.reg_sub(cond, cond, MAGIC1)
            g.reg_mov(cond, 1)
            with g.While(cond):
                g.reg_add(cnt, cnt, 16)
                g.dma_start(C.pfl_sb[:, :],
                            bass.AP(C.sh_fl, off, [[16, 1], [1, 16]])
                            ).then_inc(C.s_poll, 16)
                g.wait_ge(C.s_poll, cnt)
                g.reg_load(fl, C.pfl_sb[:1, 1:2])
                g.reg_mov(cond, fl)
                g.reg_sub(cond, cond, MAGIC2)
            g.reg_mov(off, 1)
            g.reg_sub(off, off, p)
            g.reg_mul(off, off, 128 * 2064)
            r1 = g.dma_start(AF_(C, 2560, 2048),
                             bass.AP(C.sh_ex, off, [[2064, 128], [1, 2048]]))
            g.reg_add(off, off, 2048)
            r2 = g.dma_start(AF_(C, 4608, 16),
                             bass.AP(C.sh_ex, off, [[2064, 128], [1, 16]]))
            return [r1, r2]
    P.add("gpsimd", em_ex,
          reads=FS(C, 0, 2560) + [C.Rconst],
          writes=FS(C, 2560, 2560), ndma=2, tag="4ex")

    O_BDT, O_KST, O_LNX2, O_QT, O_B, O_ATT2, O_WQ2, O_WP2, O_YT2, O_TMP2 = \
        0, 2048, 2560, 10752, 11776, 12800, 20992, 23040, 25088, 26112

    P.add("vector", lambda e: e.tensor_tensor(A(C, O_BDT, 2048),
                                              AF_(C, 0, 2048),
                                              AF_(C, 2560, 2048), OP.add),
          reads=FS(C, 0, 2048) + FS(C, 2560, 2048),
          writes=BS(C, O_BDT, 2048), tag="4tot")
    P.add("vector", lambda e: e.tensor_tensor(
        AF_(C, 7168, 16),
        AF_(C, 2048, 16), AF_(C, 4608, 16), OP.add),
        reads=FS(C, 2048, 512) + FS(C, 4608, 512),
        writes=FS(C, 7168, 512), tag="4kt")

    for tt in range(4):
        emit_ln(C, lambda c, tt=tt: C.x_res[:, c, 512 * tt:512 * tt + 512],
                [C.Rxres[tt]], O_LNX2, O_TMP2, (6, 7), f"4m{tt}")

        for grp in range(16):
            def em_wq(e, grp=grp):
                return e.dma_start(
                    out=A(C, O_WQ2, 2048),
                    in_=_dap(C.wB["wqb"], grp * 262144, [[2048, 128], [1, 2048]]))
            P.add("sync", em_wq, reads=[C.RwB["wqb"]],
                  writes=BS(C, O_WQ2, 2048), ndma=1, tag=f"4wq{tt}_{grp}")

            qb = grp % 2

            def em_qmm(e, qb=qb):
                out = []
                for k in range(16):
                    out.append(e.matmul(C.pb[qb][:, :],
                                        A(C, O_WQ2 + 128 * k, 128),
                                        A(C, O_LNX2 + 512 * k, 512),
                                        start=(k == 0), stop=(k == 15)))
                return out
            P.add("tensor", em_qmm,
                  reads=BS(C, O_WQ2, 2048) + BS(C, O_LNX2, 8192),
                  writes=[C.Rpb[qb]], tag=f"4qm{tt}_{grp}")

            qo = O_QT + 512 * (grp % 2)

            def em_qev(e, qb=qb, qo=qo):
                return e.tensor_relu(A(C, qo, 512), C.pb[qb][:, :])
            P.add("vector", em_qev, reads=[C.Rpb[qb]], writes=BS(C, qo, 512),
                  tag=f"4qe{tt}_{grp}")

            bo = O_B + 512 * (grp % 2)

            def em_b(e, grp=grp, qo=qo, bo=bo):
                return e.tensor_scalar(
                    A(C, bo, 512), A(C, qo, 512),
                    C.af32[:, 7168 + grp:7168 + grp + 1],
                    None, OP.mult)
            P.add("vector", em_b,
                  reads=BS(C, qo, 512) + FS(C, 7168, 512),
                  writes=BS(C, bo, 512), tag=f"4b{tt}_{grp}")

            db = 2 + grp % 2

            def em_den(e, bo=bo, db=db):
                return e.matmul(C.pb[db][:, :], C.bdb_sb[:, :], A(C, bo, 512),
                                start=True, stop=True)
            P.add("tensor", em_den, reads=BS(C, bo, 512) + [C.Rconst],
                  writes=[C.Rpb[db]], tag=f"4dn{tt}_{grp}")

            ob = 4 + grp % 2

            def em_ov(e, grp=grp, qo=qo, ob=ob):
                return e.matmul(C.pb[ob][:, :], A(C, O_BDT + 128 * grp, 128),
                                A(C, qo, 512), start=True, stop=True)
            P.add("tensor", em_ov,
                  reads=BS(C, O_BDT + 128 * grp, 128) + BS(C, qo, 512),
                  writes=[C.Rpb[ob]], tag=f"4ov{tt}_{grp}")

            do = 5120 + 512 * (grp % 2)

            def em_eps(e, db=db, do=do):
                return e.activation(AF_(C, do, 512), C.pb[db][:, :],
                                    AF.Identity, bias=f32(1e-15))
            P.add("scalar", em_eps, reads=[C.Rpb[db]], writes=FS(C, do, 512),
                  tag=f"4ep{tt}_{grp}")

            ro = 6144 + 512 * (grp % 2)

            def em_rec(e, do=do, ro=ro):
                return e.reciprocal(AF_(C, ro, 512), AF_(C, do, 512))
            P.add("vector", em_rec, reads=FS(C, do, 512),
                  writes=FS(C, ro, 512), tag=f"4rc{tt}_{grp}")

            ao = O_ATT2 + 512 * grp

            def em_att(e, ob=ob, ro=ro, ao=ao):
                return e.tensor_tensor(A(C, ao, 512), C.pb[ob][:, :],
                                       AF_(C, ro, 512), OP.mult)
            P.add("vector", em_att, reads=[C.Rpb[ob]] + FS(C, ro, 512),
                  writes=BS(C, ao, 512), tag=f"4at{tt}_{grp}")

        for oc in range(16):
            def em_wp(e, oc=oc):
                return e.dma_start(
                    out=A(C, O_WP2, 2048),
                    in_=_dap(C.wB["wpb"], oc * 262144, [[2048, 128], [1, 2048]]))
            P.add("sync", em_wp, reads=[C.RwB["wpb"]],
                  writes=BS(C, O_WP2, 2048), ndma=1, tag=f"4wp{tt}_{oc}")

            ob = oc % 2

            def em_pmm(e, ob=ob):
                out = []
                for k in range(16):
                    out.append(e.matmul(C.pb[ob][:, :],
                                        A(C, O_WP2 + 128 * k, 128),
                                        A(C, O_ATT2 + 512 * k, 512),
                                        start=(k == 0), stop=(k == 15)))
                return out
            P.add("tensor", em_pmm,
                  reads=BS(C, O_WP2, 2048) + BS(C, O_ATT2, 8192),
                  writes=[C.Rpb[ob]], tag=f"4pm{tt}_{oc}")

            yo = O_YT2 + 512 * (oc % 2)

            def em_yb(e, oc=oc, ob=ob, yo=yo):
                return e.tensor_scalar(A(C, yo, 512), C.pb[ob][:, :],
                                       C.biasb_sb[:, oc:oc + 1], None, OP.add)
            P.add("vector", em_yb, reads=[C.Rpb[ob], C.Rconst],
                  writes=BS(C, yo, 512), tag=f"4yb{tt}_{oc}")

            def em_res(e, oc=oc, yo=yo, tt=tt):
                xs = C.x_res[:, oc, 512 * tt:512 * tt + 512]
                return e.tensor_tensor(xs, A(C, yo, 512), xs, OP.add)
            P.add("vector", em_res, reads=BS(C, yo, 512),
                  writes=[C.Rxres[tt]], tag=f"4rs{tt}_{oc}")



def emit_dbg(C, what):
    P, di = C.P, C.di
    O_LNX, O_TMP = 0, 47104
    emit_ln(C, lambda c: C.x_res[:, c, 0:512], [C.Rxres[0]],
            O_LNX, O_TMP, (6, 7), "dbn")
    routs = [Res(f"dbg{i}") for i in range(16)]
    if what == "lnx":
        for c in range(16):
            def em(e, c=c):
                return e.dma_start(
                    out=_dap(di["y"], c * 128 * 2048, [[2048, 128], [1, 512]]),
                    in_=A(C, O_LNX + 512 * c, 512))
            P.add("gpsimd", em, reads=BS(C, O_LNX + 512 * c, 512),
                  writes=[routs[c]], ndma=1, tag=f"dbgo{c}")
        for r in range(4):
            def em2(e, r=r):
                return e.dma_start(
                    out=_dap(di["y"], r * 2048 + 512, [[2048, 1], [1, 512]]),
                    in_=C.rows[0:1, 512 * r:512 * r + 512])
            P.add("gpsimd", em2, reads=[C.Rrows], writes=[routs[4 + r]],
                  ndma=1, tag=f"dbgr{r}")
    else:
        for s in range(4):
            def em_wg(e, s=s):
                return e.dma_start(
                    out=A(C, 8192, 2048),
                    in_=_dap(C.wB["wg"], s * 262144, [[2048, 128], [1, 2048]]))
            P.add("sync", em_wg, reads=[C.RwB["wg"]], writes=BS(C, 8192, 2048),
                  ndma=1, tag=f"dbwg{s}")
            def em_wu(e, s=s):
                return e.dma_start(
                    out=A(C, 12288, 2048),
                    in_=_dap(C.wB["wu"], s * 262144, [[2048, 128], [1, 2048]]))
            P.add("sync", em_wu, reads=[C.RwB["wu"]], writes=BS(C, 12288, 2048),
                  ndma=1, tag=f"dbwu{s}")
            def em_gmm(e):
                out = []
                for k in range(16):
                    out.append(e.matmul(C.pb[0][:, :], A(C, 8192 + 128 * k, 128),
                                        A(C, O_LNX + 512 * k, 512),
                                        start=(k == 0), stop=(k == 15)))
                return out
            P.add("tensor", em_gmm, reads=BS(C, 8192, 2048) + BS(C, O_LNX, 8192),
                  writes=[C.Rpb[0]], tag=f"dbgm{s}")
            def em_umm(e):
                out = []
                for k in range(16):
                    out.append(e.matmul(C.pb[2][:, :], A(C, 12288 + 128 * k, 128),
                                        A(C, O_LNX + 512 * k, 512),
                                        start=(k == 0), stop=(k == 15)))
                return out
            P.add("tensor", em_umm, reads=BS(C, 12288, 2048) + BS(C, O_LNX, 8192),
                  writes=[C.Rpb[2]], tag=f"dbum{s}")
            def em_sil(e):
                return e.activation(AF_(C, 0, 512), C.pb[0][:, :], AF.Silu)
            P.add("scalar", em_sil, reads=[C.Rpb[0]], writes=FS(C, 0, 512),
                  tag=f"dbsi{s}")
            def em_h(e, s=s):
                return e.tensor_tensor(AF_(C, 1024 + 512 * s, 512), AF_(C, 0, 512),
                                       C.pb[2][:, :], OP.mult)
            P.add("vector", em_h, reads=FS(C, 0, 512) + [C.Rpb[2]],
                  writes=FS(C, 1024 + 512 * s, 512), tag=f"dbh{s}")
        for s in range(4):
            def em(e, s=s):
                return e.dma_start(
                    out=_dap(di["y"], s * 128 * 2048, [[2048, 128], [1, 512]]),
                    in_=AF_(C, 1024 + 512 * s, 512))
            P.add("gpsimd", em, reads=FS(C, 1024 + 512 * s, 512),
                  writes=[routs[s]], ndma=1, tag=f"dbgo{s}")
    P.add("gpsimd", None, reads=routs, tag="dbfin")



def emit_P6(C):
    P, di = C.P, C.di
    routs = [Res(f"out{c}") for c in range(16)]
    for c in range(16):
        def em(e, c=c):
            return e.dma_start(
                out=_dap(di["y"], c * 128 * 2048, [[2048, 128], [1, 2048]]),
                in_=C.x_res[:, c, :])
        P.add("gpsimd", em, reads=list(C.Rxres), writes=[routs[c]],
              ndma=1, tag=f"o{c}")
    P.add("gpsimd", None, reads=routs, tag="fin")



_NC_CACHE = {}


def kernel(**inputs):
    if "nc" not in _NC_CACHE:
        _NC_CACHE["nc"] = build_nc()
    nc = _NC_CACHE["nc"]
    in_maps = [_prep_core(inputs, core) for core in range(8)]
    res = run_bass_kernel_spmd(nc, in_maps, core_ids=list(range(8)))
    out = np.zeros((4, 4096, 2048), dtype=np.float32)
    for core in range(8):
        seq, par = core // 2, core % 2
        own = _own_blocks(par)
        own_tok = np.concatenate([np.arange(128 * g, 128 * g + 128) for g in own])
        y = np.asarray(res.results[core]["y"])
        out[seq][own_tok, :] = y.T
    return out


# version 03
# speedup vs baseline: 1.2768x; 1.2768x over previous
import numpy as np
import ml_dtypes
import concourse.bass as bass
import concourse.mybir as mybir
from concourse.bass_utils import run_bass_kernel_spmd
from concourse.alu_op_type import AluOpType as OP

F32 = mybir.dt.float32
BF16 = mybir.dt.bfloat16
I32 = mybir.dt.int32
AF = mybir.ActivationFunctionType
f32 = np.float32

ABF_N = 50176
AF32_N = 8192
SLOT = 512
MAGIC1 = 0x12AB34CD
MAGIC2 = 0x56EF78A1
INV_SQRT_HD = f32(1.0 / np.sqrt(128.0))
EXP_SHIFT = f32(-20.0)

PHASES = (1, 2, 3, 4, 5)
SPARSIFY = True
P2_SB = (2, 3, 6, 7)
P2_LAG = 2
DBG = None

ENGINES = ("sync", "gpsimd", "scalar", "vector", "tensor")

_seq = [0]


class Ev:
    __slots__ = ("eng", "val", "waited", "seq")

    def __init__(self, eng):
        self.eng = eng
        self.val = None
        self.waited = False
        _seq[0] += 1
        self.seq = _seq[0]


class Res:
    __slots__ = ("name", "w", "rs")

    def __init__(self, name=""):
        self.name = name
        self.w = None
        self.rs = []


class Step:
    __slots__ = ("eng", "emit", "deps", "ev", "ndma", "tag", "signal")

    def __init__(self, eng, emit, deps, ev, ndma, tag):
        self.eng = eng
        self.emit = emit
        self.deps = deps
        self.ev = ev
        self.ndma = ndma
        self.tag = tag
        self.signal = False


class Prog:
    def __init__(self):
        self.steps = {e: [] for e in ENGINES}

    def add(self, eng, emit, reads=(), writes=(), ndma=0, tag=""):
        ev = Ev(eng)
        deps = {}
        for r in reads:
            if r.w is not None:
                deps[r.w.seq] = r.w
        for r in writes:
            if r.w is not None:
                deps[r.w.seq] = r.w
            for e in r.rs:
                deps[e.seq] = e
        deps.pop(ev.seq, None)
        for r in reads:
            r.rs.append(ev)
        for r in writes:
            r.w = ev
            r.rs = []
        final = []
        for d in sorted(deps.values(), key=lambda d: d.seq):
            if d.eng == eng == "tensor":
                continue
            final.append(d)
            d.waited = True
        self.steps[eng].append(Step(eng, emit, final, ev, ndma, tag))
        return ev

    def plan(self):
        for eng in ENGINES:
            cnt = 0
            for st in self.steps[eng]:
                if st.ev.waited or st.ndma:
                    st.signal = True
                    cnt += 16 * st.ndma if st.ndma else 1
                    st.ev.val = cnt

    def emit_engine(self, eng_name, eng_obj, sems):
        water = {}
        for st in self.steps[eng_name]:
            need = {}
            for d in st.deps:
                assert d.val is not None, f"dep of {st.tag} not planned"
                need[d.eng] = max(need.get(d.eng, 0), d.val)
            for deng, val in sorted(need.items()):
                if water.get(deng, 0) >= val:
                    continue
                eng_obj.wait_ge(sems[deng], val)
                water[deng] = val
            out = st.emit(eng_obj) if st.emit is not None else None
            if st.signal:
                assert out is not None, f"step {st.tag} must emit to signal"
                instrs = out if isinstance(out, (list, tuple)) else [out]
                if st.ndma:
                    assert len(instrs) == st.ndma, \
                        f"{st.tag}: ndma={st.ndma} got {len(instrs)}"
                    for i in instrs:
                        i.then_inc(sems[eng_name], 16)
                else:
                    instrs[-1].then_inc(sems[eng_name], 1)



def _own_blocks(par):
    keep = (0, 3) if par == 0 else (1, 2)
    return [g for g in range(32) if g % 4 in keep]


def _prep_core(inputs, core):
    seq, par = core // 2, core % 2
    bf = ml_dtypes.bfloat16
    x = np.asarray(inputs["L0_attn_a_x"][seq])
    xT = np.ascontiguousarray(x.T)
    own = _own_blocks(par)
    own_tok = np.concatenate([np.arange(128 * g, 128 * g + 128) for g in own])
    x_own = np.ascontiguousarray(xT[:, own_tok])

    def lhs_stream(W):
        return np.ascontiguousarray(
            W.reshape(16, 128, 16, 128).transpose(0, 3, 2, 1))

    def mov_stream(W):
        return np.ascontiguousarray(W.T.reshape(16, 128, 2048))

    Wqkv = np.asarray(inputs["L0_attn_a_W_qkv"])
    Wq, Wk, Wv = Wqkv[0:2048], Wqkv[2048:4096], Wqkv[4096:6144]
    Wo = np.asarray(inputs["L0_attn_a_W_out"])
    Wqkvb = np.asarray(inputs["L0_attn_b_w_qkv"])
    Wqb, Wkb, Wvb = Wqkvb[0:2048], Wqkvb[2048:4096], Wqkvb[4096:6144]
    Wpb = np.asarray(inputs["L0_attn_b_w_proj"])
    bpb = np.asarray(inputs["L0_attn_b_b_proj"])
    Wg = np.asarray(inputs["L0_ffn_w_gate"])
    Wu = np.asarray(inputs["L0_ffn_w_up"])
    Wd = np.asarray(inputs["L0_ffn_w_down"])

    def gate_stream(W):
        return np.ascontiguousarray(
            W.reshape(16, 128, 16, 128, 4).transpose(2, 4, 1, 0, 3))

    wd_h = np.ascontiguousarray(
        Wd.reshape(16, 128, 4, 16, 128).transpose(3, 1, 0, 2, 4)
    ).reshape(16, 128, 64, 128)

    masks = np.zeros((4, 8, 128, 512), dtype=np.float32)
    for t in range(4):
        qg = [own[4 * t + i] for i in range(4)]
        qpos = np.concatenate([128 * g + np.arange(128) for g in qg])
        for j in range(8):
            kpos = 128 * (8 * t + j) + np.arange(128)
            masks[t, j] = (kpos[:, None] <= qpos[None, :]).astype(np.float32)

    bd = np.zeros((128, 128), dtype=np.float32)
    for i in range(4):
        bd[32 * i:32 * i + 32, 32 * i:32 * i + 32] = 1.0

    return {
        "x_own": x_own,
        "x_full": xT,
        "wq": lhs_stream(Wq), "wk": lhs_stream(Wk), "wo": lhs_stream(Wo),
        "wv": mov_stream(Wv),
        "wqb": lhs_stream(Wqb), "wpb": lhs_stream(Wpb),
        "wkb": mov_stream(Wkb), "wvb": mov_stream(Wvb),
        "wg": gate_stream(Wg), "wu": gate_stream(Wu), "wd": wd_h,
        "biasb": np.ascontiguousarray(bpb.reshape(16, 128).T),
        "masks": masks.astype(bf),
        "bdones_f": bd,
        "bdones_b": bd.astype(bf),
        "parity": np.array([[par]], dtype=np.int32),
    }



class Ctx:
    pass


def _dap(t, off, dims):
    return bass.AP(t, off, dims)


def build_nc(phases=PHASES):
    nc = bass.Bass("TRN2", target_bir_lowering=False)
    C = Ctx()
    C.nc = nc
    P = Prog()
    C.P = P

    di = {}
    di["x_own"] = nc.dram_tensor("x_own", [2048, 2048], F32, kind="ExternalInput")
    di["x_full"] = nc.dram_tensor("x_full", [2048, 4096], F32, kind="ExternalInput")
    for n in ("wq", "wk", "wo", "wqb", "wpb"):
        di[n] = nc.dram_tensor(n, [16, 128, 16, 128], F32, kind="ExternalInput")
    for n in ("wv", "wkb", "wvb"):
        di[n] = nc.dram_tensor(n, [16, 128, 2048], F32, kind="ExternalInput")
    di["wg"] = nc.dram_tensor("wg", [16, 4, 128, 16, 128], F32, kind="ExternalInput")
    di["wu"] = nc.dram_tensor("wu", [16, 4, 128, 16, 128], F32, kind="ExternalInput")
    di["wd"] = nc.dram_tensor("wd", [16, 128, 64, 128], F32, kind="ExternalInput")
    di["biasb"] = nc.dram_tensor("biasb", [128, 16], F32, kind="ExternalInput")
    di["masks"] = nc.dram_tensor("masks", [4, 8, 128, 512], BF16, kind="ExternalInput")
    di["bdones_f"] = nc.dram_tensor("bdones_f", [128, 128], F32, kind="ExternalInput")
    di["bdones_b"] = nc.dram_tensor("bdones_b", [128, 128], BF16, kind="ExternalInput")
    di["parity"] = nc.dram_tensor("parity", [1, 1], I32, kind="ExternalInput")
    di["y"] = nc.dram_tensor("y", [2048, 2048], F32, kind="ExternalOutput")
    C.di = di

    C.wsize = {
        "wq": 16 * 128 * 16 * 128, "wk": 16 * 128 * 16 * 128,
        "wo": 16 * 128 * 16 * 128, "wqb": 16 * 128 * 16 * 128,
        "wpb": 16 * 128 * 16 * 128,
        "wv": 16 * 128 * 2048, "wkb": 16 * 128 * 2048, "wvb": 16 * 128 * 2048,
        "wg": 16 * 4 * 128 * 16 * 128, "wu": 16 * 4 * 128 * 16 * 128,
        "wd": 16 * 128 * 64 * 128,
    }
    C.wB = {n: nc.dram_tensor(n + "_bf", [C.wsize[n] // 128, 128], BF16)
            for n in C.wsize}
    C.RwB = {n: Res("R_" + n) for n in C.wB}

    C.Kscr = nc.dram_tensor("Kscr", [16, 128, 4096], BF16)
    C.Vscr = nc.dram_tensor("Vscr", [16, 4096, 128], BF16)
    C.RK = Res("RK")
    C.RV = Res("RV")

    C.sh_ex = nc.dram_tensor("sh_ex", [2, 128, 2064], F32, addr_space="Shared")
    C.sh_fl = nc.dram_tensor("sh_fl", [2, 1, 16], I32, addr_space="Shared")

    from contextlib import ExitStack
    with ExitStack() as _st:
        en = _st.enter_context
        x_res = en(nc.sbuf_tensor("x_res", [128, 16, 2048], BF16))
        abf = en(nc.sbuf_tensor("abf", [128, ABF_N], BF16))
        af32 = en(nc.sbuf_tensor("af32", [128, AF32_N], F32))
        rows = en(nc.sbuf_tensor("rows", [1, 2048], F32))
        rows_bf = en(nc.sbuf_tensor("rows_bf", [1, 2048], BF16))
        ones_row = en(nc.sbuf_tensor("ones_row", [1, 128], BF16))
        ones_col = en(nc.sbuf_tensor("ones_col", [128, 1], BF16))
        bdb_sb = en(nc.sbuf_tensor("bdb_sb", [128, 128], BF16))
        bdf_sb = en(nc.sbuf_tensor("bdf_sb", [128, 128], F32))
        biasb_sb = en(nc.sbuf_tensor("biasb_sb", [128, 16], F32))
        par_sb = en(nc.sbuf_tensor("par_sb", [1, 1], I32))
        flag_sb = en(nc.sbuf_tensor("flag_sb", [1, 16], I32))
        pfl_sb = en(nc.sbuf_tensor("pfl_sb", [1, 16], I32))
        pbs = [en(nc.psum_tensor(f"pb{i}", [128, 512], F32)) for i in range(8)]
        s_sync = en(nc.semaphore("s_sync"))
        s_gp = en(nc.semaphore("s_gp"))
        s_sc = en(nc.semaphore("s_sc"))
        s_ve = en(nc.semaphore("s_ve"))
        s_pe = en(nc.semaphore("s_pe"))
        s_poll = en(nc.semaphore("s_poll"))
        block = en(nc.Block())

        C.x_res = x_res
        C.abf = abf
        C.af32 = af32
        C.rows = rows
        C.rows_bf = rows_bf
        C.ones_row = ones_row
        C.ones_col = ones_col
        C.bdb_sb = bdb_sb
        C.bdf_sb = bdf_sb
        C.biasb_sb = biasb_sb
        C.par_sb = par_sb
        C.flag_sb = flag_sb
        C.pfl_sb = pfl_sb
        C.pb = pbs
        C.Rpb = [Res(f"pb{i}") for i in range(8)]
        C.Rxres = [Res(f"xres{t}") for t in range(4)]
        C.Rrows = Res("rows")
        C.Rden = Res("rden")
        C.Rconst = Res("const")
        C.s_poll = s_poll
        C.sems = {"sync": s_sync, "gpsimd": s_gp, "scalar": s_sc,
                  "vector": s_ve, "tensor": s_pe}
        C.bslots = [Res(f"b{i}") for i in range(ABF_N // SLOT)]
        C.fslots = [Res(f"f{i}") for i in range(AF32_N // SLOT)]

        emit_P0(C)
        if DBG is not None:
            emit_dbg(C, DBG)
            emit_P6_skip = True
            phases = ()
        else:
            emit_P6_skip = False
        if 1 in phases:
            emit_P1(C)
        feed_casts(C, 10 ** 6)
        if 2 in phases:
            emit_P2(C)
        if 3 in phases:
            emit_ffn(C, "P3")
        if 4 in phases:
            emit_P4(C)
        if 5 in phases:
            emit_ffn(C, "P5")
        if not emit_P6_skip:
            emit_P6(C)

        P.plan()

        @block.sync
        def _(e):
            P.emit_engine("sync", e, C.sems)

        @block.gpsimd
        def _(e):
            P.emit_engine("gpsimd", e, C.sems)

        @block.scalar
        def _(e):
            P.emit_engine("scalar", e, C.sems)

        @block.vector
        def _(e):
            P.emit_engine("vector", e, C.sems)

        @block.tensor
        def _(e):
            P.emit_engine("tensor", e, C.sems)

    return nc



def A(C, off, n):
    assert off + n <= ABF_N, (off, n)
    return C.abf[:, off:off + n]


def AF_(C, off, n):
    assert off + n <= AF32_N, (off, n)
    return C.af32[:, off:off + n]


def BS(C, off, n):
    return C.bslots[off // SLOT:(off + n + SLOT - 1) // SLOT]


def FS(C, off, n):
    return C.fslots[off // SLOT:(off + n + SLOT - 1) // SLOT]



def emit_P0(C):
    P, di = C.P, C.di

    P.add("vector", lambda e: e.memset(C.ones_row[:, :], 1.0),
          writes=[C.Rconst], tag="c_or")
    P.add("vector", lambda e: e.memset(C.ones_col[:, :], 1.0),
          writes=[C.Rconst], tag="c_oc")
    P.add("vector", lambda e: e.memset(C.flag_sb[:, 0:1], MAGIC1),
          writes=[C.Rconst], tag="c_f1")
    P.add("vector", lambda e: e.memset(C.flag_sb[:, 1:2], MAGIC2),
          writes=[C.Rconst], tag="c_f2")
    P.add("sync", lambda e: [
        e.dma_start(out=C.bdb_sb[:, :], in_=_dap(di["bdones_b"], 0, [[128, 128], [1, 128]])),
        e.dma_start(out=C.bdf_sb[:, :], in_=_dap(di["bdones_f"], 0, [[128, 128], [1, 128]])),
        e.dma_start(out=C.biasb_sb[:, :], in_=_dap(di["biasb"], 0, [[16, 128], [1, 16]])),
        e.dma_start(out=C.par_sb[:, :], in_=_dap(di["parity"], 0, [[1, 1], [1, 1]])),
    ], writes=[C.Rconst], ndma=4, tag="c_ld")

    for c in range(16):
        def em(e, c=c):
            return e.dma_start(
                out=C.x_res[:, c, :],
                in_=_dap(di["x_own"], c * 128 * 2048, [[2048, 128], [1, 2048]]))
        P.add("gpsimd", em, writes=list(C.Rxres), ndma=1, tag=f"xi{c}")

    C._bounce_i = 0

    def cast_jobs(name, nchunks=None):
        cols = C.wsize[name] // 128
        ch = 2048
        return [(name, i, min(ch, cols - i), cols) for i in range(0, cols, ch)]

    def emit_cast(job):
        name, i, w, cols = job
        slot = C._bounce_i % 2
        C._bounce_i += 1
        boff = slot * 2048

        def eml(e, name=name, i=i, w=w, cols=cols, boff=boff):
            return e.dma_start(out=AF_(C, boff, w),
                               in_=_dap(di[name], i, [[cols, 128], [1, w]]))

        def ems(e, name=name, i=i, w=w, cols=cols, boff=boff):
            return e.dma_start(out=_dap(C.wB[name], i, [[cols, 128], [1, w]]),
                               in_=AF_(C, boff, w))
        P.add("sync", eml, writes=FS(C, boff, w), ndma=1, tag=f"pl{name}{i}")
        P.add("gpsimd", ems, reads=FS(C, boff, w), writes=[C.RwB[name]],
              ndma=1, tag=f"ps{name}{i}")

    C.emit_cast = emit_cast
    for job in cast_jobs("wk") + cast_jobs("wv"):
        emit_cast(job)
    C.cast_queue = []
    for n2 in ["wq", "wo", "wg", "wu", "wd", "wqb", "wkb", "wvb", "wpb"]:
        C.cast_queue.extend(cast_jobs(n2))


def feed_casts(C, n):
    for _ in range(n):
        if not C.cast_queue:
            return
        C.emit_cast(C.cast_queue.pop(0))



def emit_ln(C, get_x, reads_x, lnx_off, tmp_off, banks, tag):
    P = C.P
    W = 512
    b0, b1 = banks
    x2o = [tmp_off, tmp_off + W]
    rbco, mbco = tmp_off + 2 * W, tmp_off + 3 * W

    def em_sum(e):
        out = []
        for c in range(16):
            out.append(e.matmul(C.pb[b0][0:1, 0:W], C.ones_col[:, :], get_x(c),
                                start=(c == 0), stop=(c == 15)))
        return out
    P.add("tensor", em_sum, reads=list(reads_x) + [C.Rconst],
          writes=[C.Rpb[b0]], tag=tag + "sum")

    for c in range(16):
        o = x2o[c % 2]

        def em_sq(e, c=c, o=o):
            return e.activation(A(C, o, W), get_x(c), AF.Square)
        P.add("scalar", em_sq, reads=list(reads_x), writes=BS(C, o, W),
              tag=f"{tag}sq{c}")

        def em_mm(e, c=c, o=o):
            return e.matmul(C.pb[b1][0:1, 0:W], C.ones_col[:, :], A(C, o, W),
                            start=(c == 0), stop=(c == 15))
        P.add("tensor", em_mm, reads=BS(C, o, W) + [C.Rconst],
              writes=[C.Rpb[b1]], tag=f"{tag}sm{c}")

    mean = C.rows[0:1, 0:W]
    m2 = C.rows[0:1, 512:512 + W]
    var = C.rows[0:1, 1024:1024 + W]
    rstd = C.rows[0:1, 1536:1536 + W]
    rstd_bf = C.rows_bf[0:1, 0:W]
    nmr_bf = C.rows_bf[0:1, 512:512 + W]

    P.add("vector", lambda e: e.tensor_scalar(mean, C.pb[b0][0:1, 0:W],
                                              float(1.0 / 2048), None, OP.mult),
          reads=[C.Rpb[b0]], writes=[C.Rrows], tag=tag + "mn")
    P.add("vector", lambda e: e.tensor_tensor(m2, mean, mean, OP.mult),
          reads=[C.Rrows], writes=[C.Rrows], tag=tag + "m2")
    P.add("vector", lambda e: e.scalar_tensor_tensor(var, C.pb[b1][0:1, 0:W],
                                                     float(1.0 / 2048), m2,
                                                     OP.mult, OP.subtract),
          reads=[C.Rpb[b1], C.Rrows], writes=[C.Rrows], tag=tag + "vr")
    P.add("scalar", lambda e: e.activation(m2, var, AF.Ln, bias=f32(1e-5)),
          reads=[C.Rrows], writes=[C.Rrows], tag=tag + "ln")
    P.add("scalar", lambda e: e.activation(rstd, m2, AF.Exp, scale=f32(-0.5)),
          reads=[C.Rrows], writes=[C.Rrows], tag=tag + "rs")
    P.add("vector", lambda e: e.scalar_tensor_tensor(var, mean, float(-1.0),
                                                     rstd, OP.mult, OP.mult),
          reads=[C.Rrows], writes=[C.Rrows], tag=tag + "nm")
    P.add("vector", lambda e: e.tensor_copy(rstd_bf, rstd),
          reads=[C.Rrows], writes=[C.Rrows], tag=tag + "cr")
    P.add("vector", lambda e: e.tensor_copy(nmr_bf, var),
          reads=[C.Rrows], writes=[C.Rrows], tag=tag + "cm")

    P.add("tensor", lambda e: e.matmul(C.pb[b0][:, 0:W], C.ones_row[:, :],
                                       rstd_bf, start=True, stop=True),
          reads=[C.Rrows, C.Rconst], writes=[C.Rpb[b0]], tag=tag + "br")
    P.add("tensor", lambda e: e.matmul(C.pb[b1][:, 0:W], C.ones_row[:, :],
                                       nmr_bf, start=True, stop=True),
          reads=[C.Rrows, C.Rconst], writes=[C.Rpb[b1]], tag=tag + "bm")
    P.add("scalar", lambda e: e.copy(A(C, rbco, W), C.pb[b0][:, 0:W]),
          reads=[C.Rpb[b0]], writes=BS(C, rbco, W), tag=tag + "er")
    P.add("scalar", lambda e: e.copy(A(C, mbco, W), C.pb[b1][:, 0:W]),
          reads=[C.Rpb[b1]], writes=BS(C, mbco, W), tag=tag + "em")

    for c in range(16):
        o = x2o[c % 2]

        def em_t1(e, c=c, o=o):
            return e.tensor_tensor(A(C, o, W), get_x(c), A(C, rbco, W), OP.mult)
        P.add("vector", em_t1, reads=list(reads_x) + BS(C, rbco, W),
              writes=BS(C, o, W), tag=f"{tag}t1{c}")

        def em_t2(e, c=c, o=o):
            return e.tensor_tensor(A(C, lnx_off + 512 * c, W), A(C, o, W),
                                   A(C, mbco, W), OP.add)
        P.add("vector", em_t2, reads=BS(C, o, W) + BS(C, mbco, W),
              writes=BS(C, lnx_off + 512 * c, W), tag=f"{tag}t2{c}")



def emit_P1(C):
    P, di = C.P, C.di
    O_XBF, O_LNX, O_TMP, O_WK, O_KEV, O_WV, O_VEV = \
        0, 16384, 32768, 34816, 38912, 39936, 48128

    for sl in range(8):
        lnoff = O_LNX + (sl % 2) * 8192
        xoff = O_XBF + (sl % 2) * 8192

        def em_lx(e, sl=sl, xoff=xoff):
            out = []
            for c in range(16):
                out.append(e.dma_start(
                    out=A(C, xoff + 512 * c, 512),
                    in_=_dap(di["x_full"], c * 128 * 4096 + 512 * sl,
                             [[4096, 128], [1, 512]])))
            return out
        P.add("gpsimd", em_lx, writes=BS(C, xoff, 8192), ndma=16,
              tag=f"1lx{sl}")
        feed_casts(C, 8)

        emit_ln(C, lambda c, xoff=xoff: A(C, xoff + 512 * c, 512),
                BS(C, xoff, 8192), lnoff, O_TMP, (6, 7), f"1n{sl}")

        for oc in range(16):
            wo = O_WK + 2048 * (oc % 2)

            def em_wk(e, oc=oc, wo=wo):
                return e.dma_start(
                    out=A(C, wo, 2048),
                    in_=_dap(C.wB["wk"], oc * 262144, [[2048, 128], [1, 2048]]))
            P.add("sync", em_wk, reads=[C.RwB["wk"]], writes=BS(C, wo, 2048),
                  ndma=1, tag=f"1wk{sl}_{oc}")

            bank = oc % 4

            def em_mm(e, wo=wo, bank=bank, lnoff=lnoff):
                out = []
                for k in range(16):
                    out.append(e.matmul(C.pb[bank][:, :],
                                        A(C, wo + 128 * k, 128),
                                        A(C, lnoff + 512 * k, 512),
                                        start=(k == 0), stop=(k == 15)))
                return out
            P.add("tensor", em_mm,
                  reads=BS(C, wo, 2048) + BS(C, lnoff, 8192),
                  writes=[C.Rpb[bank]], tag=f"1km{sl}_{oc}")

            eo = O_KEV + 512 * (oc % 2)

            def em_ev(e, bank=bank, eo=eo):
                return e.tensor_copy(A(C, eo, 512), C.pb[bank][:, :])
            P.add("vector", em_ev, reads=[C.Rpb[bank]], writes=BS(C, eo, 512),
                  tag=f"1ke{sl}_{oc}")

            def em_st(e, oc=oc, eo=eo, sl=sl):
                return e.dma_start(
                    out=_dap(C.Kscr, oc * 128 * 4096 + 512 * sl,
                             [[4096, 128], [1, 512]]),
                    in_=A(C, eo, 512))
            P.add("gpsimd", em_st, reads=BS(C, eo, 512), writes=[C.RK],
                  ndma=1, tag=f"1ks{sl}_{oc}")

        for nt in range(4):
            for k in range(16):
                def em_wv(e, nt=nt, k=k):
                    return e.dma_start(
                        out=A(C, O_WV + 512 * k, 512),
                        in_=_dap(C.wB["wv"], k * 262144 + 512 * nt,
                                 [[2048, 128], [1, 512]]))
                P.add("sync", em_wv, reads=[C.RwB["wv"]],
                      writes=BS(C, O_WV + 512 * k, 512), ndma=1,
                      tag=f"1wv{sl}_{nt}_{k}")
            for blk in range(4):
                bank = 4 + blk % 2

                def em_vmm(e, blk=blk, bank=bank, lnoff=lnoff):
                    out = []
                    for k in range(16):
                        out.append(e.matmul(C.pb[bank][:, :],
                                            A(C, lnoff + 512 * k + 128 * blk, 128),
                                            A(C, O_WV + 512 * k, 512),
                                            start=(k == 0), stop=(k == 15)))
                    return out
                P.add("tensor", em_vmm,
                      reads=BS(C, O_WV, 8192) + BS(C, lnoff, 8192),
                      writes=[C.Rpb[bank]], tag=f"1vm{sl}_{nt}_{blk}")

                eo = O_VEV + 512 * (blk % 2)

                def em_vev(e, bank=bank, eo=eo):
                    return e.tensor_copy(A(C, eo, 512), C.pb[bank][:, :])
                P.add("vector", em_vev, reads=[C.Rpb[bank]],
                      writes=BS(C, eo, 512), tag=f"1ve{sl}_{nt}_{blk}")

                def em_vst(e, nt=nt, blk=blk, eo=eo, sl=sl):
                    off = (4 * nt) * 4096 * 128 + (512 * sl + 128 * blk) * 128
                    return e.dma_start(
                        out=_dap(C.Vscr, off,
                                 [[128, 128], [4096 * 128, 4], [1, 128]]),
                        in_=A(C, eo, 512))
                P.add("gpsimd", em_vst, reads=BS(C, eo, 512), writes=[C.RV],
                      ndma=1, tag=f"1vs{sl}_{nt}_{blk}")



def emit_P2(C):
    P, di = C.P, C.di
    O_MASK, O_LNX, O_QH, O_KB, O_VB, O_EXP, O_ATT, O_WQ, O_WO, O_TMP, \
        O_OUN, O_RBC, O_YT = (0, 4096, 12288, 13312, 21504, 29696, 31744,
                              39936, 41984, 44032, 46080, 47104, 48128)
    rden = C.af32[0:1, 4096:4608]
    rden_bf = C.rows_bf[0:1, 1024:1536]
    SB = P2_SB
    LAG = P2_LAG

    for t in range(4):
        PT = 8 * (t + 1)

        def em_mask(e, t=t):
            return e.dma_start(
                out=A(C, O_MASK, 4096),
                in_=_dap(di["masks"], t * 8 * 128 * 512,
                         [[512, 128], [65536, 8], [1, 512]]))
        P.add("sync", em_mask, writes=BS(C, O_MASK, 4096), ndma=1,
              tag=f"2mk{t}")

        emit_ln(C, lambda c, t=t: C.x_res[:, c, 512 * t:512 * t + 512],
                [C.Rxres[t]], O_LNX, O_TMP, (6, 7), f"2n{t}")

        for h in range(16):
            wqo = O_WQ + 2048 * (h % 2)

            def em_wq(e, h=h, wqo=wqo):
                return e.dma_start(
                    out=A(C, wqo, 2048),
                    in_=_dap(C.wB["wq"], h * 262144, [[2048, 128], [1, 2048]]))
            P.add("sync", em_wq, reads=[C.RwB["wq"]],
                  writes=BS(C, wqo, 2048), ndma=1, tag=f"2wq{t}_{h}")

            qb = h % 2

            def em_qmm(e, qb=qb, wqo=wqo):
                out = []
                for k in range(16):
                    out.append(e.matmul(C.pb[qb][:, :],
                                        A(C, wqo + 128 * k, 128),
                                        A(C, O_LNX + 512 * k, 512),
                                        start=(k == 0), stop=(k == 15)))
                return out
            P.add("tensor", em_qmm,
                  reads=BS(C, wqo, 2048) + BS(C, O_LNX, 8192),
                  writes=[C.Rpb[qb]], tag=f"2qm{t}_{h}")

            qo = O_QH + 512 * (h % 2)

            def em_qev(e, qb=qb, qo=qo):
                return e.tensor_copy(A(C, qo, 512), C.pb[qb][:, :])
            P.add("vector", em_qev, reads=[C.Rpb[qb]], writes=BS(C, qo, 512),
                  tag=f"2qe{t}_{h}")

            ko = O_KB + 4096 * (h % 2)
            vo = O_VB + 4096 * (h % 2)

            def em_k(e, h=h, ko=ko, PT=PT):
                return e.dma_start(
                    out=A(C, ko, 128 * PT),
                    in_=_dap(C.Kscr, h * 128 * 4096, [[4096, 128], [1, 128 * PT]]))
            P.add("sync", em_k, reads=[C.RK], writes=BS(C, ko, 4096), ndma=1,
                  tag=f"2kl{t}_{h}")

            def em_v(e, h=h, vo=vo, PT=PT):
                return e.dma_start(
                    out=A(C, vo, 128 * PT),
                    in_=_dap(C.Vscr, h * 4096 * 128,
                             [[128, 128], [128 * 128, PT], [1, 128]]))
            P.add("sync", em_v, reads=[C.RV], writes=BS(C, vo, 4096), ndma=1,
                  tag=f"2vl{t}_{h}")

            tail = []

            def emit_dp(kc):
                eo, erd = tail[kc]

                def em_den(e, kc=kc, eo=eo, PT=PT):
                    return e.matmul(C.pb[4][0:1, :], C.ones_col[:, :],
                                    A(C, eo, 512),
                                    start=(kc == 0), stop=(kc == PT - 1))
                P.add("tensor", em_den, reads=erd + [C.Rconst],
                      writes=[C.Rpb[4]], tag=f"2d{t}_{h}_{kc}")

                def em_pv(e, kc=kc, eo=eo, vo=vo, PT=PT):
                    return e.matmul(C.pb[5][:, :], A(C, vo + 128 * kc, 128),
                                    A(C, eo, 512),
                                    start=(kc == 0), stop=(kc == PT - 1))
                P.add("tensor", em_pv, reads=erd + BS(C, vo + 128 * kc, 128),
                      writes=[C.Rpb[5]], tag=f"2p{t}_{h}_{kc}")

            for kc in range(PT):
                sb = SB[kc % len(SB)]

                def em_s(e, kc=kc, sb=sb, ko=ko, qo=qo):
                    return e.matmul(C.pb[sb][:, :], A(C, ko + 128 * kc, 128),
                                    A(C, qo, 512), start=True, stop=True)
                P.add("tensor", em_s,
                      reads=BS(C, ko + 128 * kc, 128) + BS(C, qo, 512),
                      writes=[C.Rpb[sb]], tag=f"2s{t}_{h}_{kc}")

                eo = O_EXP + 512 * (kc % len(SB))

                def em_exp(e, sb=sb, eo=eo):
                    return e.activation(A(C, eo, 512), C.pb[sb][:, :], AF.Exp,
                                        bias=EXP_SHIFT, scale=INV_SQRT_HD)
                P.add("scalar", em_exp, reads=[C.Rpb[sb]],
                      writes=BS(C, eo, 512), tag=f"2e{t}_{h}_{kc}")

                if kc >= 8 * t:
                    j = kc - 8 * t
                    mo = O_TMP + 512 * (kc % len(SB))

                    def em_msk(e, j=j, eo=eo, mo=mo):
                        return e.tensor_tensor(
                            A(C, mo, 512), A(C, eo, 512),
                            C.abf[:, O_MASK + 512 * j:O_MASK + 512 * j + 512],
                            OP.mult)
                    P.add("vector", em_msk,
                          reads=BS(C, eo, 512) + BS(C, O_MASK, 4096),
                          writes=BS(C, mo, 512), tag=f"2x{t}_{h}_{kc}")
                    tail.append((mo, BS(C, mo, 512)))
                else:
                    tail.append((eo, BS(C, eo, 512)))

                if kc >= LAG:
                    emit_dp(kc - LAG)
            for kc in range(max(0, PT - LAG), PT):
                emit_dp(kc)

            P.add("vector", lambda e: e.reciprocal(rden, C.pb[4][0:1, :]),
                  reads=[C.Rpb[4]], writes=[C.Rden] + FS(C, 4096, 512),
                  tag=f"2r{t}_{h}")
            P.add("vector", lambda e: e.tensor_copy(rden_bf, rden),
                  reads=[C.Rden], writes=[C.Rden], tag=f"2rc{t}_{h}")
            P.add("tensor", lambda e, qb=qb: e.matmul(C.pb[qb][:, :],
                                                      C.ones_row[:, :],
                                                      rden_bf, start=True,
                                                      stop=True),
                  reads=[C.Rden, C.Rconst], writes=[C.Rpb[qb]], tag=f"2b{t}_{h}")

            ro = O_RBC + 512 * (h % 2)
            oo = O_OUN + 512 * (h % 2)
            ao = O_ATT + 512 * h

            def em_rbc(e, ro=ro, qb=qb):
                return e.tensor_copy(A(C, ro, 512), C.pb[qb][:, :])
            P.add("vector", em_rbc, reads=[C.Rpb[qb]], writes=BS(C, ro, 512),
                  tag=f"2rb{t}_{h}")

            def em_oun(e, oo=oo):
                return e.tensor_copy(A(C, oo, 512), C.pb[5][:, :])
            P.add("vector", em_oun, reads=[C.Rpb[5]], writes=BS(C, oo, 512),
                  tag=f"2ou{t}_{h}")

            def em_att(e, ao=ao, oo=oo, ro=ro):
                return e.tensor_tensor(A(C, ao, 512), A(C, oo, 512),
                                       A(C, ro, 512), OP.mult)
            P.add("vector", em_att, reads=BS(C, oo, 512) + BS(C, ro, 512),
                  writes=BS(C, ao, 512), tag=f"2a{t}_{h}")

        for oc in range(16):
            def em_wo(e, oc=oc):
                return e.dma_start(
                    out=A(C, O_WO, 2048),
                    in_=_dap(C.wB["wo"], oc * 262144, [[2048, 128], [1, 2048]]))
            P.add("sync", em_wo, reads=[C.RwB["wo"]],
                  writes=BS(C, O_WO, 2048), ndma=1, tag=f"2wo{t}_{oc}")

            ob = oc % 2

            def em_omm(e, ob=ob):
                out = []
                for k in range(16):
                    out.append(e.matmul(C.pb[ob][:, :],
                                        A(C, O_WO + 128 * k, 128),
                                        A(C, O_ATT + 512 * k, 512),
                                        start=(k == 0), stop=(k == 15)))
                return out
            P.add("tensor", em_omm,
                  reads=BS(C, O_WO, 2048) + BS(C, O_ATT, 8192),
                  writes=[C.Rpb[ob]], tag=f"2om{t}_{oc}")

            yo = O_YT + 512 * (oc % 2)

            def em_yev(e, ob=ob, yo=yo):
                return e.tensor_copy(A(C, yo, 512), C.pb[ob][:, :])
            P.add("vector", em_yev, reads=[C.Rpb[ob]], writes=BS(C, yo, 512),
                  tag=f"2ye{t}_{oc}")

            def em_res(e, oc=oc, yo=yo, t=t):
                xs = C.x_res[:, oc, 512 * t:512 * t + 512]
                return e.tensor_tensor(xs, A(C, yo, 512), xs, OP.add)
            P.add("vector", em_res, reads=BS(C, yo, 512),
                  writes=[C.Rxres[t]], tag=f"2rs{t}_{oc}")



def emit_ffn(C, ph):
    P = C.P
    O_LNX, O_WG, O_WU, O_HSP, O_TMP, O_YT = 0, 8192, 12288, 14336, 47104, 49152
    O_WD = 0

    for tt in range(4):
        emit_ln(C, lambda c, tt=tt: C.x_res[:, c, 512 * tt:512 * tt + 512],
                [C.Rxres[tt]], O_LNX, O_TMP, (6, 7), f"{ph}n{tt}")

        for gc in range(16):
            for s in range(4):
                wgo = O_WG + 2048 * (s % 2)

                def em_wg(e, gc=gc, s=s, wgo=wgo):
                    return e.dma_start(
                        out=A(C, wgo, 2048),
                        in_=_dap(C.wB["wg"], (gc * 4 + s) * 262144,
                                 [[2048, 128], [1, 2048]]))
                P.add("sync", em_wg, reads=[C.RwB["wg"]],
                      writes=BS(C, wgo, 2048), ndma=1, tag=f"{ph}wg{tt}_{gc}_{s}")

                def em_wu(e, gc=gc, s=s):
                    return e.dma_start(
                        out=A(C, O_WU, 2048),
                        in_=_dap(C.wB["wu"], (gc * 4 + s) * 262144,
                                 [[2048, 128], [1, 2048]]))
                P.add("sync", em_wu, reads=[C.RwB["wu"]],
                      writes=BS(C, O_WU, 2048), ndma=1, tag=f"{ph}wu{tt}_{gc}_{s}")

                gb = s % 2
                ub = 2 + s % 2

                def em_gmm(e, wgo=wgo, gb=gb):
                    out = []
                    for k in range(16):
                        out.append(e.matmul(C.pb[gb][:, :],
                                            A(C, wgo + 128 * k, 128),
                                            A(C, O_LNX + 512 * k, 512),
                                            start=(k == 0), stop=(k == 15)))
                    return out
                P.add("tensor", em_gmm,
                      reads=BS(C, wgo, 2048) + BS(C, O_LNX, 8192),
                      writes=[C.Rpb[gb]], tag=f"{ph}gm{tt}_{gc}_{s}")

                def em_umm(e, ub=ub):
                    out = []
                    for k in range(16):
                        out.append(e.matmul(C.pb[ub][:, :],
                                            A(C, O_WU + 128 * k, 128),
                                            A(C, O_LNX + 512 * k, 512),
                                            start=(k == 0), stop=(k == 15)))
                    return out
                P.add("tensor", em_umm,
                      reads=BS(C, O_WU, 2048) + BS(C, O_LNX, 8192),
                      writes=[C.Rpb[ub]], tag=f"{ph}um{tt}_{gc}_{s}")

                so = 512 * (s % 2)

                def em_sil(e, gb=gb, so=so):
                    return e.activation(AF_(C, so, 512), C.pb[gb][:, :], AF.Silu)
                P.add("scalar", em_sil, reads=[C.Rpb[gb]],
                      writes=FS(C, so, 512), tag=f"{ph}si{tt}_{gc}_{s}")

                ho = 1024 + 512 * s

                def em_h(e, ub=ub, so=so, ho=ho):
                    return e.tensor_tensor(AF_(C, ho, 512), AF_(C, so, 512),
                                           C.pb[ub][:, :], OP.mult)
                P.add("vector", em_h, reads=FS(C, so, 512) + [C.Rpb[ub]],
                      writes=FS(C, ho, 512), tag=f"{ph}h{tt}_{gc}_{s}")

            for s in range(4):
                def em_abs(e, s=s):
                    return e.tensor_tensor(AF_(C, 3072 + 512 * s, 512),
                                           AF_(C, 1024 + 512 * s, 512),
                                           AF_(C, 1024 + 512 * s, 512),
                                           OP.mult)
                P.add("vector", em_abs, reads=FS(C, 1024 + 512 * s, 512),
                      writes=FS(C, 3072 + 512 * s, 512), tag=f"{ph}ab{tt}_{gc}_{s}")
            QO = 5120
            spec = [(QO, 3072, 3584, OP.max), (QO + 512, 3072, 3584, OP.min),
                    (QO + 1024, 4096, 4608, OP.max),
                    (QO + 1536, 4096, 4608, OP.min)]
            for do, o1, o2, op in spec:
                def em_p(e, do=do, o1=o1, o2=o2, op=op):
                    return e.tensor_tensor(AF_(C, do, 512), AF_(C, o1, 512),
                                           AF_(C, o2, 512), op)
                P.add("vector", em_p,
                      reads=FS(C, o1, 512) + FS(C, o2, 512),
                      writes=FS(C, do, 512), tag=f"{ph}mx{tt}_{gc}_{do}")
            P.add("vector", lambda e: e.tensor_tensor(
                AF_(C, QO + 2048, 512), AF_(C, QO, 512), AF_(C, QO + 1024, 512),
                OP.min),
                reads=FS(C, QO, 512) + FS(C, QO + 1024, 512),
                writes=FS(C, QO + 2048, 512), tag=f"{ph}t1{tt}_{gc}")
            P.add("vector", lambda e: e.tensor_tensor(
                AF_(C, QO, 512), AF_(C, QO + 512, 512), AF_(C, QO + 1536, 512),
                OP.max),
                reads=FS(C, QO + 512, 512) + FS(C, QO + 1536, 512),
                writes=FS(C, QO, 512), tag=f"{ph}t2{tt}_{gc}")
            P.add("vector", lambda e: e.tensor_tensor(
                AF_(C, QO + 2560, 512), AF_(C, QO + 2048, 512), AF_(C, QO, 512),
                OP.max),
                reads=FS(C, QO + 2048, 512) + FS(C, QO, 512),
                writes=FS(C, QO + 2560, 512), tag=f"{ph}m2{tt}_{gc}")
            for s in range(4):
                def em_ge(e, s=s):
                    return e.tensor_tensor(AF_(C, QO + 512, 512),
                                           AF_(C, 3072 + 512 * s, 512),
                                           AF_(C, QO + 2560, 512), OP.is_ge)
                P.add("vector", em_ge,
                      reads=FS(C, 3072 + 512 * s, 512) + FS(C, QO + 2560, 512),
                      writes=FS(C, QO + 512, 512), tag=f"{ph}ge{tt}_{gc}_{s}")

                hso = O_HSP + 512 * (4 * gc + s)

                if SPARSIFY:
                    def em_ap(e, s=s, hso=hso):
                        return e.tensor_tensor(A(C, hso, 512),
                                               AF_(C, 1024 + 512 * s, 512),
                                               AF_(C, QO + 512, 512), OP.mult)
                    P.add("vector", em_ap,
                          reads=FS(C, 1024 + 512 * s, 512) + FS(C, QO + 512, 512),
                          writes=BS(C, hso, 512), tag=f"{ph}sp{tt}_{gc}_{s}")
                else:
                    def em_ap(e, s=s, hso=hso):
                        return e.tensor_copy(A(C, hso, 512),
                                             AF_(C, 1024 + 512 * s, 512))
                    P.add("vector", em_ap,
                          reads=FS(C, 1024 + 512 * s, 512),
                          writes=BS(C, hso, 512), tag=f"{ph}sp{tt}_{gc}_{s}")

        for m in range(16):
            db = 4 + m % 2
            for half in range(2):
                wdo = O_WD + 4096 * ((2 * m + half) % 3)

                def em_wd(e, m=m, half=half, wdo=wdo):
                    return e.dma_start(
                        out=A(C, wdo, 4096),
                        in_=_dap(C.wB["wd"], m * 1048576 + 4096 * half,
                                 [[8192, 128], [1, 4096]]))
                P.add("sync", em_wd, reads=[C.RwB["wd"]],
                      writes=BS(C, wdo, 4096), ndma=1,
                      tag=f"{ph}wd{tt}_{m}_{half}")

                def em_dmm(e, wdo=wdo, db=db, half=half):
                    out = []
                    for i in range(32):
                        out.append(e.matmul(
                            C.pb[db][:, :],
                            A(C, wdo + 128 * i, 128),
                            A(C, O_HSP + 512 * (32 * half + i), 512),
                            start=(half == 0 and i == 0),
                            stop=(half == 1 and i == 31)))
                    return out
                P.add("tensor", em_dmm,
                      reads=BS(C, wdo, 4096) + BS(C, O_HSP + 16384 * half, 16384),
                      writes=[C.Rpb[db]], tag=f"{ph}dm{tt}_{m}_{half}")

            yo = O_YT + 512 * (m % 2)

            def em_ye(e, db=db, yo=yo):
                return e.tensor_copy(A(C, yo, 512), C.pb[db][:, :])
            P.add("vector", em_ye, reads=[C.Rpb[db]], writes=BS(C, yo, 512),
                  tag=f"{ph}ye{tt}_{m}")

            def em_res(e, m=m, yo=yo, tt=tt):
                xs = C.x_res[:, m, 512 * tt:512 * tt + 512]
                return e.tensor_tensor(xs, A(C, yo, 512), xs, OP.add)
            P.add("vector", em_res, reads=BS(C, yo, 512),
                  writes=[C.Rxres[tt]], tag=f"{ph}rs{tt}_{m}")



def emit_P4(C):
    P, di = C.P, C.di
    O_LNX, O_KT, O_VT, O_WM, O_TMP = 0, 8192, 16384, 24576, 26624

    P.add("vector", lambda e: e.memset(AF_(C, 0, 2048), 0.0),
          writes=FS(C, 0, 2048), tag="4bd0")
    P.add("vector", lambda e: e.memset(AF_(C, 2048, 16), 0.0),
          writes=FS(C, 2048, 512), tag="4ks0")

    for tt in range(4):
        emit_ln(C, lambda c, tt=tt: C.x_res[:, c, 512 * tt:512 * tt + 512],
                [C.Rxres[tt]], O_LNX, O_TMP, (6, 7), f"4n{tt}")

        for nt in range(4):
            for k in range(16):
                wko = O_WM + 512 * (k % 2)
                wvo = O_WM + 1024 + 512 * (k % 2)

                def em_wk(e, k=k, nt=nt, wko=wko):
                    return e.dma_start(
                        out=A(C, wko, 512),
                        in_=_dap(C.wB["wkb"], k * 262144 + 512 * nt,
                                 [[2048, 128], [1, 512]]))
                P.add("sync", em_wk, reads=[C.RwB["wkb"]],
                      writes=BS(C, wko, 512), ndma=1, tag=f"4wk{tt}_{nt}_{k}")

                def em_wv(e, k=k, nt=nt, wvo=wvo):
                    return e.dma_start(
                        out=A(C, wvo, 512),
                        in_=_dap(C.wB["wvb"], k * 262144 + 512 * nt,
                                 [[2048, 128], [1, 512]]))
                P.add("sync", em_wv, reads=[C.RwB["wvb"]],
                      writes=BS(C, wvo, 512), ndma=1, tag=f"4wv{tt}_{nt}_{k}")

                def em_kvmm(e, k=k, wko=wko, wvo=wvo):
                    out = []
                    for blk in range(4):
                        lhs = A(C, O_LNX + 512 * k + 128 * blk, 128)
                        out.append(e.matmul(C.pb[blk][:, :], lhs, A(C, wko, 512),
                                            start=(k == 0), stop=(k == 15)))
                        out.append(e.matmul(C.pb[4 + blk][:, :], lhs, A(C, wvo, 512),
                                            start=(k == 0), stop=(k == 15)))
                    return out
                P.add("tensor", em_kvmm,
                      reads=BS(C, wko, 512) + BS(C, wvo, 512) + BS(C, O_LNX, 8192),
                      writes=[C.Rpb[i] for i in range(8)],
                      tag=f"4kv{tt}_{nt}_{k}")

            for blk in range(4):
                kto = O_KT + 2048 * blk + 512 * nt
                vto = O_VT + 2048 * blk + 512 * nt

                def em_kev(e, blk=blk, kto=kto):
                    return e.tensor_relu(A(C, kto, 512), C.pb[blk][:, :])
                P.add("vector", em_kev, reads=[C.Rpb[blk]],
                      writes=BS(C, kto, 512), tag=f"4ke{tt}_{nt}_{blk}")

                def em_vev(e, blk=blk, vto=vto):
                    return e.tensor_copy(A(C, vto, 512), C.pb[4 + blk][:, :])
                P.add("vector", em_vev, reads=[C.Rpb[4 + blk]],
                      writes=BS(C, vto, 512), tag=f"4ve{tt}_{nt}_{blk}")

        for grp in range(16):
            vb = grp % 2

            def em_vk(e, grp=grp, vb=vb):
                out = []
                for blk in range(4):
                    out.append(e.matmul(
                        C.pb[vb][:, 0:128],
                        A(C, O_KT + 2048 * blk + 128 * grp, 128),
                        A(C, O_VT + 2048 * blk + 128 * grp, 128),
                        start=(blk == 0), stop=(blk == 3)))
                return out
            P.add("tensor", em_vk,
                  reads=BS(C, O_KT, 8192) + BS(C, O_VT, 8192),
                  writes=[C.Rpb[vb]], tag=f"4vk{tt}_{grp}")

            def em_vka(e, grp=grp, vb=vb):
                dst = AF_(C, 128 * grp, 128)
                return e.tensor_tensor(dst, C.pb[vb][:, 0:128], dst, OP.add)
            P.add("vector", em_vka, reads=[C.Rpb[vb]] + FS(C, 128 * grp, 128),
                  writes=FS(C, 128 * grp, 128), tag=f"4va{tt}_{grp}")

            def em_ks(e, grp=grp):
                out = []
                for blk in range(4):
                    out.append(e.matmul(
                        C.pb[2][:, grp:grp + 1],
                        A(C, O_KT + 2048 * blk + 128 * grp, 128),
                        C.ones_col[:, :],
                        start=(blk == 0), stop=(blk == 3)))
                return out
            P.add("tensor", em_ks, reads=BS(C, O_KT, 8192) + [C.Rconst],
                  writes=[C.Rpb[2]], tag=f"4ks{tt}_{grp}")

        def em_ksa(e):
            dst = AF_(C, 2048, 16)
            return e.tensor_tensor(dst, C.pb[2][:, 0:16], dst, OP.add)
        P.add("vector", em_ksa, reads=[C.Rpb[2]] + FS(C, 2048, 512),
              writes=FS(C, 2048, 512), tag=f"4ka{tt}")

    for grp in range(16):
        def em_m(e, grp=grp):
            dst = AF_(C, 128 * grp, 128)
            return e.tensor_tensor(dst, dst, C.bdf_sb[:, :], OP.mult)
        P.add("vector", em_m, reads=[C.Rconst] + FS(C, 128 * grp, 128),
              writes=FS(C, 128 * grp, 128), tag=f"4bm{grp}")

    def em_ex(g):
        with (
            g.register("p") as p,
            g.register("off") as off,
            g.register("cnt") as cnt,
            g.register("cond") as cond,
            g.register("fl") as fl,
        ):
            g.reg_load(p, C.par_sb[:1, :1])
            g.reg_mov(off, 0)
            g.reg_add(off, off, p)
            g.reg_mul(off, off, 128 * 2064)
            g.dma_start(bass.AP(C.sh_ex, off, [[2064, 128], [1, 2048]]),
                        AF_(C, 0, 2048)).then_inc(C.s_poll, 16)
            g.reg_add(off, off, 2048)
            g.dma_start(bass.AP(C.sh_ex, off, [[2064, 128], [1, 16]]),
                        AF_(C, 2048, 16)).then_inc(C.s_poll, 16)
            g.wait_ge(C.s_poll, 32)
            g.reg_mov(off, 0)
            g.reg_add(off, off, p)
            g.reg_mul(off, off, 16)
            g.dma_start(bass.AP(C.sh_fl, off, [[16, 1], [1, 16]]),
                        C.flag_sb[:, :]).then_inc(C.s_poll, 16)
            g.wait_ge(C.s_poll, 48)
            g.reg_mov(off, 1)
            g.reg_sub(off, off, p)
            g.reg_mul(off, off, 16)
            g.reg_mov(cnt, 48)
            g.reg_mov(cond, 1)
            with g.While(cond):
                g.reg_add(cnt, cnt, 16)
                g.dma_start(C.pfl_sb[:, :],
                            bass.AP(C.sh_fl, off, [[16, 1], [1, 16]])
                            ).then_inc(C.s_poll, 16)
                g.wait_ge(C.s_poll, cnt)
                g.reg_load(fl, C.pfl_sb[:1, 0:1])
                g.reg_mov(cond, fl)
                g.reg_sub(cond, cond, MAGIC1)
            g.reg_mov(cond, 1)
            with g.While(cond):
                g.reg_add(cnt, cnt, 16)
                g.dma_start(C.pfl_sb[:, :],
                            bass.AP(C.sh_fl, off, [[16, 1], [1, 16]])
                            ).then_inc(C.s_poll, 16)
                g.wait_ge(C.s_poll, cnt)
                g.reg_load(fl, C.pfl_sb[:1, 1:2])
                g.reg_mov(cond, fl)
                g.reg_sub(cond, cond, MAGIC2)
            g.reg_mov(off, 1)
            g.reg_sub(off, off, p)
            g.reg_mul(off, off, 128 * 2064)
            r1 = g.dma_start(AF_(C, 2560, 2048),
                             bass.AP(C.sh_ex, off, [[2064, 128], [1, 2048]]))
            g.reg_add(off, off, 2048)
            r2 = g.dma_start(AF_(C, 4608, 16),
                             bass.AP(C.sh_ex, off, [[2064, 128], [1, 16]]))
            return [r1, r2]
    P.add("gpsimd", em_ex,
          reads=FS(C, 0, 2560) + [C.Rconst],
          writes=FS(C, 2560, 2560), ndma=2, tag="4ex")

    O_BDT, O_KST, O_LNX2, O_QT, O_B, O_ATT2, O_WQ2, O_WP2, O_YT2, O_TMP2 = \
        0, 2048, 2560, 10752, 11776, 12800, 20992, 23040, 25088, 26112

    P.add("vector", lambda e: e.tensor_tensor(A(C, O_BDT, 2048),
                                              AF_(C, 0, 2048),
                                              AF_(C, 2560, 2048), OP.add),
          reads=FS(C, 0, 2048) + FS(C, 2560, 2048),
          writes=BS(C, O_BDT, 2048), tag="4tot")
    P.add("vector", lambda e: e.tensor_tensor(
        AF_(C, 7168, 16),
        AF_(C, 2048, 16), AF_(C, 4608, 16), OP.add),
        reads=FS(C, 2048, 512) + FS(C, 4608, 512),
        writes=FS(C, 7168, 512), tag="4kt")

    for tt in range(4):
        emit_ln(C, lambda c, tt=tt: C.x_res[:, c, 512 * tt:512 * tt + 512],
                [C.Rxres[tt]], O_LNX2, O_TMP2, (6, 7), f"4m{tt}")

        for grp in range(16):
            def em_wq(e, grp=grp):
                return e.dma_start(
                    out=A(C, O_WQ2, 2048),
                    in_=_dap(C.wB["wqb"], grp * 262144, [[2048, 128], [1, 2048]]))
            P.add("sync", em_wq, reads=[C.RwB["wqb"]],
                  writes=BS(C, O_WQ2, 2048), ndma=1, tag=f"4wq{tt}_{grp}")

            qb = grp % 2

            def em_qmm(e, qb=qb):
                out = []
                for k in range(16):
                    out.append(e.matmul(C.pb[qb][:, :],
                                        A(C, O_WQ2 + 128 * k, 128),
                                        A(C, O_LNX2 + 512 * k, 512),
                                        start=(k == 0), stop=(k == 15)))
                return out
            P.add("tensor", em_qmm,
                  reads=BS(C, O_WQ2, 2048) + BS(C, O_LNX2, 8192),
                  writes=[C.Rpb[qb]], tag=f"4qm{tt}_{grp}")

            qo = O_QT + 512 * (grp % 2)

            def em_qev(e, qb=qb, qo=qo):
                return e.tensor_relu(A(C, qo, 512), C.pb[qb][:, :])
            P.add("vector", em_qev, reads=[C.Rpb[qb]], writes=BS(C, qo, 512),
                  tag=f"4qe{tt}_{grp}")

            bo = O_B + 512 * (grp % 2)

            def em_b(e, grp=grp, qo=qo, bo=bo):
                return e.tensor_scalar(
                    A(C, bo, 512), A(C, qo, 512),
                    C.af32[:, 7168 + grp:7168 + grp + 1],
                    None, OP.mult)
            P.add("vector", em_b,
                  reads=BS(C, qo, 512) + FS(C, 7168, 512),
                  writes=BS(C, bo, 512), tag=f"4b{tt}_{grp}")

            db = 2 + grp % 2

            def em_den(e, bo=bo, db=db):
                return e.matmul(C.pb[db][:, :], C.bdb_sb[:, :], A(C, bo, 512),
                                start=True, stop=True)
            P.add("tensor", em_den, reads=BS(C, bo, 512) + [C.Rconst],
                  writes=[C.Rpb[db]], tag=f"4dn{tt}_{grp}")

            ob = 4 + grp % 2

            def em_ov(e, grp=grp, qo=qo, ob=ob):
                return e.matmul(C.pb[ob][:, :], A(C, O_BDT + 128 * grp, 128),
                                A(C, qo, 512), start=True, stop=True)
            P.add("tensor", em_ov,
                  reads=BS(C, O_BDT + 128 * grp, 128) + BS(C, qo, 512),
                  writes=[C.Rpb[ob]], tag=f"4ov{tt}_{grp}")

            do = 5120 + 512 * (grp % 2)

            def em_eps(e, db=db, do=do):
                return e.activation(AF_(C, do, 512), C.pb[db][:, :],
                                    AF.Identity, bias=f32(1e-15))
            P.add("scalar", em_eps, reads=[C.Rpb[db]], writes=FS(C, do, 512),
                  tag=f"4ep{tt}_{grp}")

            ro = 6144 + 512 * (grp % 2)

            def em_rec(e, do=do, ro=ro):
                return e.reciprocal(AF_(C, ro, 512), AF_(C, do, 512))
            P.add("vector", em_rec, reads=FS(C, do, 512),
                  writes=FS(C, ro, 512), tag=f"4rc{tt}_{grp}")

            ao = O_ATT2 + 512 * grp

            def em_att(e, ob=ob, ro=ro, ao=ao):
                return e.tensor_tensor(A(C, ao, 512), C.pb[ob][:, :],
                                       AF_(C, ro, 512), OP.mult)
            P.add("vector", em_att, reads=[C.Rpb[ob]] + FS(C, ro, 512),
                  writes=BS(C, ao, 512), tag=f"4at{tt}_{grp}")

        for oc in range(16):
            def em_wp(e, oc=oc):
                return e.dma_start(
                    out=A(C, O_WP2, 2048),
                    in_=_dap(C.wB["wpb"], oc * 262144, [[2048, 128], [1, 2048]]))
            P.add("sync", em_wp, reads=[C.RwB["wpb"]],
                  writes=BS(C, O_WP2, 2048), ndma=1, tag=f"4wp{tt}_{oc}")

            ob = oc % 2

            def em_pmm(e, ob=ob):
                out = []
                for k in range(16):
                    out.append(e.matmul(C.pb[ob][:, :],
                                        A(C, O_WP2 + 128 * k, 128),
                                        A(C, O_ATT2 + 512 * k, 512),
                                        start=(k == 0), stop=(k == 15)))
                return out
            P.add("tensor", em_pmm,
                  reads=BS(C, O_WP2, 2048) + BS(C, O_ATT2, 8192),
                  writes=[C.Rpb[ob]], tag=f"4pm{tt}_{oc}")

            yo = O_YT2 + 512 * (oc % 2)

            def em_yb(e, oc=oc, ob=ob, yo=yo):
                return e.tensor_scalar(A(C, yo, 512), C.pb[ob][:, :],
                                       C.biasb_sb[:, oc:oc + 1], None, OP.add)
            P.add("vector", em_yb, reads=[C.Rpb[ob], C.Rconst],
                  writes=BS(C, yo, 512), tag=f"4yb{tt}_{oc}")

            def em_res(e, oc=oc, yo=yo, tt=tt):
                xs = C.x_res[:, oc, 512 * tt:512 * tt + 512]
                return e.tensor_tensor(xs, A(C, yo, 512), xs, OP.add)
            P.add("vector", em_res, reads=BS(C, yo, 512),
                  writes=[C.Rxres[tt]], tag=f"4rs{tt}_{oc}")



def emit_dbg(C, what):
    P, di = C.P, C.di
    O_LNX, O_TMP = 0, 47104
    emit_ln(C, lambda c: C.x_res[:, c, 0:512], [C.Rxres[0]],
            O_LNX, O_TMP, (6, 7), "dbn")
    routs = [Res(f"dbg{i}") for i in range(16)]
    if what == "lnx":
        for c in range(16):
            def em(e, c=c):
                return e.dma_start(
                    out=_dap(di["y"], c * 128 * 2048, [[2048, 128], [1, 512]]),
                    in_=A(C, O_LNX + 512 * c, 512))
            P.add("gpsimd", em, reads=BS(C, O_LNX + 512 * c, 512),
                  writes=[routs[c]], ndma=1, tag=f"dbgo{c}")
        for r in range(4):
            def em2(e, r=r):
                return e.dma_start(
                    out=_dap(di["y"], r * 2048 + 512, [[2048, 1], [1, 512]]),
                    in_=C.rows[0:1, 512 * r:512 * r + 512])
            P.add("gpsimd", em2, reads=[C.Rrows], writes=[routs[4 + r]],
                  ndma=1, tag=f"dbgr{r}")
    else:
        for s in range(4):
            def em_wg(e, s=s):
                return e.dma_start(
                    out=A(C, 8192, 2048),
                    in_=_dap(C.wB["wg"], s * 262144, [[2048, 128], [1, 2048]]))
            P.add("sync", em_wg, reads=[C.RwB["wg"]], writes=BS(C, 8192, 2048),
                  ndma=1, tag=f"dbwg{s}")
            def em_wu(e, s=s):
                return e.dma_start(
                    out=A(C, 12288, 2048),
                    in_=_dap(C.wB["wu"], s * 262144, [[2048, 128], [1, 2048]]))
            P.add("sync", em_wu, reads=[C.RwB["wu"]], writes=BS(C, 12288, 2048),
                  ndma=1, tag=f"dbwu{s}")
            def em_gmm(e):
                out = []
                for k in range(16):
                    out.append(e.matmul(C.pb[0][:, :], A(C, 8192 + 128 * k, 128),
                                        A(C, O_LNX + 512 * k, 512),
                                        start=(k == 0), stop=(k == 15)))
                return out
            P.add("tensor", em_gmm, reads=BS(C, 8192, 2048) + BS(C, O_LNX, 8192),
                  writes=[C.Rpb[0]], tag=f"dbgm{s}")
            def em_umm(e):
                out = []
                for k in range(16):
                    out.append(e.matmul(C.pb[2][:, :], A(C, 12288 + 128 * k, 128),
                                        A(C, O_LNX + 512 * k, 512),
                                        start=(k == 0), stop=(k == 15)))
                return out
            P.add("tensor", em_umm, reads=BS(C, 12288, 2048) + BS(C, O_LNX, 8192),
                  writes=[C.Rpb[2]], tag=f"dbum{s}")
            def em_sil(e):
                return e.activation(AF_(C, 0, 512), C.pb[0][:, :], AF.Silu)
            P.add("scalar", em_sil, reads=[C.Rpb[0]], writes=FS(C, 0, 512),
                  tag=f"dbsi{s}")
            def em_h(e, s=s):
                return e.tensor_tensor(AF_(C, 1024 + 512 * s, 512), AF_(C, 0, 512),
                                       C.pb[2][:, :], OP.mult)
            P.add("vector", em_h, reads=FS(C, 0, 512) + [C.Rpb[2]],
                  writes=FS(C, 1024 + 512 * s, 512), tag=f"dbh{s}")
        for s in range(4):
            def em(e, s=s):
                return e.dma_start(
                    out=_dap(di["y"], s * 128 * 2048, [[2048, 128], [1, 512]]),
                    in_=AF_(C, 1024 + 512 * s, 512))
            P.add("gpsimd", em, reads=FS(C, 1024 + 512 * s, 512),
                  writes=[routs[s]], ndma=1, tag=f"dbgo{s}")
    P.add("gpsimd", None, reads=routs, tag="dbfin")



def emit_P6(C):
    P, di = C.P, C.di
    routs = [Res(f"out{c}") for c in range(16)]
    for c in range(16):
        def em(e, c=c):
            return e.dma_start(
                out=_dap(di["y"], c * 128 * 2048, [[2048, 128], [1, 2048]]),
                in_=C.x_res[:, c, :])
        P.add("gpsimd", em, reads=list(C.Rxres), writes=[routs[c]],
              ndma=1, tag=f"o{c}")
    P.add("gpsimd", None, reads=routs, tag="fin")



_NC_CACHE = {}


def kernel(**inputs):
    if "nc" not in _NC_CACHE:
        _NC_CACHE["nc"] = build_nc()
    nc = _NC_CACHE["nc"]
    in_maps = [_prep_core(inputs, core) for core in range(8)]
    res = run_bass_kernel_spmd(nc, in_maps, core_ids=list(range(8)))
    out = np.zeros((4, 4096, 2048), dtype=np.float32)
    for core in range(8):
        seq, par = core // 2, core % 2
        own = _own_blocks(par)
        own_tok = np.concatenate([np.arange(128 * g, 128 * g + 128) for g in own])
        y = np.asarray(res.results[core]["y"])
        out[seq][own_tok, :] = y.T
    return out


# version 05
# speedup vs baseline: 1.4137x; 1.1072x over previous
import numpy as np
import ml_dtypes
import concourse.bass as bass
import concourse.mybir as mybir
from concourse.bass_utils import run_bass_kernel_spmd
from concourse.alu_op_type import AluOpType as OP

F32 = mybir.dt.float32
BF16 = mybir.dt.bfloat16
I32 = mybir.dt.int32
AF = mybir.ActivationFunctionType
f32 = np.float32

ABF_N = 50176
AF32_N = 8192
SLOT = 512
MAGIC1 = 0x12AB34CD
MAGIC2 = 0x56EF78A1
MAGIC3 = 0x3C4D5E6F
MAGIC4 = 0x718293A4
INV_SQRT_HD = f32(1.0 / np.sqrt(128.0))
EXP_SHIFT = f32(-20.0)

PHASES = (1, 2, 3, 4, 5)
SPARSIFY = True
P2_SB = (2, 3, 6, 7)
P2_LAG = 2
DBG = None

ENGINES = ("sync", "gpsimd", "scalar", "vector", "tensor")

_seq = [0]


class Ev:
    __slots__ = ("eng", "val", "waited", "seq")

    def __init__(self, eng):
        self.eng = eng
        self.val = None
        self.waited = False
        _seq[0] += 1
        self.seq = _seq[0]


class Res:
    __slots__ = ("name", "w", "rs")

    def __init__(self, name=""):
        self.name = name
        self.w = None
        self.rs = []


class Step:
    __slots__ = ("eng", "emit", "deps", "ev", "ndma", "tag", "signal")

    def __init__(self, eng, emit, deps, ev, ndma, tag):
        self.eng = eng
        self.emit = emit
        self.deps = deps
        self.ev = ev
        self.ndma = ndma
        self.tag = tag
        self.signal = False


class Prog:
    def __init__(self):
        self.steps = {e: [] for e in ENGINES}

    def add(self, eng, emit, reads=(), writes=(), ndma=0, tag=""):
        ev = Ev(eng)
        deps = {}
        for r in reads:
            if r.w is not None:
                deps[r.w.seq] = r.w
        for r in writes:
            if r.w is not None:
                deps[r.w.seq] = r.w
            for e in r.rs:
                deps[e.seq] = e
        deps.pop(ev.seq, None)
        for r in reads:
            r.rs.append(ev)
        for r in writes:
            r.w = ev
            r.rs = []
        final = []
        for d in sorted(deps.values(), key=lambda d: d.seq):
            if d.eng == eng == "tensor":
                continue
            final.append(d)
            d.waited = True
        self.steps[eng].append(Step(eng, emit, final, ev, ndma, tag))
        return ev

    def plan(self):
        for eng in ENGINES:
            cnt = 0
            for st in self.steps[eng]:
                if st.ev.waited or st.ndma:
                    st.signal = True
                    cnt += 16 * st.ndma if st.ndma else 1
                    st.ev.val = cnt

    def emit_engine(self, eng_name, eng_obj, sems):
        water = {}
        for st in self.steps[eng_name]:
            need = {}
            for d in st.deps:
                assert d.val is not None, f"dep of {st.tag} not planned"
                need[d.eng] = max(need.get(d.eng, 0), d.val)
            for deng, val in sorted(need.items()):
                if water.get(deng, 0) >= val:
                    continue
                eng_obj.wait_ge(sems[deng], val)
                water[deng] = val
            out = st.emit(eng_obj) if st.emit is not None else None
            if st.signal:
                assert out is not None, f"step {st.tag} must emit to signal"
                instrs = out if isinstance(out, (list, tuple)) else [out]
                if st.ndma:
                    assert len(instrs) == st.ndma, \
                        f"{st.tag}: ndma={st.ndma} got {len(instrs)}"
                    for i in instrs:
                        i.then_inc(sems[eng_name], 16)
                else:
                    instrs[-1].then_inc(sems[eng_name], 1)



def _own_blocks(par):
    keep = (0, 3) if par == 0 else (1, 2)
    return [g for g in range(32) if g % 4 in keep]


def _prep_core(inputs, core):
    seq, par = core // 2, core % 2
    bf = ml_dtypes.bfloat16
    x = np.asarray(inputs["L0_attn_a_x"][seq])
    xT = np.ascontiguousarray(x.T)
    own = _own_blocks(par)
    own_tok = np.concatenate([np.arange(128 * g, 128 * g + 128) for g in own])
    x_own = np.ascontiguousarray(xT[:, own_tok])

    def lhs_stream(W):
        return np.ascontiguousarray(
            W.reshape(16, 128, 16, 128).transpose(0, 3, 2, 1))

    def mov_stream(W):
        return np.ascontiguousarray(W.T.reshape(16, 128, 2048))

    Wqkv = np.asarray(inputs["L0_attn_a_W_qkv"])
    Wq, Wk, Wv = Wqkv[0:2048], Wqkv[2048:4096], Wqkv[4096:6144]
    Wo = np.asarray(inputs["L0_attn_a_W_out"])
    Wqkvb = np.asarray(inputs["L0_attn_b_w_qkv"])
    Wqb, Wkb, Wvb = Wqkvb[0:2048], Wqkvb[2048:4096], Wqkvb[4096:6144]
    Wpb = np.asarray(inputs["L0_attn_b_w_proj"])
    bpb = np.asarray(inputs["L0_attn_b_b_proj"])
    Wg = np.asarray(inputs["L0_ffn_w_gate"])
    Wu = np.asarray(inputs["L0_ffn_w_up"])
    Wd = np.asarray(inputs["L0_ffn_w_down"])

    def gate_stream(W):
        return np.ascontiguousarray(
            W.reshape(16, 128, 16, 128, 4).transpose(2, 4, 1, 0, 3))

    wd_h = np.ascontiguousarray(
        Wd.reshape(16, 128, 4, 16, 128).transpose(3, 1, 0, 2, 4)
    ).reshape(16, 128, 64, 128)

    masks = np.zeros((4, 8, 128, 512), dtype=np.float32)
    for t in range(4):
        qg = [own[4 * t + i] for i in range(4)]
        qpos = np.concatenate([128 * g + np.arange(128) for g in qg])
        for j in range(8):
            kpos = 128 * (8 * t + j) + np.arange(128)
            masks[t, j] = (kpos[:, None] <= qpos[None, :]).astype(np.float32)

    bd = np.zeros((128, 128), dtype=np.float32)
    for i in range(4):
        bd[32 * i:32 * i + 32, 32 * i:32 * i + 32] = 1.0

    return {
        "x_own": x_own,
        "x_full": xT,
        "wq": lhs_stream(Wq), "wk": lhs_stream(Wk), "wo": lhs_stream(Wo),
        "wv": mov_stream(Wv),
        "wqb": lhs_stream(Wqb), "wpb": lhs_stream(Wpb),
        "wkb": mov_stream(Wkb), "wvb": mov_stream(Wvb),
        "wg": gate_stream(Wg), "wu": gate_stream(Wu), "wd": wd_h,
        "biasb": np.ascontiguousarray(bpb.reshape(16, 128).T),
        "masks": masks.astype(bf),
        "bdones_f": bd,
        "bdones_b": bd.astype(bf),
        "parity": np.array([[par]], dtype=np.int32),
    }



class Ctx:
    pass


def _dap(t, off, dims):
    return bass.AP(t, off, dims)


def build_nc(phases=PHASES):
    nc = bass.Bass("TRN2", target_bir_lowering=False)
    C = Ctx()
    C.nc = nc
    P = Prog()
    C.P = P

    di = {}
    di["x_own"] = nc.dram_tensor("x_own", [2048, 2048], F32, kind="ExternalInput")
    di["x_full"] = nc.dram_tensor("x_full", [2048, 4096], F32, kind="ExternalInput")
    for n in ("wq", "wk", "wo", "wqb", "wpb"):
        di[n] = nc.dram_tensor(n, [16, 128, 16, 128], F32, kind="ExternalInput")
    for n in ("wv", "wkb", "wvb"):
        di[n] = nc.dram_tensor(n, [16, 128, 2048], F32, kind="ExternalInput")
    di["wg"] = nc.dram_tensor("wg", [16, 4, 128, 16, 128], F32, kind="ExternalInput")
    di["wu"] = nc.dram_tensor("wu", [16, 4, 128, 16, 128], F32, kind="ExternalInput")
    di["wd"] = nc.dram_tensor("wd", [16, 128, 64, 128], F32, kind="ExternalInput")
    di["biasb"] = nc.dram_tensor("biasb", [128, 16], F32, kind="ExternalInput")
    di["masks"] = nc.dram_tensor("masks", [4, 8, 128, 512], BF16, kind="ExternalInput")
    di["bdones_f"] = nc.dram_tensor("bdones_f", [128, 128], F32, kind="ExternalInput")
    di["bdones_b"] = nc.dram_tensor("bdones_b", [128, 128], BF16, kind="ExternalInput")
    di["parity"] = nc.dram_tensor("parity", [1, 1], I32, kind="ExternalInput")
    di["y"] = nc.dram_tensor("y", [2048, 2048], F32, kind="ExternalOutput")
    C.di = di

    C.wsize = {
        "wq": 16 * 128 * 16 * 128, "wk": 16 * 128 * 16 * 128,
        "wo": 16 * 128 * 16 * 128, "wqb": 16 * 128 * 16 * 128,
        "wpb": 16 * 128 * 16 * 128,
        "wv": 16 * 128 * 2048, "wkb": 16 * 128 * 2048, "wvb": 16 * 128 * 2048,
        "wg": 16 * 4 * 128 * 16 * 128, "wu": 16 * 4 * 128 * 16 * 128,
        "wd": 16 * 128 * 64 * 128,
    }
    C.wB = {n: nc.dram_tensor(n + "_bf", [C.wsize[n] // 128, 128], BF16)
            for n in C.wsize}
    C.RwB = {n: Res("R_" + n) for n in C.wB}

    C.Kscr = nc.dram_tensor("Kscr", [16, 128, 4096], BF16)
    C.Vscr = nc.dram_tensor("Vscr", [16, 4096, 128], BF16)
    C.RK = Res("RK")
    C.RV = Res("RV")

    C.sh_ex = nc.dram_tensor("sh_ex", [2, 128, 2064], F32, addr_space="Shared")
    C.sh_fl = nc.dram_tensor("sh_fl", [2, 1, 16], I32, addr_space="Shared")

    from contextlib import ExitStack
    with ExitStack() as _st:
        en = _st.enter_context
        x_res = en(nc.sbuf_tensor("x_res", [128, 16, 2048], BF16))
        abf = en(nc.sbuf_tensor("abf", [128, ABF_N], BF16))
        af32 = en(nc.sbuf_tensor("af32", [128, AF32_N], F32))
        rows = en(nc.sbuf_tensor("rows", [1, 2048], F32))
        rows_bf = en(nc.sbuf_tensor("rows_bf", [1, 2048], BF16))
        ones_row = en(nc.sbuf_tensor("ones_row", [1, 128], BF16))
        ones_col = en(nc.sbuf_tensor("ones_col", [128, 1], BF16))
        bdb_sb = en(nc.sbuf_tensor("bdb_sb", [128, 128], BF16))
        bdf_sb = en(nc.sbuf_tensor("bdf_sb", [128, 128], F32))
        biasb_sb = en(nc.sbuf_tensor("biasb_sb", [128, 16], F32))
        par_sb = en(nc.sbuf_tensor("par_sb", [1, 1], I32))
        flag_sb = en(nc.sbuf_tensor("flag_sb", [1, 16], I32))
        pfl_sb = en(nc.sbuf_tensor("pfl_sb", [1, 16], I32))
        pbs = [en(nc.psum_tensor(f"pb{i}", [128, 512], F32)) for i in range(8)]
        s_sync = en(nc.semaphore("s_sync"))
        s_gp = en(nc.semaphore("s_gp"))
        s_sc = en(nc.semaphore("s_sc"))
        s_ve = en(nc.semaphore("s_ve"))
        s_pe = en(nc.semaphore("s_pe"))
        s_poll = en(nc.semaphore("s_poll"))
        s_bar = en(nc.semaphore("s_bar"))
        block = en(nc.Block())

        C.x_res = x_res
        C.abf = abf
        C.af32 = af32
        C.rows = rows
        C.rows_bf = rows_bf
        C.ones_row = ones_row
        C.ones_col = ones_col
        C.bdb_sb = bdb_sb
        C.bdf_sb = bdf_sb
        C.biasb_sb = biasb_sb
        C.par_sb = par_sb
        C.flag_sb = flag_sb
        C.pfl_sb = pfl_sb
        C.pb = pbs
        C.Rpb = [Res(f"pb{i}") for i in range(8)]
        C.Rxres = [Res(f"xres{t}") for t in range(4)]
        C.Rrows = Res("rows")
        C.Rden = Res("rden")
        C.Rconst = Res("const")
        C.s_poll = s_poll
        C.s_bar = s_bar
        C.sems = {"sync": s_sync, "gpsimd": s_gp, "scalar": s_sc,
                  "vector": s_ve, "tensor": s_pe}
        C.bslots = [Res(f"b{i}") for i in range(ABF_N // SLOT)]
        C.fslots = [Res(f"f{i}") for i in range(AF32_N // SLOT)]

        emit_P0(C)
        if DBG is not None:
            emit_dbg(C, DBG)
            emit_P6_skip = True
            phases = ()
        else:
            emit_P6_skip = False
        if 1 in phases:
            emit_P1(C)
        if 2 in phases:
            emit_P2(C)
        feed_casts(C, 10 ** 6)
        if 3 in phases:
            emit_ffn(C, "P3")
        if 4 in phases:
            emit_P4(C)
        if 5 in phases:
            emit_ffn(C, "P5")
        if not emit_P6_skip:
            emit_P6(C)

        P.plan()

        @block.sync
        def _(e):
            P.emit_engine("sync", e, C.sems)

        @block.gpsimd
        def _(e):
            P.emit_engine("gpsimd", e, C.sems)

        @block.scalar
        def _(e):
            P.emit_engine("scalar", e, C.sems)

        @block.vector
        def _(e):
            P.emit_engine("vector", e, C.sems)

        @block.tensor
        def _(e):
            P.emit_engine("tensor", e, C.sems)

    return nc



def A(C, off, n):
    assert off + n <= ABF_N, (off, n)
    return C.abf[:, off:off + n]


def AF_(C, off, n):
    assert off + n <= AF32_N, (off, n)
    return C.af32[:, off:off + n]


def BS(C, off, n):
    return C.bslots[off // SLOT:(off + n + SLOT - 1) // SLOT]


def FS(C, off, n):
    return C.fslots[off // SLOT:(off + n + SLOT - 1) // SLOT]



def emit_P0(C):
    P, di = C.P, C.di

    P.add("vector", lambda e: e.memset(C.ones_row[:, :], 1.0),
          writes=[C.Rconst], tag="c_or")
    P.add("vector", lambda e: e.memset(C.ones_col[:, :], 1.0),
          writes=[C.Rconst], tag="c_oc")
    P.add("vector", lambda e: e.memset(C.flag_sb[:, 0:1], MAGIC1),
          writes=[C.Rconst], tag="c_f1")
    P.add("vector", lambda e: e.memset(C.flag_sb[:, 1:2], MAGIC2),
          writes=[C.Rconst], tag="c_f2")
    P.add("vector", lambda e: e.memset(C.flag_sb[:, 2:3], MAGIC3),
          writes=[C.Rconst], tag="c_f3")
    P.add("vector", lambda e: e.memset(C.flag_sb[:, 3:4], MAGIC4),
          writes=[C.Rconst], tag="c_f4")
    P.add("sync", lambda e: [
        e.dma_start(out=C.bdb_sb[:, :], in_=_dap(di["bdones_b"], 0, [[128, 128], [1, 128]])),
        e.dma_start(out=C.bdf_sb[:, :], in_=_dap(di["bdones_f"], 0, [[128, 128], [1, 128]])),
        e.dma_start(out=C.biasb_sb[:, :], in_=_dap(di["biasb"], 0, [[16, 128], [1, 16]])),
        e.dma_start(out=C.par_sb[:, :], in_=_dap(di["parity"], 0, [[1, 1], [1, 1]])),
    ], writes=[C.Rconst], ndma=4, tag="c_ld")

    C.Rbar = Res("bar")

    def em_bar(g):
        with (
            g.register("bp") as bp,
            g.register("bwoff") as bwoff,
            g.register("broff") as broff,
            g.register("bcnt") as bcnt,
            g.register("bcond") as bcond,
            g.register("bfl") as bfl,
        ):
            g.reg_load(bp, C.par_sb[:1, :1])
            g.reg_mov(bwoff, 0)
            g.reg_add(bwoff, bwoff, bp)
            g.reg_mul(bwoff, bwoff, 16)
            g.reg_add(bwoff, bwoff, 2)
            g.reg_mov(broff, 1)
            g.reg_sub(broff, broff, bp)
            g.reg_mul(broff, broff, 16)
            g.reg_add(broff, broff, 2)
            g.reg_mov(bcnt, 0)
            g.reg_mov(bcond, 1)
            with g.While(bcond):
                g.reg_add(bcnt, bcnt, 32)
                g.dma_start(bass.AP(C.sh_fl, bwoff, [[16, 1], [1, 2]]),
                            C.flag_sb[:, 2:4]).then_inc(C.s_bar, 16)
                g.dma_start(C.pfl_sb[:, 0:2],
                            bass.AP(C.sh_fl, broff, [[16, 1], [1, 2]])
                            ).then_inc(C.s_bar, 16)
                g.wait_ge(C.s_bar, bcnt)
                g.reg_load(bfl, C.pfl_sb[:1, 0:1])
                g.reg_mov(bcond, bfl)
                g.reg_sub(bcond, bcond, MAGIC3)
            g.reg_mov(bcond, 1)
            with g.While(bcond):
                g.reg_add(bcnt, bcnt, 32)
                g.dma_start(bass.AP(C.sh_fl, bwoff, [[16, 1], [1, 2]]),
                            C.flag_sb[:, 2:4]).then_inc(C.s_bar, 16)
                g.dma_start(C.pfl_sb[:, 0:2],
                            bass.AP(C.sh_fl, broff, [[16, 1], [1, 2]])
                            ).then_inc(C.s_bar, 16)
                g.wait_ge(C.s_bar, bcnt)
                g.reg_load(bfl, C.pfl_sb[:1, 1:2])
                g.reg_mov(bcond, bfl)
                g.reg_sub(bcond, bcond, MAGIC4)
            g.reg_add(bcnt, bcnt, 16)
            g.dma_start(bass.AP(C.sh_fl, bwoff, [[16, 1], [1, 2]]),
                        C.flag_sb[:, 2:4]).then_inc(C.s_bar, 16)
            g.wait_ge(C.s_bar, bcnt)
            return g.dma_start(C.pfl_sb[:, 0:2],
                               bass.AP(C.sh_fl, broff, [[16, 1], [1, 2]]))
    P.add("gpsimd", em_bar, reads=[C.Rconst], writes=[C.Rbar], ndma=1,
          tag="bar0")

    for c in range(16):
        def em(e, c=c):
            return e.dma_start(
                out=C.x_res[:, c, :],
                in_=_dap(di["x_own"], c * 128 * 2048, [[2048, 128], [1, 2048]]))
        P.add("gpsimd", em, reads=[C.Rbar], writes=list(C.Rxres), ndma=1,
              tag=f"xi{c}")

    C._bounce_i = 0

    def cast_jobs(name, nchunks=None):
        cols = C.wsize[name] // 128
        ch = 2048
        return [(name, i, min(ch, cols - i), cols) for i in range(0, cols, ch)]

    def emit_cast(job):
        name, i, w, cols = job
        slot = C._bounce_i % 2
        C._bounce_i += 1
        boff = slot * 2048

        def eml(e, name=name, i=i, w=w, cols=cols, boff=boff):
            return e.dma_start(out=AF_(C, boff, w),
                               in_=_dap(di[name], i, [[cols, 128], [1, w]]))

        def ems(e, name=name, i=i, w=w, cols=cols, boff=boff):
            return e.dma_start(out=_dap(C.wB[name], i, [[cols, 128], [1, w]]),
                               in_=AF_(C, boff, w))
        P.add("sync", eml, reads=[C.Rbar], writes=FS(C, boff, w), ndma=1,
              tag=f"pl{name}{i}")
        P.add("gpsimd", ems, reads=FS(C, boff, w), writes=[C.RwB[name]],
              ndma=1, tag=f"ps{name}{i}")

    C.emit_cast = emit_cast
    for job in cast_jobs("wk") + cast_jobs("wv"):
        emit_cast(job)
    C.cast_queue = []
    for n2 in ["wq", "wo", "wg", "wu", "wd", "wqb", "wkb", "wvb", "wpb"]:
        C.cast_queue.extend(cast_jobs(n2))


def feed_casts(C, n):
    for _ in range(n):
        if not C.cast_queue:
            return
        C.emit_cast(C.cast_queue.pop(0))



def emit_ln(C, get_x, reads_x, lnx_off, tmp_off, banks, tag):
    P = C.P
    W = 512
    b0, b1 = banks
    x2o = [tmp_off, tmp_off + W]
    rbco, mbco = tmp_off + 2 * W, tmp_off + 3 * W

    def em_sum(e):
        out = []
        for c in range(16):
            out.append(e.matmul(C.pb[b0][0:1, 0:W], C.ones_col[:, :], get_x(c),
                                start=(c == 0), stop=(c == 15)))
        return out
    P.add("tensor", em_sum, reads=list(reads_x) + [C.Rconst],
          writes=[C.Rpb[b0]], tag=tag + "sum")

    for c in range(16):
        o = x2o[c % 2]

        def em_sq(e, c=c, o=o):
            return e.activation(A(C, o, W), get_x(c), AF.Square)
        P.add("scalar", em_sq, reads=list(reads_x), writes=BS(C, o, W),
              tag=f"{tag}sq{c}")

        def em_mm(e, c=c, o=o):
            return e.matmul(C.pb[b1][0:1, 0:W], C.ones_col[:, :], A(C, o, W),
                            start=(c == 0), stop=(c == 15))
        P.add("tensor", em_mm, reads=BS(C, o, W) + [C.Rconst],
              writes=[C.Rpb[b1]], tag=f"{tag}sm{c}")

    mean = C.rows[0:1, 0:W]
    m2 = C.rows[0:1, 512:512 + W]
    var = C.rows[0:1, 1024:1024 + W]
    rstd = C.rows[0:1, 1536:1536 + W]
    rstd_bf = C.rows_bf[0:1, 0:W]
    nmr_bf = C.rows_bf[0:1, 512:512 + W]

    P.add("vector", lambda e: e.tensor_scalar(mean, C.pb[b0][0:1, 0:W],
                                              float(1.0 / 2048), None, OP.mult),
          reads=[C.Rpb[b0]], writes=[C.Rrows], tag=tag + "mn")
    P.add("vector", lambda e: e.tensor_tensor(m2, mean, mean, OP.mult),
          reads=[C.Rrows], writes=[C.Rrows], tag=tag + "m2")
    P.add("vector", lambda e: e.scalar_tensor_tensor(var, C.pb[b1][0:1, 0:W],
                                                     float(1.0 / 2048), m2,
                                                     OP.mult, OP.subtract),
          reads=[C.Rpb[b1], C.Rrows], writes=[C.Rrows], tag=tag + "vr")
    P.add("scalar", lambda e: e.activation(m2, var, AF.Ln, bias=f32(1e-5)),
          reads=[C.Rrows], writes=[C.Rrows], tag=tag + "ln")
    P.add("scalar", lambda e: e.activation(rstd, m2, AF.Exp, scale=f32(-0.5)),
          reads=[C.Rrows], writes=[C.Rrows], tag=tag + "rs")
    P.add("vector", lambda e: e.scalar_tensor_tensor(var, mean, float(-1.0),
                                                     rstd, OP.mult, OP.mult),
          reads=[C.Rrows], writes=[C.Rrows], tag=tag + "nm")
    P.add("vector", lambda e: e.tensor_copy(rstd_bf, rstd),
          reads=[C.Rrows], writes=[C.Rrows], tag=tag + "cr")
    P.add("vector", lambda e: e.tensor_copy(nmr_bf, var),
          reads=[C.Rrows], writes=[C.Rrows], tag=tag + "cm")

    P.add("tensor", lambda e: e.matmul(C.pb[b0][:, 0:W], C.ones_row[:, :],
                                       rstd_bf, start=True, stop=True),
          reads=[C.Rrows, C.Rconst], writes=[C.Rpb[b0]], tag=tag + "br")
    P.add("tensor", lambda e: e.matmul(C.pb[b1][:, 0:W], C.ones_row[:, :],
                                       nmr_bf, start=True, stop=True),
          reads=[C.Rrows, C.Rconst], writes=[C.Rpb[b1]], tag=tag + "bm")
    P.add("scalar", lambda e: e.copy(A(C, rbco, W), C.pb[b0][:, 0:W]),
          reads=[C.Rpb[b0]], writes=BS(C, rbco, W), tag=tag + "er")
    P.add("scalar", lambda e: e.copy(A(C, mbco, W), C.pb[b1][:, 0:W]),
          reads=[C.Rpb[b1]], writes=BS(C, mbco, W), tag=tag + "em")

    for c in range(16):
        o = x2o[c % 2]

        def em_t1(e, c=c, o=o):
            return e.tensor_tensor(A(C, o, W), get_x(c), A(C, rbco, W), OP.mult)
        P.add("vector", em_t1, reads=list(reads_x) + BS(C, rbco, W),
              writes=BS(C, o, W), tag=f"{tag}t1{c}")

        def em_t2(e, c=c, o=o):
            return e.tensor_tensor(A(C, lnx_off + 512 * c, W), A(C, o, W),
                                   A(C, mbco, W), OP.add)
        P.add("vector", em_t2, reads=BS(C, o, W) + BS(C, mbco, W),
              writes=BS(C, lnx_off + 512 * c, W), tag=f"{tag}t2{c}")



def emit_P1(C):
    P, di = C.P, C.di
    O_XBF, O_LNX, O_TMP, O_WK, O_KEV, O_WV, O_VEV = \
        0, 16384, 32768, 34816, 38912, 39936, 48128

    for sl in range(8):
        lnoff = O_LNX + (sl % 2) * 8192
        xoff = O_XBF + (sl % 2) * 8192

        def em_lx(e, sl=sl, xoff=xoff):
            out = []
            for c in range(16):
                out.append(e.dma_start(
                    out=A(C, xoff + 512 * c, 512),
                    in_=_dap(di["x_full"], c * 128 * 4096 + 512 * sl,
                             [[4096, 128], [1, 512]])))
            return out
        P.add("gpsimd", em_lx, writes=BS(C, xoff, 8192), ndma=16,
              tag=f"1lx{sl}")
        feed_casts(C, 8)

        emit_ln(C, lambda c, xoff=xoff: A(C, xoff + 512 * c, 512),
                BS(C, xoff, 8192), lnoff, O_TMP, (6, 7), f"1n{sl}")

        for oc in range(16):
            wo = O_WK + 2048 * (oc % 2)

            def em_wk(e, oc=oc, wo=wo):
                return e.dma_start(
                    out=A(C, wo, 2048),
                    in_=_dap(C.wB["wk"], oc * 262144, [[2048, 128], [1, 2048]]))
            P.add("sync", em_wk, reads=[C.RwB["wk"]], writes=BS(C, wo, 2048),
                  ndma=1, tag=f"1wk{sl}_{oc}")

            bank = oc % 4

            def em_mm(e, wo=wo, bank=bank, lnoff=lnoff):
                out = []
                for k in range(16):
                    out.append(e.matmul(C.pb[bank][:, :],
                                        A(C, wo + 128 * k, 128),
                                        A(C, lnoff + 512 * k, 512),
                                        start=(k == 0), stop=(k == 15)))
                return out
            P.add("tensor", em_mm,
                  reads=BS(C, wo, 2048) + BS(C, lnoff, 8192),
                  writes=[C.Rpb[bank]], tag=f"1km{sl}_{oc}")

            eo = O_KEV + 512 * (oc % 2)

            def em_ev(e, bank=bank, eo=eo):
                return e.tensor_copy(A(C, eo, 512), C.pb[bank][:, :])
            P.add("vector", em_ev, reads=[C.Rpb[bank]], writes=BS(C, eo, 512),
                  tag=f"1ke{sl}_{oc}")

            def em_st(e, oc=oc, eo=eo, sl=sl):
                return e.dma_start(
                    out=_dap(C.Kscr, oc * 128 * 4096 + 512 * sl,
                             [[4096, 128], [1, 512]]),
                    in_=A(C, eo, 512))
            P.add("gpsimd", em_st, reads=BS(C, eo, 512), writes=[C.RK],
                  ndma=1, tag=f"1ks{sl}_{oc}")

        for nt in range(4):
            for k in range(16):
                def em_wv(e, nt=nt, k=k):
                    return e.dma_start(
                        out=A(C, O_WV + 512 * k, 512),
                        in_=_dap(C.wB["wv"], k * 262144 + 512 * nt,
                                 [[2048, 128], [1, 512]]))
                P.add("sync", em_wv, reads=[C.RwB["wv"]],
                      writes=BS(C, O_WV + 512 * k, 512), ndma=1,
                      tag=f"1wv{sl}_{nt}_{k}")
            for blk in range(4):
                bank = 4 + blk % 2

                def em_vmm(e, blk=blk, bank=bank, lnoff=lnoff):
                    out = []
                    for k in range(16):
                        out.append(e.matmul(C.pb[bank][:, :],
                                            A(C, lnoff + 512 * k + 128 * blk, 128),
                                            A(C, O_WV + 512 * k, 512),
                                            start=(k == 0), stop=(k == 15)))
                    return out
                P.add("tensor", em_vmm,
                      reads=BS(C, O_WV, 8192) + BS(C, lnoff, 8192),
                      writes=[C.Rpb[bank]], tag=f"1vm{sl}_{nt}_{blk}")

                eo = O_VEV + 512 * (blk % 2)

                def em_vev(e, bank=bank, eo=eo):
                    return e.tensor_copy(A(C, eo, 512), C.pb[bank][:, :])
                P.add("vector", em_vev, reads=[C.Rpb[bank]],
                      writes=BS(C, eo, 512), tag=f"1ve{sl}_{nt}_{blk}")

                def em_vst(e, nt=nt, blk=blk, eo=eo, sl=sl):
                    off = (4 * nt) * 4096 * 128 + (512 * sl + 128 * blk) * 128
                    return e.dma_start(
                        out=_dap(C.Vscr, off,
                                 [[128, 128], [4096 * 128, 4], [1, 128]]),
                        in_=A(C, eo, 512))
                P.add("gpsimd", em_vst, reads=BS(C, eo, 512), writes=[C.RV],
                      ndma=1, tag=f"1vs{sl}_{nt}_{blk}")



def emit_P2(C):
    P, di = C.P, C.di
    O_MASK, O_LNX, O_QH, O_KB, O_VB, O_EXP, O_ATT, O_WQ, O_WO, O_TMP, \
        O_OUN, O_RBC, O_YT = (0, 4096, 12288, 13312, 21504, 29696, 31744,
                              39936, 41984, 44032, 46080, 47104, 48128)
    rden = C.af32[0:1, 4096:4608]
    rden_bf = C.rows_bf[0:1, 1024:1536]
    SB = P2_SB
    LAG = P2_LAG

    for t in range(4):
        PT = 8 * (t + 1)

        def em_mask(e, t=t):
            return e.dma_start(
                out=A(C, O_MASK, 4096),
                in_=_dap(di["masks"], t * 8 * 128 * 512,
                         [[512, 128], [65536, 8], [1, 512]]))
        P.add("sync", em_mask, writes=BS(C, O_MASK, 4096), ndma=1,
              tag=f"2mk{t}")

        emit_ln(C, lambda c, t=t: C.x_res[:, c, 512 * t:512 * t + 512],
                [C.Rxres[t]], O_LNX, O_TMP, (6, 7), f"2n{t}")

        for h in range(16):
            wqo = O_WQ + 2048 * (h % 2)

            def em_wq(e, h=h, wqo=wqo):
                return e.dma_start(
                    out=A(C, wqo, 2048),
                    in_=_dap(C.wB["wq"], h * 262144, [[2048, 128], [1, 2048]]))
            P.add("sync", em_wq, reads=[C.RwB["wq"]],
                  writes=BS(C, wqo, 2048), ndma=1, tag=f"2wq{t}_{h}")

            qb = h % 2

            def em_qmm(e, qb=qb, wqo=wqo):
                out = []
                for k in range(16):
                    out.append(e.matmul(C.pb[qb][:, :],
                                        A(C, wqo + 128 * k, 128),
                                        A(C, O_LNX + 512 * k, 512),
                                        start=(k == 0), stop=(k == 15)))
                return out
            P.add("tensor", em_qmm,
                  reads=BS(C, wqo, 2048) + BS(C, O_LNX, 8192),
                  writes=[C.Rpb[qb]], tag=f"2qm{t}_{h}")

            qo = O_QH + 512 * (h % 2)

            def em_qev(e, qb=qb, qo=qo):
                return e.tensor_copy(A(C, qo, 512), C.pb[qb][:, :])
            P.add("vector", em_qev, reads=[C.Rpb[qb]], writes=BS(C, qo, 512),
                  tag=f"2qe{t}_{h}")

            feed_casts(C, 3)
            ko = O_KB + 4096 * (h % 2)
            vo = O_VB + 4096 * (h % 2)

            def em_k(e, h=h, ko=ko, PT=PT):
                return e.dma_start(
                    out=A(C, ko, 128 * PT),
                    in_=_dap(C.Kscr, h * 128 * 4096, [[4096, 128], [1, 128 * PT]]))
            P.add("sync", em_k, reads=[C.RK], writes=BS(C, ko, 4096), ndma=1,
                  tag=f"2kl{t}_{h}")

            def em_v(e, h=h, vo=vo, PT=PT):
                return e.dma_start(
                    out=A(C, vo, 128 * PT),
                    in_=_dap(C.Vscr, h * 4096 * 128,
                             [[128, 128], [128 * 128, PT], [1, 128]]))
            P.add("sync", em_v, reads=[C.RV], writes=BS(C, vo, 4096), ndma=1,
                  tag=f"2vl{t}_{h}")

            tail = []

            def emit_dp(kc):
                eo, erd = tail[kc]

                def em_den(e, kc=kc, eo=eo, PT=PT):
                    return e.matmul(C.pb[4][0:1, :], C.ones_col[:, :],
                                    A(C, eo, 512),
                                    start=(kc == 0), stop=(kc == PT - 1))
                P.add("tensor", em_den, reads=erd + [C.Rconst],
                      writes=[C.Rpb[4]], tag=f"2d{t}_{h}_{kc}")

                def em_pv(e, kc=kc, eo=eo, vo=vo, PT=PT):
                    return e.matmul(C.pb[5][:, :], A(C, vo + 128 * kc, 128),
                                    A(C, eo, 512),
                                    start=(kc == 0), stop=(kc == PT - 1))
                P.add("tensor", em_pv, reads=erd + BS(C, vo + 128 * kc, 128),
                      writes=[C.Rpb[5]], tag=f"2p{t}_{h}_{kc}")

            for kc in range(PT):
                sb = SB[kc % len(SB)]

                def em_s(e, kc=kc, sb=sb, ko=ko, qo=qo):
                    return e.matmul(C.pb[sb][:, :], A(C, ko + 128 * kc, 128),
                                    A(C, qo, 512), start=True, stop=True)
                P.add("tensor", em_s,
                      reads=BS(C, ko + 128 * kc, 128) + BS(C, qo, 512),
                      writes=[C.Rpb[sb]], tag=f"2s{t}_{h}_{kc}")

                eo = O_EXP + 512 * (kc % len(SB))

                def em_exp(e, sb=sb, eo=eo):
                    return e.activation(A(C, eo, 512), C.pb[sb][:, :], AF.Exp,
                                        bias=EXP_SHIFT, scale=INV_SQRT_HD)
                P.add("scalar", em_exp, reads=[C.Rpb[sb]],
                      writes=BS(C, eo, 512), tag=f"2e{t}_{h}_{kc}")

                if kc >= 8 * t:
                    j = kc - 8 * t
                    mo = O_TMP + 512 * (kc % len(SB))

                    def em_msk(e, j=j, eo=eo, mo=mo):
                        return e.tensor_tensor(
                            A(C, mo, 512), A(C, eo, 512),
                            C.abf[:, O_MASK + 512 * j:O_MASK + 512 * j + 512],
                            OP.mult)
                    P.add("vector", em_msk,
                          reads=BS(C, eo, 512) + BS(C, O_MASK, 4096),
                          writes=BS(C, mo, 512), tag=f"2x{t}_{h}_{kc}")
                    tail.append((mo, BS(C, mo, 512)))
                else:
                    tail.append((eo, BS(C, eo, 512)))

                if kc >= LAG:
                    emit_dp(kc - LAG)
            for kc in range(max(0, PT - LAG), PT):
                emit_dp(kc)

            P.add("vector", lambda e: e.reciprocal(rden, C.pb[4][0:1, :]),
                  reads=[C.Rpb[4]], writes=[C.Rden] + FS(C, 4096, 512),
                  tag=f"2r{t}_{h}")
            P.add("vector", lambda e: e.tensor_copy(rden_bf, rden),
                  reads=[C.Rden], writes=[C.Rden], tag=f"2rc{t}_{h}")
            P.add("tensor", lambda e, qb=qb: e.matmul(C.pb[qb][:, :],
                                                      C.ones_row[:, :],
                                                      rden_bf, start=True,
                                                      stop=True),
                  reads=[C.Rden, C.Rconst], writes=[C.Rpb[qb]], tag=f"2b{t}_{h}")

            ro = O_RBC + 512 * (h % 2)
            oo = O_OUN + 512 * (h % 2)
            ao = O_ATT + 512 * h

            def em_rbc(e, ro=ro, qb=qb):
                return e.tensor_copy(A(C, ro, 512), C.pb[qb][:, :])
            P.add("vector", em_rbc, reads=[C.Rpb[qb]], writes=BS(C, ro, 512),
                  tag=f"2rb{t}_{h}")

            def em_oun(e, oo=oo):
                return e.tensor_copy(A(C, oo, 512), C.pb[5][:, :])
            P.add("vector", em_oun, reads=[C.Rpb[5]], writes=BS(C, oo, 512),
                  tag=f"2ou{t}_{h}")

            def em_att(e, ao=ao, oo=oo, ro=ro):
                return e.tensor_tensor(A(C, ao, 512), A(C, oo, 512),
                                       A(C, ro, 512), OP.mult)
            P.add("vector", em_att, reads=BS(C, oo, 512) + BS(C, ro, 512),
                  writes=BS(C, ao, 512), tag=f"2a{t}_{h}")

        for oc in range(16):
            def em_wo(e, oc=oc):
                return e.dma_start(
                    out=A(C, O_WO, 2048),
                    in_=_dap(C.wB["wo"], oc * 262144, [[2048, 128], [1, 2048]]))
            P.add("sync", em_wo, reads=[C.RwB["wo"]],
                  writes=BS(C, O_WO, 2048), ndma=1, tag=f"2wo{t}_{oc}")

            ob = oc % 2

            def em_omm(e, ob=ob):
                out = []
                for k in range(16):
                    out.append(e.matmul(C.pb[ob][:, :],
                                        A(C, O_WO + 128 * k, 128),
                                        A(C, O_ATT + 512 * k, 512),
                                        start=(k == 0), stop=(k == 15)))
                return out
            P.add("tensor", em_omm,
                  reads=BS(C, O_WO, 2048) + BS(C, O_ATT, 8192),
                  writes=[C.Rpb[ob]], tag=f"2om{t}_{oc}")

            yo = O_YT + 512 * (oc % 2)

            def em_yev(e, ob=ob, yo=yo):
                return e.tensor_copy(A(C, yo, 512), C.pb[ob][:, :])
            P.add("vector", em_yev, reads=[C.Rpb[ob]], writes=BS(C, yo, 512),
                  tag=f"2ye{t}_{oc}")

            def em_res(e, oc=oc, yo=yo, t=t):
                xs = C.x_res[:, oc, 512 * t:512 * t + 512]
                return e.tensor_tensor(xs, A(C, yo, 512), xs, OP.add)
            P.add("vector", em_res, reads=BS(C, yo, 512),
                  writes=[C.Rxres[t]], tag=f"2rs{t}_{oc}")



def emit_ffn(C, ph):
    P = C.P
    O_LNX, O_WG, O_WU, O_HSP, O_TMP, O_YT = 0, 8192, 12288, 14336, 47104, 49152
    O_WD = 0

    for tt in range(4):
        emit_ln(C, lambda c, tt=tt: C.x_res[:, c, 512 * tt:512 * tt + 512],
                [C.Rxres[tt]], O_LNX, O_TMP, (6, 7), f"{ph}n{tt}")

        for gc in range(16):
            for s in range(4):
                wgo = O_WG + 2048 * (s % 2)

                def em_wg(e, gc=gc, s=s, wgo=wgo):
                    return e.dma_start(
                        out=A(C, wgo, 2048),
                        in_=_dap(C.wB["wg"], (gc * 4 + s) * 262144,
                                 [[2048, 128], [1, 2048]]))
                P.add("sync", em_wg, reads=[C.RwB["wg"]],
                      writes=BS(C, wgo, 2048), ndma=1, tag=f"{ph}wg{tt}_{gc}_{s}")

                def em_wu(e, gc=gc, s=s):
                    return e.dma_start(
                        out=A(C, O_WU, 2048),
                        in_=_dap(C.wB["wu"], (gc * 4 + s) * 262144,
                                 [[2048, 128], [1, 2048]]))
                P.add("sync", em_wu, reads=[C.RwB["wu"]],
                      writes=BS(C, O_WU, 2048), ndma=1, tag=f"{ph}wu{tt}_{gc}_{s}")

                gb = s % 2
                ub = 2 + s % 2

                def em_gmm(e, wgo=wgo, gb=gb):
                    out = []
                    for k in range(16):
                        out.append(e.matmul(C.pb[gb][:, :],
                                            A(C, wgo + 128 * k, 128),
                                            A(C, O_LNX + 512 * k, 512),
                                            start=(k == 0), stop=(k == 15)))
                    return out
                P.add("tensor", em_gmm,
                      reads=BS(C, wgo, 2048) + BS(C, O_LNX, 8192),
                      writes=[C.Rpb[gb]], tag=f"{ph}gm{tt}_{gc}_{s}")

                def em_umm(e, ub=ub):
                    out = []
                    for k in range(16):
                        out.append(e.matmul(C.pb[ub][:, :],
                                            A(C, O_WU + 128 * k, 128),
                                            A(C, O_LNX + 512 * k, 512),
                                            start=(k == 0), stop=(k == 15)))
                    return out
                P.add("tensor", em_umm,
                      reads=BS(C, O_WU, 2048) + BS(C, O_LNX, 8192),
                      writes=[C.Rpb[ub]], tag=f"{ph}um{tt}_{gc}_{s}")

                so = 512 * (s % 2)

                def em_sil(e, gb=gb, so=so):
                    return e.activation(AF_(C, so, 512), C.pb[gb][:, :], AF.Silu)
                P.add("scalar", em_sil, reads=[C.Rpb[gb]],
                      writes=FS(C, so, 512), tag=f"{ph}si{tt}_{gc}_{s}")

                ho = 1024 + 512 * s

                def em_h(e, ub=ub, so=so, ho=ho):
                    return e.tensor_tensor(AF_(C, ho, 512), AF_(C, so, 512),
                                           C.pb[ub][:, :], OP.mult)
                P.add("vector", em_h, reads=FS(C, so, 512) + [C.Rpb[ub]],
                      writes=FS(C, ho, 512), tag=f"{ph}h{tt}_{gc}_{s}")

            for s in range(4):
                def em_abs(e, s=s):
                    return e.tensor_tensor(AF_(C, 3072 + 512 * s, 512),
                                           AF_(C, 1024 + 512 * s, 512),
                                           AF_(C, 1024 + 512 * s, 512),
                                           OP.mult)
                P.add("vector", em_abs, reads=FS(C, 1024 + 512 * s, 512),
                      writes=FS(C, 3072 + 512 * s, 512), tag=f"{ph}ab{tt}_{gc}_{s}")
            QO = 5120
            spec = [(QO, 3072, 3584, OP.max), (QO + 512, 3072, 3584, OP.min),
                    (QO + 1024, 4096, 4608, OP.max),
                    (QO + 1536, 4096, 4608, OP.min)]
            for do, o1, o2, op in spec:
                def em_p(e, do=do, o1=o1, o2=o2, op=op):
                    return e.tensor_tensor(AF_(C, do, 512), AF_(C, o1, 512),
                                           AF_(C, o2, 512), op)
                P.add("vector", em_p,
                      reads=FS(C, o1, 512) + FS(C, o2, 512),
                      writes=FS(C, do, 512), tag=f"{ph}mx{tt}_{gc}_{do}")
            P.add("vector", lambda e: e.tensor_tensor(
                AF_(C, QO + 2048, 512), AF_(C, QO, 512), AF_(C, QO + 1024, 512),
                OP.min),
                reads=FS(C, QO, 512) + FS(C, QO + 1024, 512),
                writes=FS(C, QO + 2048, 512), tag=f"{ph}t1{tt}_{gc}")
            P.add("vector", lambda e: e.tensor_tensor(
                AF_(C, QO, 512), AF_(C, QO + 512, 512), AF_(C, QO + 1536, 512),
                OP.max),
                reads=FS(C, QO + 512, 512) + FS(C, QO + 1536, 512),
                writes=FS(C, QO, 512), tag=f"{ph}t2{tt}_{gc}")
            P.add("vector", lambda e: e.tensor_tensor(
                AF_(C, QO + 2560, 512), AF_(C, QO + 2048, 512), AF_(C, QO, 512),
                OP.max),
                reads=FS(C, QO + 2048, 512) + FS(C, QO, 512),
                writes=FS(C, QO + 2560, 512), tag=f"{ph}m2{tt}_{gc}")
            for s in range(4):
                def em_ge(e, s=s):
                    return e.tensor_tensor(AF_(C, QO + 512, 512),
                                           AF_(C, 3072 + 512 * s, 512),
                                           AF_(C, QO + 2560, 512), OP.is_ge)
                P.add("vector", em_ge,
                      reads=FS(C, 3072 + 512 * s, 512) + FS(C, QO + 2560, 512),
                      writes=FS(C, QO + 512, 512), tag=f"{ph}ge{tt}_{gc}_{s}")

                hso = O_HSP + 512 * (4 * gc + s)

                if SPARSIFY:
                    def em_ap(e, s=s, hso=hso):
                        return e.tensor_tensor(A(C, hso, 512),
                                               AF_(C, 1024 + 512 * s, 512),
                                               AF_(C, QO + 512, 512), OP.mult)
                    P.add("vector", em_ap,
                          reads=FS(C, 1024 + 512 * s, 512) + FS(C, QO + 512, 512),
                          writes=BS(C, hso, 512), tag=f"{ph}sp{tt}_{gc}_{s}")
                else:
                    def em_ap(e, s=s, hso=hso):
                        return e.tensor_copy(A(C, hso, 512),
                                             AF_(C, 1024 + 512 * s, 512))
                    P.add("vector", em_ap,
                          reads=FS(C, 1024 + 512 * s, 512),
                          writes=BS(C, hso, 512), tag=f"{ph}sp{tt}_{gc}_{s}")

        for m in range(16):
            db = 4 + m % 2
            for half in range(2):
                wdo = O_WD + 4096 * ((2 * m + half) % 3)

                def em_wd(e, m=m, half=half, wdo=wdo):
                    return e.dma_start(
                        out=A(C, wdo, 4096),
                        in_=_dap(C.wB["wd"], m * 1048576 + 4096 * half,
                                 [[8192, 128], [1, 4096]]))
                P.add("sync", em_wd, reads=[C.RwB["wd"]],
                      writes=BS(C, wdo, 4096), ndma=1,
                      tag=f"{ph}wd{tt}_{m}_{half}")

                def em_dmm(e, wdo=wdo, db=db, half=half):
                    out = []
                    for i in range(32):
                        out.append(e.matmul(
                            C.pb[db][:, :],
                            A(C, wdo + 128 * i, 128),
                            A(C, O_HSP + 512 * (32 * half + i), 512),
                            start=(half == 0 and i == 0),
                            stop=(half == 1 and i == 31)))
                    return out
                P.add("tensor", em_dmm,
                      reads=BS(C, wdo, 4096) + BS(C, O_HSP + 16384 * half, 16384),
                      writes=[C.Rpb[db]], tag=f"{ph}dm{tt}_{m}_{half}")

            yo = O_YT + 512 * (m % 2)

            def em_ye(e, db=db, yo=yo):
                return e.tensor_copy(A(C, yo, 512), C.pb[db][:, :])
            P.add("vector", em_ye, reads=[C.Rpb[db]], writes=BS(C, yo, 512),
                  tag=f"{ph}ye{tt}_{m}")

            def em_res(e, m=m, yo=yo, tt=tt):
                xs = C.x_res[:, m, 512 * tt:512 * tt + 512]
                return e.tensor_tensor(xs, A(C, yo, 512), xs, OP.add)
            P.add("vector", em_res, reads=BS(C, yo, 512),
                  writes=[C.Rxres[tt]], tag=f"{ph}rs{tt}_{m}")



def emit_P4(C):
    P, di = C.P, C.di
    O_LNX, O_KT, O_VT, O_WM, O_TMP = 0, 8192, 16384, 24576, 26624

    P.add("vector", lambda e: e.memset(AF_(C, 0, 2048), 0.0),
          writes=FS(C, 0, 2048), tag="4bd0")
    P.add("vector", lambda e: e.memset(AF_(C, 2048, 16), 0.0),
          writes=FS(C, 2048, 512), tag="4ks0")

    for tt in range(4):
        emit_ln(C, lambda c, tt=tt: C.x_res[:, c, 512 * tt:512 * tt + 512],
                [C.Rxres[tt]], O_LNX, O_TMP, (6, 7), f"4n{tt}")

        for nt in range(4):
            for k in range(16):
                wko = O_WM + 512 * (k % 2)
                wvo = O_WM + 1024 + 512 * (k % 2)

                def em_wk(e, k=k, nt=nt, wko=wko):
                    return e.dma_start(
                        out=A(C, wko, 512),
                        in_=_dap(C.wB["wkb"], k * 262144 + 512 * nt,
                                 [[2048, 128], [1, 512]]))
                P.add("sync", em_wk, reads=[C.RwB["wkb"]],
                      writes=BS(C, wko, 512), ndma=1, tag=f"4wk{tt}_{nt}_{k}")

                def em_wv(e, k=k, nt=nt, wvo=wvo):
                    return e.dma_start(
                        out=A(C, wvo, 512),
                        in_=_dap(C.wB["wvb"], k * 262144 + 512 * nt,
                                 [[2048, 128], [1, 512]]))
                P.add("sync", em_wv, reads=[C.RwB["wvb"]],
                      writes=BS(C, wvo, 512), ndma=1, tag=f"4wv{tt}_{nt}_{k}")

                def em_kvmm(e, k=k, wko=wko, wvo=wvo):
                    out = []
                    for blk in range(4):
                        lhs = A(C, O_LNX + 512 * k + 128 * blk, 128)
                        out.append(e.matmul(C.pb[blk][:, :], lhs, A(C, wko, 512),
                                            start=(k == 0), stop=(k == 15)))
                        out.append(e.matmul(C.pb[4 + blk][:, :], lhs, A(C, wvo, 512),
                                            start=(k == 0), stop=(k == 15)))
                    return out
                P.add("tensor", em_kvmm,
                      reads=BS(C, wko, 512) + BS(C, wvo, 512) + BS(C, O_LNX, 8192),
                      writes=[C.Rpb[i] for i in range(8)],
                      tag=f"4kv{tt}_{nt}_{k}")

            for blk in range(4):
                kto = O_KT + 2048 * blk + 512 * nt
                vto = O_VT + 2048 * blk + 512 * nt

                def em_kev(e, blk=blk, kto=kto):
                    return e.tensor_relu(A(C, kto, 512), C.pb[blk][:, :])
                P.add("vector", em_kev, reads=[C.Rpb[blk]],
                      writes=BS(C, kto, 512), tag=f"4ke{tt}_{nt}_{blk}")

                def em_vev(e, blk=blk, vto=vto):
                    return e.tensor_copy(A(C, vto, 512), C.pb[4 + blk][:, :])
                P.add("vector", em_vev, reads=[C.Rpb[4 + blk]],
                      writes=BS(C, vto, 512), tag=f"4ve{tt}_{nt}_{blk}")

        for grp in range(16):
            vb = grp % 2

            def em_vk(e, grp=grp, vb=vb):
                out = []
                for blk in range(4):
                    out.append(e.matmul(
                        C.pb[vb][:, 0:128],
                        A(C, O_KT + 2048 * blk + 128 * grp, 128),
                        A(C, O_VT + 2048 * blk + 128 * grp, 128),
                        start=(blk == 0), stop=(blk == 3)))
                return out
            P.add("tensor", em_vk,
                  reads=BS(C, O_KT, 8192) + BS(C, O_VT, 8192),
                  writes=[C.Rpb[vb]], tag=f"4vk{tt}_{grp}")

            def em_vka(e, grp=grp, vb=vb):
                dst = AF_(C, 128 * grp, 128)
                return e.tensor_tensor(dst, C.pb[vb][:, 0:128], dst, OP.add)
            P.add("vector", em_vka, reads=[C.Rpb[vb]] + FS(C, 128 * grp, 128),
                  writes=FS(C, 128 * grp, 128), tag=f"4va{tt}_{grp}")

            def em_ks(e, grp=grp):
                out = []
                for blk in range(4):
                    out.append(e.matmul(
                        C.pb[2][:, grp:grp + 1],
                        A(C, O_KT + 2048 * blk + 128 * grp, 128),
                        C.ones_col[:, :],
                        start=(blk == 0), stop=(blk == 3)))
                return out
            P.add("tensor", em_ks, reads=BS(C, O_KT, 8192) + [C.Rconst],
                  writes=[C.Rpb[2]], tag=f"4ks{tt}_{grp}")

        def em_ksa(e):
            dst = AF_(C, 2048, 16)
            return e.tensor_tensor(dst, C.pb[2][:, 0:16], dst, OP.add)
        P.add("vector", em_ksa, reads=[C.Rpb[2]] + FS(C, 2048, 512),
              writes=FS(C, 2048, 512), tag=f"4ka{tt}")

    for grp in range(16):
        def em_m(e, grp=grp):
            dst = AF_(C, 128 * grp, 128)
            return e.tensor_tensor(dst, dst, C.bdf_sb[:, :], OP.mult)
        P.add("vector", em_m, reads=[C.Rconst] + FS(C, 128 * grp, 128),
              writes=FS(C, 128 * grp, 128), tag=f"4bm{grp}")

    def em_ex(g):
        with (
            g.register("p") as p,
            g.register("off") as off,
            g.register("cnt") as cnt,
            g.register("cond") as cond,
            g.register("fl") as fl,
        ):
            g.reg_load(p, C.par_sb[:1, :1])
            g.reg_mov(off, 0)
            g.reg_add(off, off, p)
            g.reg_mul(off, off, 128 * 2064)
            g.dma_start(bass.AP(C.sh_ex, off, [[2064, 128], [1, 2048]]),
                        AF_(C, 0, 2048)).then_inc(C.s_poll, 16)
            g.reg_add(off, off, 2048)
            g.dma_start(bass.AP(C.sh_ex, off, [[2064, 128], [1, 16]]),
                        AF_(C, 2048, 16)).then_inc(C.s_poll, 16)
            g.wait_ge(C.s_poll, 32)
            g.reg_mov(off, 0)
            g.reg_add(off, off, p)
            g.reg_mul(off, off, 16)
            g.dma_start(bass.AP(C.sh_fl, off, [[16, 1], [1, 16]]),
                        C.flag_sb[:, :]).then_inc(C.s_poll, 16)
            g.wait_ge(C.s_poll, 48)
            g.reg_mov(off, 1)
            g.reg_sub(off, off, p)
            g.reg_mul(off, off, 16)
            g.reg_mov(cnt, 48)
            g.reg_mov(cond, 1)
            with g.While(cond):
                g.reg_add(cnt, cnt, 16)
                g.dma_start(C.pfl_sb[:, :],
                            bass.AP(C.sh_fl, off, [[16, 1], [1, 16]])
                            ).then_inc(C.s_poll, 16)
                g.wait_ge(C.s_poll, cnt)
                g.reg_load(fl, C.pfl_sb[:1, 0:1])
                g.reg_mov(cond, fl)
                g.reg_sub(cond, cond, MAGIC1)
            g.reg_mov(cond, 1)
            with g.While(cond):
                g.reg_add(cnt, cnt, 16)
                g.dma_start(C.pfl_sb[:, :],
                            bass.AP(C.sh_fl, off, [[16, 1], [1, 16]])
                            ).then_inc(C.s_poll, 16)
                g.wait_ge(C.s_poll, cnt)
                g.reg_load(fl, C.pfl_sb[:1, 1:2])
                g.reg_mov(cond, fl)
                g.reg_sub(cond, cond, MAGIC2)
            g.reg_mov(off, 1)
            g.reg_sub(off, off, p)
            g.reg_mul(off, off, 128 * 2064)
            r1 = g.dma_start(AF_(C, 2560, 2048),
                             bass.AP(C.sh_ex, off, [[2064, 128], [1, 2048]]))
            g.reg_add(off, off, 2048)
            r2 = g.dma_start(AF_(C, 4608, 16),
                             bass.AP(C.sh_ex, off, [[2064, 128], [1, 16]]))
            return [r1, r2]
    P.add("gpsimd", em_ex,
          reads=FS(C, 0, 2560) + [C.Rconst],
          writes=FS(C, 2560, 2560), ndma=2, tag="4ex")

    O_BDT, O_KST, O_LNX2, O_QT, O_B, O_ATT2, O_WQ2, O_WP2, O_YT2, O_TMP2 = \
        0, 2048, 2560, 10752, 11776, 12800, 20992, 23040, 25088, 26112

    P.add("vector", lambda e: e.tensor_tensor(A(C, O_BDT, 2048),
                                              AF_(C, 0, 2048),
                                              AF_(C, 2560, 2048), OP.add),
          reads=FS(C, 0, 2048) + FS(C, 2560, 2048),
          writes=BS(C, O_BDT, 2048), tag="4tot")
    P.add("vector", lambda e: e.tensor_tensor(
        AF_(C, 7168, 16),
        AF_(C, 2048, 16), AF_(C, 4608, 16), OP.add),
        reads=FS(C, 2048, 512) + FS(C, 4608, 512),
        writes=FS(C, 7168, 512), tag="4kt")

    for tt in range(4):
        emit_ln(C, lambda c, tt=tt: C.x_res[:, c, 512 * tt:512 * tt + 512],
                [C.Rxres[tt]], O_LNX2, O_TMP2, (6, 7), f"4m{tt}")

        for grp in range(16):
            def em_wq(e, grp=grp):
                return e.dma_start(
                    out=A(C, O_WQ2, 2048),
                    in_=_dap(C.wB["wqb"], grp * 262144, [[2048, 128], [1, 2048]]))
            P.add("sync", em_wq, reads=[C.RwB["wqb"]],
                  writes=BS(C, O_WQ2, 2048), ndma=1, tag=f"4wq{tt}_{grp}")

            qb = grp % 2

            def em_qmm(e, qb=qb):
                out = []
                for k in range(16):
                    out.append(e.matmul(C.pb[qb][:, :],
                                        A(C, O_WQ2 + 128 * k, 128),
                                        A(C, O_LNX2 + 512 * k, 512),
                                        start=(k == 0), stop=(k == 15)))
                return out
            P.add("tensor", em_qmm,
                  reads=BS(C, O_WQ2, 2048) + BS(C, O_LNX2, 8192),
                  writes=[C.Rpb[qb]], tag=f"4qm{tt}_{grp}")

            qo = O_QT + 512 * (grp % 2)

            def em_qev(e, qb=qb, qo=qo):
                return e.tensor_relu(A(C, qo, 512), C.pb[qb][:, :])
            P.add("vector", em_qev, reads=[C.Rpb[qb]], writes=BS(C, qo, 512),
                  tag=f"4qe{tt}_{grp}")

            bo = O_B + 512 * (grp % 2)

            def em_b(e, grp=grp, qo=qo, bo=bo):
                return e.tensor_scalar(
                    A(C, bo, 512), A(C, qo, 512),
                    C.af32[:, 7168 + grp:7168 + grp + 1],
                    None, OP.mult)
            P.add("vector", em_b,
                  reads=BS(C, qo, 512) + FS(C, 7168, 512),
                  writes=BS(C, bo, 512), tag=f"4b{tt}_{grp}")

            db = 2 + grp % 2

            def em_den(e, bo=bo, db=db):
                return e.matmul(C.pb[db][:, :], C.bdb_sb[:, :], A(C, bo, 512),
                                start=True, stop=True)
            P.add("tensor", em_den, reads=BS(C, bo, 512) + [C.Rconst],
                  writes=[C.Rpb[db]], tag=f"4dn{tt}_{grp}")

            ob = 4 + grp % 2

            def em_ov(e, grp=grp, qo=qo, ob=ob):
                return e.matmul(C.pb[ob][:, :], A(C, O_BDT + 128 * grp, 128),
                                A(C, qo, 512), start=True, stop=True)
            P.add("tensor", em_ov,
                  reads=BS(C, O_BDT + 128 * grp, 128) + BS(C, qo, 512),
                  writes=[C.Rpb[ob]], tag=f"4ov{tt}_{grp}")

            do = 5120 + 512 * (grp % 2)

            def em_eps(e, db=db, do=do):
                return e.activation(AF_(C, do, 512), C.pb[db][:, :],
                                    AF.Identity, bias=f32(1e-15))
            P.add("scalar", em_eps, reads=[C.Rpb[db]], writes=FS(C, do, 512),
                  tag=f"4ep{tt}_{grp}")

            ro = 6144 + 512 * (grp % 2)

            def em_rec(e, do=do, ro=ro):
                return e.reciprocal(AF_(C, ro, 512), AF_(C, do, 512))
            P.add("vector", em_rec, reads=FS(C, do, 512),
                  writes=FS(C, ro, 512), tag=f"4rc{tt}_{grp}")

            ao = O_ATT2 + 512 * grp

            def em_att(e, ob=ob, ro=ro, ao=ao):
                return e.tensor_tensor(A(C, ao, 512), C.pb[ob][:, :],
                                       AF_(C, ro, 512), OP.mult)
            P.add("vector", em_att, reads=[C.Rpb[ob]] + FS(C, ro, 512),
                  writes=BS(C, ao, 512), tag=f"4at{tt}_{grp}")

        for oc in range(16):
            def em_wp(e, oc=oc):
                return e.dma_start(
                    out=A(C, O_WP2, 2048),
                    in_=_dap(C.wB["wpb"], oc * 262144, [[2048, 128], [1, 2048]]))
            P.add("sync", em_wp, reads=[C.RwB["wpb"]],
                  writes=BS(C, O_WP2, 2048), ndma=1, tag=f"4wp{tt}_{oc}")

            ob = oc % 2

            def em_pmm(e, ob=ob):
                out = []
                for k in range(16):
                    out.append(e.matmul(C.pb[ob][:, :],
                                        A(C, O_WP2 + 128 * k, 128),
                                        A(C, O_ATT2 + 512 * k, 512),
                                        start=(k == 0), stop=(k == 15)))
                return out
            P.add("tensor", em_pmm,
                  reads=BS(C, O_WP2, 2048) + BS(C, O_ATT2, 8192),
                  writes=[C.Rpb[ob]], tag=f"4pm{tt}_{oc}")

            yo = O_YT2 + 512 * (oc % 2)

            def em_yb(e, oc=oc, ob=ob, yo=yo):
                return e.tensor_scalar(A(C, yo, 512), C.pb[ob][:, :],
                                       C.biasb_sb[:, oc:oc + 1], None, OP.add)
            P.add("vector", em_yb, reads=[C.Rpb[ob], C.Rconst],
                  writes=BS(C, yo, 512), tag=f"4yb{tt}_{oc}")

            def em_res(e, oc=oc, yo=yo, tt=tt):
                xs = C.x_res[:, oc, 512 * tt:512 * tt + 512]
                return e.tensor_tensor(xs, A(C, yo, 512), xs, OP.add)
            P.add("vector", em_res, reads=BS(C, yo, 512),
                  writes=[C.Rxres[tt]], tag=f"4rs{tt}_{oc}")



def emit_dbg(C, what):
    P, di = C.P, C.di
    O_LNX, O_TMP = 0, 47104
    emit_ln(C, lambda c: C.x_res[:, c, 0:512], [C.Rxres[0]],
            O_LNX, O_TMP, (6, 7), "dbn")
    routs = [Res(f"dbg{i}") for i in range(16)]
    if what == "lnx":
        for c in range(16):
            def em(e, c=c):
                return e.dma_start(
                    out=_dap(di["y"], c * 128 * 2048, [[2048, 128], [1, 512]]),
                    in_=A(C, O_LNX + 512 * c, 512))
            P.add("gpsimd", em, reads=BS(C, O_LNX + 512 * c, 512),
                  writes=[routs[c]], ndma=1, tag=f"dbgo{c}")
        for r in range(4):
            def em2(e, r=r):
                return e.dma_start(
                    out=_dap(di["y"], r * 2048 + 512, [[2048, 1], [1, 512]]),
                    in_=C.rows[0:1, 512 * r:512 * r + 512])
            P.add("gpsimd", em2, reads=[C.Rrows], writes=[routs[4 + r]],
                  ndma=1, tag=f"dbgr{r}")
    else:
        for s in range(4):
            def em_wg(e, s=s):
                return e.dma_start(
                    out=A(C, 8192, 2048),
                    in_=_dap(C.wB["wg"], s * 262144, [[2048, 128], [1, 2048]]))
            P.add("sync", em_wg, reads=[C.RwB["wg"]], writes=BS(C, 8192, 2048),
                  ndma=1, tag=f"dbwg{s}")
            def em_wu(e, s=s):
                return e.dma_start(
                    out=A(C, 12288, 2048),
                    in_=_dap(C.wB["wu"], s * 262144, [[2048, 128], [1, 2048]]))
            P.add("sync", em_wu, reads=[C.RwB["wu"]], writes=BS(C, 12288, 2048),
                  ndma=1, tag=f"dbwu{s}")
            def em_gmm(e):
                out = []
                for k in range(16):
                    out.append(e.matmul(C.pb[0][:, :], A(C, 8192 + 128 * k, 128),
                                        A(C, O_LNX + 512 * k, 512),
                                        start=(k == 0), stop=(k == 15)))
                return out
            P.add("tensor", em_gmm, reads=BS(C, 8192, 2048) + BS(C, O_LNX, 8192),
                  writes=[C.Rpb[0]], tag=f"dbgm{s}")
            def em_umm(e):
                out = []
                for k in range(16):
                    out.append(e.matmul(C.pb[2][:, :], A(C, 12288 + 128 * k, 128),
                                        A(C, O_LNX + 512 * k, 512),
                                        start=(k == 0), stop=(k == 15)))
                return out
            P.add("tensor", em_umm, reads=BS(C, 12288, 2048) + BS(C, O_LNX, 8192),
                  writes=[C.Rpb[2]], tag=f"dbum{s}")
            def em_sil(e):
                return e.activation(AF_(C, 0, 512), C.pb[0][:, :], AF.Silu)
            P.add("scalar", em_sil, reads=[C.Rpb[0]], writes=FS(C, 0, 512),
                  tag=f"dbsi{s}")
            def em_h(e, s=s):
                return e.tensor_tensor(AF_(C, 1024 + 512 * s, 512), AF_(C, 0, 512),
                                       C.pb[2][:, :], OP.mult)
            P.add("vector", em_h, reads=FS(C, 0, 512) + [C.Rpb[2]],
                  writes=FS(C, 1024 + 512 * s, 512), tag=f"dbh{s}")
        for s in range(4):
            def em(e, s=s):
                return e.dma_start(
                    out=_dap(di["y"], s * 128 * 2048, [[2048, 128], [1, 512]]),
                    in_=AF_(C, 1024 + 512 * s, 512))
            P.add("gpsimd", em, reads=FS(C, 1024 + 512 * s, 512),
                  writes=[routs[s]], ndma=1, tag=f"dbgo{s}")
    P.add("gpsimd", None, reads=routs, tag="dbfin")



def emit_P6(C):
    P, di = C.P, C.di
    routs = [Res(f"out{c}") for c in range(16)]
    for c in range(16):
        def em(e, c=c):
            return e.dma_start(
                out=_dap(di["y"], c * 128 * 2048, [[2048, 128], [1, 2048]]),
                in_=C.x_res[:, c, :])
        P.add("gpsimd", em, reads=list(C.Rxres), writes=[routs[c]],
              ndma=1, tag=f"o{c}")
    P.add("gpsimd", None, reads=routs, tag="fin")



_NC_CACHE = {}


def kernel(**inputs):
    if "nc" not in _NC_CACHE:
        _NC_CACHE["nc"] = build_nc()
    nc = _NC_CACHE["nc"]
    in_maps = [_prep_core(inputs, core) for core in range(8)]
    res = run_bass_kernel_spmd(nc, in_maps, core_ids=list(range(8)))
    out = np.zeros((4, 4096, 2048), dtype=np.float32)
    for core in range(8):
        seq, par = core // 2, core % 2
        own = _own_blocks(par)
        own_tok = np.concatenate([np.arange(128 * g, 128 * g + 128) for g in own])
        y = np.asarray(res.results[core]["y"])
        out[seq][own_tok, :] = y.T
    return out


# version 06
# speedup vs baseline: 1.4478x; 1.0241x over previous
import numpy as np
import ml_dtypes
import concourse.bass as bass
import concourse.mybir as mybir
from concourse.bass_utils import run_bass_kernel_spmd
from concourse.alu_op_type import AluOpType as OP

F32 = mybir.dt.float32
BF16 = mybir.dt.bfloat16
I32 = mybir.dt.int32
AF = mybir.ActivationFunctionType
f32 = np.float32

ABF_N = 50176
AF32_N = 8192
SLOT = 512
MAGIC1 = 0x12AB34CD
MAGIC2 = 0x56EF78A1
MAGIC3 = 0x3C4D5E6F
MAGIC4 = 0x718293A4
INV_SQRT_HD = f32(1.0 / np.sqrt(128.0))
EXP_SHIFT = f32(-20.0)

PHASES = (1, 2, 3, 4, 5)
SPARSIFY = True
P2_SB = (2, 3, 6, 7)
P2_LAG = 2
DBG = None

ENGINES = ("sync", "gpsimd", "scalar", "vector", "tensor")

_seq = [0]


class Ev:
    __slots__ = ("eng", "val", "waited", "seq")

    def __init__(self, eng):
        self.eng = eng
        self.val = None
        self.waited = False
        _seq[0] += 1
        self.seq = _seq[0]


class Res:
    __slots__ = ("name", "w", "rs")

    def __init__(self, name=""):
        self.name = name
        self.w = None
        self.rs = []


class Step:
    __slots__ = ("eng", "emit", "deps", "ev", "ndma", "tag", "signal")

    def __init__(self, eng, emit, deps, ev, ndma, tag):
        self.eng = eng
        self.emit = emit
        self.deps = deps
        self.ev = ev
        self.ndma = ndma
        self.tag = tag
        self.signal = False


class Prog:
    def __init__(self):
        self.steps = {e: [] for e in ENGINES}

    def add(self, eng, emit, reads=(), writes=(), ndma=0, tag=""):
        ev = Ev(eng)
        deps = {}
        for r in reads:
            if r.w is not None:
                deps[r.w.seq] = r.w
        for r in writes:
            if r.w is not None:
                deps[r.w.seq] = r.w
            for e in r.rs:
                deps[e.seq] = e
        deps.pop(ev.seq, None)
        for r in reads:
            r.rs.append(ev)
        for r in writes:
            r.w = ev
            r.rs = []
        final = []
        for d in sorted(deps.values(), key=lambda d: d.seq):
            if d.eng == eng == "tensor":
                continue
            final.append(d)
            d.waited = True
        self.steps[eng].append(Step(eng, emit, final, ev, ndma, tag))
        return ev

    def plan(self):
        for eng in ENGINES:
            cnt = 0
            for st in self.steps[eng]:
                if st.ev.waited or st.ndma:
                    st.signal = True
                    cnt += 16 * st.ndma if st.ndma else 1
                    st.ev.val = cnt

    def emit_engine(self, eng_name, eng_obj, sems):
        water = {}
        for st in self.steps[eng_name]:
            need = {}
            for d in st.deps:
                assert d.val is not None, f"dep of {st.tag} not planned"
                need[d.eng] = max(need.get(d.eng, 0), d.val)
            for deng, val in sorted(need.items()):
                if water.get(deng, 0) >= val:
                    continue
                eng_obj.wait_ge(sems[deng], val)
                water[deng] = val
            out = st.emit(eng_obj) if st.emit is not None else None
            if st.signal:
                assert out is not None, f"step {st.tag} must emit to signal"
                instrs = out if isinstance(out, (list, tuple)) else [out]
                if st.ndma:
                    assert len(instrs) == st.ndma, \
                        f"{st.tag}: ndma={st.ndma} got {len(instrs)}"
                    for i in instrs:
                        i.then_inc(sems[eng_name], 16)
                else:
                    instrs[-1].then_inc(sems[eng_name], 1)



def _own_blocks(par):
    keep = (0, 3) if par == 0 else (1, 2)
    return [g for g in range(32) if g % 4 in keep]


def _prep_core(inputs, core):
    seq, par = core // 2, core % 2
    bf = ml_dtypes.bfloat16
    x = np.asarray(inputs["L0_attn_a_x"][seq])
    xT = np.ascontiguousarray(x.T)
    own = _own_blocks(par)
    own_tok = np.concatenate([np.arange(128 * g, 128 * g + 128) for g in own])
    x_own = np.ascontiguousarray(xT[:, own_tok])

    def lhs_stream(W):
        return np.ascontiguousarray(
            W.reshape(16, 128, 16, 128).transpose(0, 3, 2, 1))

    def mov_stream(W):
        return np.ascontiguousarray(W.T.reshape(16, 128, 2048))

    Wqkv = np.asarray(inputs["L0_attn_a_W_qkv"])
    Wq, Wk, Wv = Wqkv[0:2048], Wqkv[2048:4096], Wqkv[4096:6144]
    Wo = np.asarray(inputs["L0_attn_a_W_out"])
    Wqkvb = np.asarray(inputs["L0_attn_b_w_qkv"])
    Wqb, Wkb, Wvb = Wqkvb[0:2048], Wqkvb[2048:4096], Wqkvb[4096:6144]
    Wpb = np.asarray(inputs["L0_attn_b_w_proj"])
    bpb = np.asarray(inputs["L0_attn_b_b_proj"])
    Wg = np.asarray(inputs["L0_ffn_w_gate"])
    Wu = np.asarray(inputs["L0_ffn_w_up"])
    Wd = np.asarray(inputs["L0_ffn_w_down"])

    def gate_stream(W):
        return np.ascontiguousarray(
            W.reshape(16, 128, 16, 128, 4).transpose(2, 4, 1, 0, 3))

    wd_h = np.ascontiguousarray(
        Wd.reshape(16, 128, 4, 16, 128).transpose(3, 1, 0, 2, 4)
    ).reshape(16, 128, 64, 128)

    masks = np.zeros((4, 8, 128, 512), dtype=np.float32)
    for t in range(4):
        qg = [own[4 * t + i] for i in range(4)]
        qpos = np.concatenate([128 * g + np.arange(128) for g in qg])
        for j in range(8):
            kpos = 128 * (8 * t + j) + np.arange(128)
            masks[t, j] = (kpos[:, None] <= qpos[None, :]).astype(np.float32)

    bd = np.zeros((128, 128), dtype=np.float32)
    for i in range(4):
        bd[32 * i:32 * i + 32, 32 * i:32 * i + 32] = 1.0

    return {
        "x_own": x_own,
        "x_full": xT,
        "wq": lhs_stream(Wq), "wk": lhs_stream(Wk), "wo": lhs_stream(Wo),
        "wv": mov_stream(Wv),
        "wqb": lhs_stream(Wqb), "wpb": lhs_stream(Wpb),
        "wkb": mov_stream(Wkb), "wvb": mov_stream(Wvb),
        "wg": gate_stream(Wg), "wu": gate_stream(Wu), "wd": wd_h,
        "biasb": np.ascontiguousarray(bpb.reshape(16, 128).T),
        "masks": masks.astype(bf),
        "bdones_f": bd,
        "bdones_b": bd.astype(bf),
        "parity": np.array([[par]], dtype=np.int32),
    }



class Ctx:
    pass


def _dap(t, off, dims):
    return bass.AP(t, off, dims)


def build_nc(phases=PHASES):
    nc = bass.Bass("TRN2", target_bir_lowering=False)
    C = Ctx()
    C.nc = nc
    P = Prog()
    C.P = P

    di = {}
    di["x_own"] = nc.dram_tensor("x_own", [2048, 2048], F32, kind="ExternalInput")
    di["x_full"] = nc.dram_tensor("x_full", [2048, 4096], F32, kind="ExternalInput")
    for n in ("wq", "wk", "wo", "wqb", "wpb"):
        di[n] = nc.dram_tensor(n, [16, 128, 16, 128], F32, kind="ExternalInput")
    for n in ("wv", "wkb", "wvb"):
        di[n] = nc.dram_tensor(n, [16, 128, 2048], F32, kind="ExternalInput")
    di["wg"] = nc.dram_tensor("wg", [16, 4, 128, 16, 128], F32, kind="ExternalInput")
    di["wu"] = nc.dram_tensor("wu", [16, 4, 128, 16, 128], F32, kind="ExternalInput")
    di["wd"] = nc.dram_tensor("wd", [16, 128, 64, 128], F32, kind="ExternalInput")
    di["biasb"] = nc.dram_tensor("biasb", [128, 16], F32, kind="ExternalInput")
    di["masks"] = nc.dram_tensor("masks", [4, 8, 128, 512], BF16, kind="ExternalInput")
    di["bdones_f"] = nc.dram_tensor("bdones_f", [128, 128], F32, kind="ExternalInput")
    di["bdones_b"] = nc.dram_tensor("bdones_b", [128, 128], BF16, kind="ExternalInput")
    di["parity"] = nc.dram_tensor("parity", [1, 1], I32, kind="ExternalInput")
    di["y"] = nc.dram_tensor("y", [2048, 2048], F32, kind="ExternalOutput")
    C.di = di

    C.wsize = {
        "wq": 16 * 128 * 16 * 128, "wk": 16 * 128 * 16 * 128,
        "wo": 16 * 128 * 16 * 128, "wqb": 16 * 128 * 16 * 128,
        "wpb": 16 * 128 * 16 * 128,
        "wv": 16 * 128 * 2048, "wkb": 16 * 128 * 2048, "wvb": 16 * 128 * 2048,
        "wg": 16 * 4 * 128 * 16 * 128, "wu": 16 * 4 * 128 * 16 * 128,
        "wd": 16 * 128 * 64 * 128,
    }
    C.wB = {n: nc.dram_tensor(n + "_bf", [C.wsize[n] // 128, 128], BF16)
            for n in C.wsize}
    C.RwB = {n: Res("R_" + n) for n in C.wB}

    C.Kscr = nc.dram_tensor("Kscr", [16, 128, 4096], BF16)
    C.Vscr = nc.dram_tensor("Vscr", [16, 4096, 128], BF16)
    C.RK = Res("RK")
    C.RV = Res("RV")

    C.sh_ex = nc.dram_tensor("sh_ex", [2, 128, 2064], F32, addr_space="Shared")
    C.sh_fl = nc.dram_tensor("sh_fl", [2, 1, 16], I32, addr_space="Shared")

    from contextlib import ExitStack
    with ExitStack() as _st:
        en = _st.enter_context
        x_res = en(nc.sbuf_tensor("x_res", [128, 16, 2048], BF16))
        abf = en(nc.sbuf_tensor("abf", [128, ABF_N], BF16))
        af32 = en(nc.sbuf_tensor("af32", [128, AF32_N], F32))
        rows = en(nc.sbuf_tensor("rows", [1, 2048], F32))
        rows_bf = en(nc.sbuf_tensor("rows_bf", [1, 2048], BF16))
        ones_row = en(nc.sbuf_tensor("ones_row", [1, 128], BF16))
        ones_col = en(nc.sbuf_tensor("ones_col", [128, 1], BF16))
        bdb_sb = en(nc.sbuf_tensor("bdb_sb", [128, 128], BF16))
        bdf_sb = en(nc.sbuf_tensor("bdf_sb", [128, 128], F32))
        biasb_sb = en(nc.sbuf_tensor("biasb_sb", [128, 16], F32))
        par_sb = en(nc.sbuf_tensor("par_sb", [1, 1], I32))
        flag_sb = en(nc.sbuf_tensor("flag_sb", [1, 16], I32))
        pfl_sb = en(nc.sbuf_tensor("pfl_sb", [1, 16], I32))
        pbs = [en(nc.psum_tensor(f"pb{i}", [128, 512], F32)) for i in range(8)]
        s_sync = en(nc.semaphore("s_sync"))
        s_gp = en(nc.semaphore("s_gp"))
        s_sc = en(nc.semaphore("s_sc"))
        s_ve = en(nc.semaphore("s_ve"))
        s_pe = en(nc.semaphore("s_pe"))
        s_poll = en(nc.semaphore("s_poll"))
        s_bar = en(nc.semaphore("s_bar"))
        block = en(nc.Block())

        C.x_res = x_res
        C.abf = abf
        C.af32 = af32
        C.rows = rows
        C.rows_bf = rows_bf
        C.ones_row = ones_row
        C.ones_col = ones_col
        C.bdb_sb = bdb_sb
        C.bdf_sb = bdf_sb
        C.biasb_sb = biasb_sb
        C.par_sb = par_sb
        C.flag_sb = flag_sb
        C.pfl_sb = pfl_sb
        C.pb = pbs
        C.Rpb = [Res(f"pb{i}") for i in range(8)]
        C.Rxres = [Res(f"xres{t}") for t in range(4)]
        C.Rrows = Res("rows")
        C.Rden = Res("rden")
        C.Rconst = Res("const")
        C.s_poll = s_poll
        C.s_bar = s_bar
        C.sems = {"sync": s_sync, "gpsimd": s_gp, "scalar": s_sc,
                  "vector": s_ve, "tensor": s_pe}
        C.bslots = [Res(f"b{i}") for i in range(ABF_N // SLOT)]
        C.fslots = [Res(f"f{i}") for i in range(AF32_N // SLOT)]

        emit_P0(C)
        if DBG is not None:
            emit_dbg(C, DBG)
            emit_P6_skip = True
            phases = ()
        else:
            emit_P6_skip = False
        if 1 in phases:
            emit_P1(C)
        if 2 in phases:
            emit_P2(C)
        feed_casts(C, 10 ** 6)
        if 3 in phases:
            emit_ffn(C, "P3")
        if 4 in phases:
            emit_P4(C)
        if 5 in phases:
            emit_ffn(C, "P5")
        if not emit_P6_skip:
            emit_P6(C)

        P.plan()

        @block.sync
        def _(e):
            P.emit_engine("sync", e, C.sems)

        @block.gpsimd
        def _(e):
            P.emit_engine("gpsimd", e, C.sems)

        @block.scalar
        def _(e):
            P.emit_engine("scalar", e, C.sems)

        @block.vector
        def _(e):
            P.emit_engine("vector", e, C.sems)

        @block.tensor
        def _(e):
            P.emit_engine("tensor", e, C.sems)

    return nc



def A(C, off, n):
    assert off + n <= ABF_N, (off, n)
    return C.abf[:, off:off + n]


def AF_(C, off, n):
    assert off + n <= AF32_N, (off, n)
    return C.af32[:, off:off + n]


def BS(C, off, n):
    return C.bslots[off // SLOT:(off + n + SLOT - 1) // SLOT]


def FS(C, off, n):
    return C.fslots[off // SLOT:(off + n + SLOT - 1) // SLOT]



def emit_P0(C):
    P, di = C.P, C.di

    P.add("vector", lambda e: e.memset(C.ones_row[:, :], 1.0),
          writes=[C.Rconst], tag="c_or")
    P.add("vector", lambda e: e.memset(C.ones_col[:, :], 1.0),
          writes=[C.Rconst], tag="c_oc")
    P.add("vector", lambda e: e.memset(C.flag_sb[:, 0:1], MAGIC1),
          writes=[C.Rconst], tag="c_f1")
    P.add("vector", lambda e: e.memset(C.flag_sb[:, 1:2], MAGIC2),
          writes=[C.Rconst], tag="c_f2")
    P.add("vector", lambda e: e.memset(C.flag_sb[:, 2:3], MAGIC3),
          writes=[C.Rconst], tag="c_f3")
    P.add("vector", lambda e: e.memset(C.flag_sb[:, 3:4], MAGIC4),
          writes=[C.Rconst], tag="c_f4")
    P.add("sync", lambda e: [
        e.dma_start(out=C.bdb_sb[:, :], in_=_dap(di["bdones_b"], 0, [[128, 128], [1, 128]])),
        e.dma_start(out=C.bdf_sb[:, :], in_=_dap(di["bdones_f"], 0, [[128, 128], [1, 128]])),
        e.dma_start(out=C.biasb_sb[:, :], in_=_dap(di["biasb"], 0, [[16, 128], [1, 16]])),
        e.dma_start(out=C.par_sb[:, :], in_=_dap(di["parity"], 0, [[1, 1], [1, 1]])),
    ], writes=[C.Rconst], ndma=4, tag="c_ld")

    C.Rbar = Res("bar")

    def em_bar(g):
        with (
            g.register("bp") as bp,
            g.register("bwoff") as bwoff,
            g.register("broff") as broff,
            g.register("bcnt") as bcnt,
            g.register("bcond") as bcond,
            g.register("bfl") as bfl,
        ):
            g.reg_load(bp, C.par_sb[:1, :1])
            g.reg_mov(bwoff, 0)
            g.reg_add(bwoff, bwoff, bp)
            g.reg_mul(bwoff, bwoff, 16)
            g.reg_add(bwoff, bwoff, 2)
            g.reg_mov(broff, 1)
            g.reg_sub(broff, broff, bp)
            g.reg_mul(broff, broff, 16)
            g.reg_add(broff, broff, 2)
            g.reg_mov(bcnt, 0)
            g.reg_mov(bcond, 1)
            with g.While(bcond):
                g.reg_add(bcnt, bcnt, 32)
                g.dma_start(bass.AP(C.sh_fl, bwoff, [[16, 1], [1, 2]]),
                            C.flag_sb[:, 2:4]).then_inc(C.s_bar, 16)
                g.dma_start(C.pfl_sb[:, 0:2],
                            bass.AP(C.sh_fl, broff, [[16, 1], [1, 2]])
                            ).then_inc(C.s_bar, 16)
                g.wait_ge(C.s_bar, bcnt)
                g.reg_load(bfl, C.pfl_sb[:1, 0:1])
                g.reg_mov(bcond, bfl)
                g.reg_sub(bcond, bcond, MAGIC3)
            g.reg_mov(bcond, 1)
            with g.While(bcond):
                g.reg_add(bcnt, bcnt, 32)
                g.dma_start(bass.AP(C.sh_fl, bwoff, [[16, 1], [1, 2]]),
                            C.flag_sb[:, 2:4]).then_inc(C.s_bar, 16)
                g.dma_start(C.pfl_sb[:, 0:2],
                            bass.AP(C.sh_fl, broff, [[16, 1], [1, 2]])
                            ).then_inc(C.s_bar, 16)
                g.wait_ge(C.s_bar, bcnt)
                g.reg_load(bfl, C.pfl_sb[:1, 1:2])
                g.reg_mov(bcond, bfl)
                g.reg_sub(bcond, bcond, MAGIC4)
            g.reg_add(bcnt, bcnt, 16)
            g.dma_start(bass.AP(C.sh_fl, bwoff, [[16, 1], [1, 2]]),
                        C.flag_sb[:, 2:4]).then_inc(C.s_bar, 16)
            g.wait_ge(C.s_bar, bcnt)
            return g.dma_start(C.pfl_sb[:, 0:2],
                               bass.AP(C.sh_fl, broff, [[16, 1], [1, 2]]))
    P.add("gpsimd", em_bar, reads=[C.Rconst], writes=[C.Rbar], ndma=1,
          tag="bar0")

    for c in range(16):
        def em(e, c=c):
            return e.dma_start(
                out=C.x_res[:, c, :],
                in_=_dap(di["x_own"], c * 128 * 2048, [[2048, 128], [1, 2048]]))
        P.add("gpsimd", em, reads=[C.Rbar], writes=list(C.Rxres), ndma=1,
              tag=f"xi{c}")

    C._bounce_i = 0

    def cast_jobs(name, nchunks=None):
        cols = C.wsize[name] // 128
        ch = 2048
        return [(name, i, min(ch, cols - i), cols) for i in range(0, cols, ch)]

    def emit_cast(job):
        name, i, w, cols = job
        slot = C._bounce_i % 2
        C._bounce_i += 1
        boff = slot * 2048

        def eml(e, name=name, i=i, w=w, cols=cols, boff=boff):
            return e.dma_start(out=AF_(C, boff, w),
                               in_=_dap(di[name], i, [[cols, 128], [1, w]]))

        def ems(e, name=name, i=i, w=w, cols=cols, boff=boff):
            return e.dma_start(out=_dap(C.wB[name], i, [[cols, 128], [1, w]]),
                               in_=AF_(C, boff, w))
        P.add("sync", eml, reads=[C.Rbar], writes=FS(C, boff, w), ndma=1,
              tag=f"pl{name}{i}")
        P.add("gpsimd", ems, reads=FS(C, boff, w), writes=[C.RwB[name]],
              ndma=1, tag=f"ps{name}{i}")

    C.emit_cast = emit_cast
    for job in cast_jobs("wk") + cast_jobs("wv"):
        emit_cast(job)
    C.cast_queue = []
    for n2 in ["wq", "wo", "wg", "wu", "wd", "wqb", "wkb", "wvb", "wpb"]:
        C.cast_queue.extend(cast_jobs(n2))


def feed_casts(C, n):
    for _ in range(n):
        if not C.cast_queue:
            return
        C.emit_cast(C.cast_queue.pop(0))



def emit_ln(C, get_x, reads_x, lnx_off, tmp_off, banks, tag):
    P = C.P
    W = 512
    b0, b1 = banks
    x2o = [tmp_off, tmp_off + W]
    rbco, mbco = tmp_off + 2 * W, tmp_off + 3 * W

    def em_sum(e):
        out = []
        for c in range(16):
            out.append(e.matmul(C.pb[b0][0:1, 0:W], C.ones_col[:, :], get_x(c),
                                start=(c == 0), stop=(c == 15)))
        return out
    P.add("tensor", em_sum, reads=list(reads_x) + [C.Rconst],
          writes=[C.Rpb[b0]], tag=tag + "sum")

    for c in range(16):
        o = x2o[c % 2]

        def em_sq(e, c=c, o=o):
            return e.activation(A(C, o, W), get_x(c), AF.Square)
        P.add("scalar", em_sq, reads=list(reads_x), writes=BS(C, o, W),
              tag=f"{tag}sq{c}")

        def em_mm(e, c=c, o=o):
            return e.matmul(C.pb[b1][0:1, 0:W], C.ones_col[:, :], A(C, o, W),
                            start=(c == 0), stop=(c == 15))
        P.add("tensor", em_mm, reads=BS(C, o, W) + [C.Rconst],
              writes=[C.Rpb[b1]], tag=f"{tag}sm{c}")

    mean = C.rows[0:1, 0:W]
    m2 = C.rows[0:1, 512:512 + W]
    var = C.rows[0:1, 1024:1024 + W]
    rstd = C.rows[0:1, 1536:1536 + W]
    rstd_bf = C.rows_bf[0:1, 0:W]
    nmr_bf = C.rows_bf[0:1, 512:512 + W]

    P.add("vector", lambda e: e.tensor_scalar(mean, C.pb[b0][0:1, 0:W],
                                              float(1.0 / 2048), None, OP.mult),
          reads=[C.Rpb[b0]], writes=[C.Rrows], tag=tag + "mn")
    P.add("vector", lambda e: e.tensor_tensor(m2, mean, mean, OP.mult),
          reads=[C.Rrows], writes=[C.Rrows], tag=tag + "m2")
    P.add("vector", lambda e: e.scalar_tensor_tensor(var, C.pb[b1][0:1, 0:W],
                                                     float(1.0 / 2048), m2,
                                                     OP.mult, OP.subtract),
          reads=[C.Rpb[b1], C.Rrows], writes=[C.Rrows], tag=tag + "vr")
    P.add("scalar", lambda e: e.activation(m2, var, AF.Ln, bias=f32(1e-5)),
          reads=[C.Rrows], writes=[C.Rrows], tag=tag + "ln")
    P.add("scalar", lambda e: e.activation(rstd, m2, AF.Exp, scale=f32(-0.5)),
          reads=[C.Rrows], writes=[C.Rrows], tag=tag + "rs")
    P.add("vector", lambda e: e.scalar_tensor_tensor(var, mean, float(-1.0),
                                                     rstd, OP.mult, OP.mult),
          reads=[C.Rrows], writes=[C.Rrows], tag=tag + "nm")
    P.add("vector", lambda e: e.tensor_copy(rstd_bf, rstd),
          reads=[C.Rrows], writes=[C.Rrows], tag=tag + "cr")
    P.add("vector", lambda e: e.tensor_copy(nmr_bf, var),
          reads=[C.Rrows], writes=[C.Rrows], tag=tag + "cm")

    P.add("tensor", lambda e: e.matmul(C.pb[b0][:, 0:W], C.ones_row[:, :],
                                       rstd_bf, start=True, stop=True),
          reads=[C.Rrows, C.Rconst], writes=[C.Rpb[b0]], tag=tag + "br")
    P.add("tensor", lambda e: e.matmul(C.pb[b1][:, 0:W], C.ones_row[:, :],
                                       nmr_bf, start=True, stop=True),
          reads=[C.Rrows, C.Rconst], writes=[C.Rpb[b1]], tag=tag + "bm")
    P.add("scalar", lambda e: e.copy(A(C, rbco, W), C.pb[b0][:, 0:W]),
          reads=[C.Rpb[b0]], writes=BS(C, rbco, W), tag=tag + "er")
    P.add("scalar", lambda e: e.copy(A(C, mbco, W), C.pb[b1][:, 0:W]),
          reads=[C.Rpb[b1]], writes=BS(C, mbco, W), tag=tag + "em")

    for c in range(16):
        o = x2o[c % 2]

        def em_t1(e, c=c, o=o):
            return e.tensor_tensor(A(C, o, W), get_x(c), A(C, rbco, W), OP.mult)
        P.add("vector", em_t1, reads=list(reads_x) + BS(C, rbco, W),
              writes=BS(C, o, W), tag=f"{tag}t1{c}")

        def em_t2(e, c=c, o=o):
            return e.tensor_tensor(A(C, lnx_off + 512 * c, W), A(C, o, W),
                                   A(C, mbco, W), OP.add)
        P.add("vector", em_t2, reads=BS(C, o, W) + BS(C, mbco, W),
              writes=BS(C, lnx_off + 512 * c, W), tag=f"{tag}t2{c}")



def emit_P1(C):
    P, di = C.P, C.di
    O_XBF, O_LNX, O_TMP, O_WK, O_KEV, O_WV, O_VEV = \
        0, 16384, 32768, 34816, 38912, 39936, 48128

    for sl in range(8):
        lnoff = O_LNX + (sl % 2) * 8192
        xoff = O_XBF + (sl % 2) * 8192

        def em_lx(e, sl=sl, xoff=xoff):
            out = []
            for c in range(16):
                out.append(e.dma_start(
                    out=A(C, xoff + 512 * c, 512),
                    in_=_dap(di["x_full"], c * 128 * 4096 + 512 * sl,
                             [[4096, 128], [1, 512]])))
            return out
        P.add("gpsimd", em_lx, writes=BS(C, xoff, 8192), ndma=16,
              tag=f"1lx{sl}")
        feed_casts(C, 8)

        emit_ln(C, lambda c, xoff=xoff: A(C, xoff + 512 * c, 512),
                BS(C, xoff, 8192), lnoff, O_TMP, (6, 7), f"1n{sl}")

        for oc in range(16):
            wo = O_WK + 2048 * (oc % 2)

            def em_wk(e, oc=oc, wo=wo):
                return e.dma_start(
                    out=A(C, wo, 2048),
                    in_=_dap(C.wB["wk"], oc * 262144, [[2048, 128], [1, 2048]]))
            P.add("sync", em_wk, reads=[C.RwB["wk"]], writes=BS(C, wo, 2048),
                  ndma=1, tag=f"1wk{sl}_{oc}")

            bank = oc % 4

            def em_mm(e, wo=wo, bank=bank, lnoff=lnoff):
                out = []
                for k in range(16):
                    out.append(e.matmul(C.pb[bank][:, :],
                                        A(C, wo + 128 * k, 128),
                                        A(C, lnoff + 512 * k, 512),
                                        start=(k == 0), stop=(k == 15)))
                return out
            P.add("tensor", em_mm,
                  reads=BS(C, wo, 2048) + BS(C, lnoff, 8192),
                  writes=[C.Rpb[bank]], tag=f"1km{sl}_{oc}")

            eo = O_KEV + 512 * (oc % 2)

            def em_ev(e, bank=bank, eo=eo):
                return e.tensor_copy(A(C, eo, 512), C.pb[bank][:, :])
            P.add("vector", em_ev, reads=[C.Rpb[bank]], writes=BS(C, eo, 512),
                  tag=f"1ke{sl}_{oc}")

            def em_st(e, oc=oc, eo=eo, sl=sl):
                return e.dma_start(
                    out=_dap(C.Kscr, oc * 128 * 4096 + 512 * sl,
                             [[4096, 128], [1, 512]]),
                    in_=A(C, eo, 512))
            P.add("gpsimd", em_st, reads=BS(C, eo, 512), writes=[C.RK],
                  ndma=1, tag=f"1ks{sl}_{oc}")

        for nt in range(4):
            for k in range(16):
                def em_wv(e, nt=nt, k=k):
                    return e.dma_start(
                        out=A(C, O_WV + 512 * k, 512),
                        in_=_dap(C.wB["wv"], k * 262144 + 512 * nt,
                                 [[2048, 128], [1, 512]]))
                P.add("sync", em_wv, reads=[C.RwB["wv"]],
                      writes=BS(C, O_WV + 512 * k, 512), ndma=1,
                      tag=f"1wv{sl}_{nt}_{k}")
            for blk in range(4):
                bank = 4 + blk % 2

                def em_vmm(e, blk=blk, bank=bank, lnoff=lnoff):
                    out = []
                    for k in range(16):
                        out.append(e.matmul(C.pb[bank][:, :],
                                            A(C, lnoff + 512 * k + 128 * blk, 128),
                                            A(C, O_WV + 512 * k, 512),
                                            start=(k == 0), stop=(k == 15)))
                    return out
                P.add("tensor", em_vmm,
                      reads=BS(C, O_WV, 8192) + BS(C, lnoff, 8192),
                      writes=[C.Rpb[bank]], tag=f"1vm{sl}_{nt}_{blk}")

                eo = O_VEV + 512 * (blk % 2)

                def em_vev(e, bank=bank, eo=eo):
                    return e.tensor_copy(A(C, eo, 512), C.pb[bank][:, :])
                P.add("vector", em_vev, reads=[C.Rpb[bank]],
                      writes=BS(C, eo, 512), tag=f"1ve{sl}_{nt}_{blk}")

                def em_vst(e, nt=nt, blk=blk, eo=eo, sl=sl):
                    off = (4 * nt) * 4096 * 128 + (512 * sl + 128 * blk) * 128
                    return e.dma_start(
                        out=_dap(C.Vscr, off,
                                 [[128, 128], [4096 * 128, 4], [1, 128]]),
                        in_=A(C, eo, 512))
                P.add("gpsimd", em_vst, reads=BS(C, eo, 512), writes=[C.RV],
                      ndma=1, tag=f"1vs{sl}_{nt}_{blk}")



def emit_P2(C):
    P, di = C.P, C.di
    O_MASK, O_LNX, O_QH, O_KB, O_VB, O_EXP, O_ATT, O_WQ, O_WO, O_TMP, \
        O_OUN, O_RBC, O_YT = (0, 4096, 12288, 13312, 21504, 29696, 31744,
                              39936, 41984, 44032, 46080, 47104, 48128)
    rden = C.af32[0:1, 4096:4608]
    rden_bf = C.rows_bf[0:1, 1024:1536]
    SB = P2_SB
    LAG = P2_LAG

    for t in range(4):
        PT = 8 * (t + 1)

        def em_mask(e, t=t):
            return e.dma_start(
                out=A(C, O_MASK, 4096),
                in_=_dap(di["masks"], t * 8 * 128 * 512,
                         [[512, 128], [65536, 8], [1, 512]]))
        P.add("sync", em_mask, writes=BS(C, O_MASK, 4096), ndma=1,
              tag=f"2mk{t}")

        emit_ln(C, lambda c, t=t: C.x_res[:, c, 512 * t:512 * t + 512],
                [C.Rxres[t]], O_LNX, O_TMP, (6, 7), f"2n{t}")

        for h in range(16):
            wqo = O_WQ + 2048 * (h % 2)

            def em_wq(e, h=h, wqo=wqo):
                return e.dma_start(
                    out=A(C, wqo, 2048),
                    in_=_dap(C.wB["wq"], h * 262144, [[2048, 128], [1, 2048]]))
            P.add("sync", em_wq, reads=[C.RwB["wq"]],
                  writes=BS(C, wqo, 2048), ndma=1, tag=f"2wq{t}_{h}")

            qb = h % 2

            def em_qmm(e, qb=qb, wqo=wqo):
                out = []
                for k in range(16):
                    out.append(e.matmul(C.pb[qb][:, :],
                                        A(C, wqo + 128 * k, 128),
                                        A(C, O_LNX + 512 * k, 512),
                                        start=(k == 0), stop=(k == 15)))
                return out
            P.add("tensor", em_qmm,
                  reads=BS(C, wqo, 2048) + BS(C, O_LNX, 8192),
                  writes=[C.Rpb[qb]], tag=f"2qm{t}_{h}")

            qo = O_QH + 512 * (h % 2)

            def em_qev(e, qb=qb, qo=qo):
                return e.tensor_copy(A(C, qo, 512), C.pb[qb][:, :])
            P.add("vector", em_qev, reads=[C.Rpb[qb]], writes=BS(C, qo, 512),
                  tag=f"2qe{t}_{h}")

            feed_casts(C, 3)
            ko = O_KB + 4096 * (h % 2)
            vo = O_VB + 4096 * (h % 2)

            def em_k(e, h=h, ko=ko, PT=PT):
                return e.dma_start(
                    out=A(C, ko, 128 * PT),
                    in_=_dap(C.Kscr, h * 128 * 4096, [[4096, 128], [1, 128 * PT]]))
            P.add("sync", em_k, reads=[C.RK], writes=BS(C, ko, 4096), ndma=1,
                  tag=f"2kl{t}_{h}")

            def em_v(e, h=h, vo=vo, PT=PT):
                return e.dma_start(
                    out=A(C, vo, 128 * PT),
                    in_=_dap(C.Vscr, h * 4096 * 128,
                             [[128, 128], [128 * 128, PT], [1, 128]]))
            P.add("sync", em_v, reads=[C.RV], writes=BS(C, vo, 4096), ndma=1,
                  tag=f"2vl{t}_{h}")

            tail = []

            def emit_dp(kc):
                eo, erd = tail[kc]

                def em_den(e, kc=kc, eo=eo, PT=PT):
                    return e.matmul(C.pb[4][0:1, :], C.ones_col[:, :],
                                    A(C, eo, 512),
                                    start=(kc == 0), stop=(kc == PT - 1))
                P.add("tensor", em_den, reads=erd + [C.Rconst],
                      writes=[C.Rpb[4]], tag=f"2d{t}_{h}_{kc}")

                def em_pv(e, kc=kc, eo=eo, vo=vo, PT=PT):
                    return e.matmul(C.pb[5][:, :], A(C, vo + 128 * kc, 128),
                                    A(C, eo, 512),
                                    start=(kc == 0), stop=(kc == PT - 1))
                P.add("tensor", em_pv, reads=erd + BS(C, vo + 128 * kc, 128),
                      writes=[C.Rpb[5]], tag=f"2p{t}_{h}_{kc}")

            for kc in range(PT):
                sb = SB[kc % len(SB)]

                def em_s(e, kc=kc, sb=sb, ko=ko, qo=qo):
                    return e.matmul(C.pb[sb][:, :], A(C, ko + 128 * kc, 128),
                                    A(C, qo, 512), start=True, stop=True)
                P.add("tensor", em_s,
                      reads=BS(C, ko + 128 * kc, 128) + BS(C, qo, 512),
                      writes=[C.Rpb[sb]], tag=f"2s{t}_{h}_{kc}")

                eo = O_EXP + 512 * (kc % len(SB))

                def em_exp(e, sb=sb, eo=eo):
                    return e.activation(A(C, eo, 512), C.pb[sb][:, :], AF.Exp,
                                        bias=EXP_SHIFT, scale=INV_SQRT_HD)
                P.add("scalar", em_exp, reads=[C.Rpb[sb]],
                      writes=BS(C, eo, 512), tag=f"2e{t}_{h}_{kc}")

                if kc >= 8 * t:
                    j = kc - 8 * t
                    mo = O_TMP + 512 * (kc % len(SB))

                    def em_msk(e, j=j, eo=eo, mo=mo):
                        return e.tensor_tensor(
                            A(C, mo, 512), A(C, eo, 512),
                            C.abf[:, O_MASK + 512 * j:O_MASK + 512 * j + 512],
                            OP.mult)
                    P.add("vector", em_msk,
                          reads=BS(C, eo, 512) + BS(C, O_MASK, 4096),
                          writes=BS(C, mo, 512), tag=f"2x{t}_{h}_{kc}")
                    tail.append((mo, BS(C, mo, 512)))
                else:
                    tail.append((eo, BS(C, eo, 512)))

                if kc >= LAG:
                    emit_dp(kc - LAG)
            for kc in range(max(0, PT - LAG), PT):
                emit_dp(kc)

            P.add("vector", lambda e: e.reciprocal(rden, C.pb[4][0:1, :]),
                  reads=[C.Rpb[4]], writes=[C.Rden] + FS(C, 4096, 512),
                  tag=f"2r{t}_{h}")
            P.add("vector", lambda e: e.tensor_copy(rden_bf, rden),
                  reads=[C.Rden], writes=[C.Rden], tag=f"2rc{t}_{h}")
            P.add("tensor", lambda e, qb=qb: e.matmul(C.pb[qb][:, :],
                                                      C.ones_row[:, :],
                                                      rden_bf, start=True,
                                                      stop=True),
                  reads=[C.Rden, C.Rconst], writes=[C.Rpb[qb]], tag=f"2b{t}_{h}")

            ro = O_RBC + 512 * (h % 2)
            oo = O_OUN + 512 * (h % 2)
            ao = O_ATT + 512 * h

            def em_rbc(e, ro=ro, qb=qb):
                return e.tensor_copy(A(C, ro, 512), C.pb[qb][:, :])
            P.add("vector", em_rbc, reads=[C.Rpb[qb]], writes=BS(C, ro, 512),
                  tag=f"2rb{t}_{h}")

            def em_oun(e, oo=oo):
                return e.tensor_copy(A(C, oo, 512), C.pb[5][:, :])
            P.add("vector", em_oun, reads=[C.Rpb[5]], writes=BS(C, oo, 512),
                  tag=f"2ou{t}_{h}")

            def em_att(e, ao=ao, oo=oo, ro=ro):
                return e.tensor_tensor(A(C, ao, 512), A(C, oo, 512),
                                       A(C, ro, 512), OP.mult)
            P.add("vector", em_att, reads=BS(C, oo, 512) + BS(C, ro, 512),
                  writes=BS(C, ao, 512), tag=f"2a{t}_{h}")

        for oc in range(16):
            woo = O_WO if oc % 2 == 0 else O_MASK

            def em_wo(e, oc=oc, woo=woo):
                return e.dma_start(
                    out=A(C, woo, 2048),
                    in_=_dap(C.wB["wo"], oc * 262144, [[2048, 128], [1, 2048]]))
            P.add("sync", em_wo, reads=[C.RwB["wo"]],
                  writes=BS(C, woo, 2048), ndma=1, tag=f"2wo{t}_{oc}")

            ob = oc % 2

            def em_omm(e, ob=ob, woo=woo):
                out = []
                for k in range(16):
                    out.append(e.matmul(C.pb[ob][:, :],
                                        A(C, woo + 128 * k, 128),
                                        A(C, O_ATT + 512 * k, 512),
                                        start=(k == 0), stop=(k == 15)))
                return out
            P.add("tensor", em_omm,
                  reads=BS(C, woo, 2048) + BS(C, O_ATT, 8192),
                  writes=[C.Rpb[ob]], tag=f"2om{t}_{oc}")

            yo = O_YT + 512 * (oc % 2)

            def em_yev(e, ob=ob, yo=yo):
                return e.tensor_copy(A(C, yo, 512), C.pb[ob][:, :])
            P.add("vector", em_yev, reads=[C.Rpb[ob]], writes=BS(C, yo, 512),
                  tag=f"2ye{t}_{oc}")

            def em_res(e, oc=oc, yo=yo, t=t):
                xs = C.x_res[:, oc, 512 * t:512 * t + 512]
                return e.tensor_tensor(xs, A(C, yo, 512), xs, OP.add)
            P.add("vector", em_res, reads=BS(C, yo, 512),
                  writes=[C.Rxres[t]], tag=f"2rs{t}_{oc}")



def emit_ffn(C, ph):
    P = C.P
    O_LNX, O_WG, O_WU, O_HSP, O_TMP, O_YT = 0, 8192, 12288, 14336, 47104, 49152
    O_WD = 0

    for tt in range(4):
        emit_ln(C, lambda c, tt=tt: C.x_res[:, c, 512 * tt:512 * tt + 512],
                [C.Rxres[tt]], O_LNX, O_TMP, (6, 7), f"{ph}n{tt}")

        for gc in range(16):
            for s in range(4):
                wgo = O_WG + 2048 * (s % 2)

                def em_wg(e, gc=gc, s=s, wgo=wgo):
                    return e.dma_start(
                        out=A(C, wgo, 2048),
                        in_=_dap(C.wB["wg"], (gc * 4 + s) * 262144,
                                 [[2048, 128], [1, 2048]]))
                P.add("sync", em_wg, reads=[C.RwB["wg"]],
                      writes=BS(C, wgo, 2048), ndma=1, tag=f"{ph}wg{tt}_{gc}_{s}")

                def em_wu(e, gc=gc, s=s):
                    return e.dma_start(
                        out=A(C, O_WU, 2048),
                        in_=_dap(C.wB["wu"], (gc * 4 + s) * 262144,
                                 [[2048, 128], [1, 2048]]))
                P.add("sync", em_wu, reads=[C.RwB["wu"]],
                      writes=BS(C, O_WU, 2048), ndma=1, tag=f"{ph}wu{tt}_{gc}_{s}")

                gb = s % 2
                ub = 2 + s % 2

                def em_gmm(e, wgo=wgo, gb=gb):
                    out = []
                    for k in range(16):
                        out.append(e.matmul(C.pb[gb][:, :],
                                            A(C, wgo + 128 * k, 128),
                                            A(C, O_LNX + 512 * k, 512),
                                            start=(k == 0), stop=(k == 15)))
                    return out
                P.add("tensor", em_gmm,
                      reads=BS(C, wgo, 2048) + BS(C, O_LNX, 8192),
                      writes=[C.Rpb[gb]], tag=f"{ph}gm{tt}_{gc}_{s}")

                def em_umm(e, ub=ub):
                    out = []
                    for k in range(16):
                        out.append(e.matmul(C.pb[ub][:, :],
                                            A(C, O_WU + 128 * k, 128),
                                            A(C, O_LNX + 512 * k, 512),
                                            start=(k == 0), stop=(k == 15)))
                    return out
                P.add("tensor", em_umm,
                      reads=BS(C, O_WU, 2048) + BS(C, O_LNX, 8192),
                      writes=[C.Rpb[ub]], tag=f"{ph}um{tt}_{gc}_{s}")

                so = 512 * (s % 2)

                def em_sil(e, gb=gb, so=so):
                    return e.activation(AF_(C, so, 512), C.pb[gb][:, :], AF.Silu)
                P.add("scalar", em_sil, reads=[C.Rpb[gb]],
                      writes=FS(C, so, 512), tag=f"{ph}si{tt}_{gc}_{s}")

                ho = 1024 + 512 * s

                def em_h(e, ub=ub, so=so, ho=ho):
                    return e.tensor_tensor(AF_(C, ho, 512), AF_(C, so, 512),
                                           C.pb[ub][:, :], OP.mult)
                P.add("vector", em_h, reads=FS(C, so, 512) + [C.Rpb[ub]],
                      writes=FS(C, ho, 512), tag=f"{ph}h{tt}_{gc}_{s}")

            for s in range(4):
                def em_abs(e, s=s):
                    return e.tensor_tensor(AF_(C, 3072 + 512 * s, 512),
                                           AF_(C, 1024 + 512 * s, 512),
                                           AF_(C, 1024 + 512 * s, 512),
                                           OP.mult)
                P.add("vector", em_abs, reads=FS(C, 1024 + 512 * s, 512),
                      writes=FS(C, 3072 + 512 * s, 512), tag=f"{ph}ab{tt}_{gc}_{s}")
            QO = 5120
            spec = [(QO, 3072, 3584, OP.max), (QO + 512, 3072, 3584, OP.min),
                    (QO + 1024, 4096, 4608, OP.max),
                    (QO + 1536, 4096, 4608, OP.min)]
            for do, o1, o2, op in spec:
                def em_p(e, do=do, o1=o1, o2=o2, op=op):
                    return e.tensor_tensor(AF_(C, do, 512), AF_(C, o1, 512),
                                           AF_(C, o2, 512), op)
                P.add("vector", em_p,
                      reads=FS(C, o1, 512) + FS(C, o2, 512),
                      writes=FS(C, do, 512), tag=f"{ph}mx{tt}_{gc}_{do}")
            P.add("vector", lambda e: e.tensor_tensor(
                AF_(C, QO + 2048, 512), AF_(C, QO, 512), AF_(C, QO + 1024, 512),
                OP.min),
                reads=FS(C, QO, 512) + FS(C, QO + 1024, 512),
                writes=FS(C, QO + 2048, 512), tag=f"{ph}t1{tt}_{gc}")
            P.add("vector", lambda e: e.tensor_tensor(
                AF_(C, QO, 512), AF_(C, QO + 512, 512), AF_(C, QO + 1536, 512),
                OP.max),
                reads=FS(C, QO + 512, 512) + FS(C, QO + 1536, 512),
                writes=FS(C, QO, 512), tag=f"{ph}t2{tt}_{gc}")
            P.add("vector", lambda e: e.tensor_tensor(
                AF_(C, QO + 2560, 512), AF_(C, QO + 2048, 512), AF_(C, QO, 512),
                OP.max),
                reads=FS(C, QO + 2048, 512) + FS(C, QO, 512),
                writes=FS(C, QO + 2560, 512), tag=f"{ph}m2{tt}_{gc}")
            for s in range(4):
                def em_ge(e, s=s):
                    return e.tensor_tensor(AF_(C, QO + 512, 512),
                                           AF_(C, 3072 + 512 * s, 512),
                                           AF_(C, QO + 2560, 512), OP.is_ge)
                P.add("vector", em_ge,
                      reads=FS(C, 3072 + 512 * s, 512) + FS(C, QO + 2560, 512),
                      writes=FS(C, QO + 512, 512), tag=f"{ph}ge{tt}_{gc}_{s}")

                hso = O_HSP + 512 * (4 * gc + s)

                if SPARSIFY:
                    def em_ap(e, s=s, hso=hso):
                        return e.tensor_tensor(A(C, hso, 512),
                                               AF_(C, 1024 + 512 * s, 512),
                                               AF_(C, QO + 512, 512), OP.mult)
                    P.add("vector", em_ap,
                          reads=FS(C, 1024 + 512 * s, 512) + FS(C, QO + 512, 512),
                          writes=BS(C, hso, 512), tag=f"{ph}sp{tt}_{gc}_{s}")
                else:
                    def em_ap(e, s=s, hso=hso):
                        return e.tensor_copy(A(C, hso, 512),
                                             AF_(C, 1024 + 512 * s, 512))
                    P.add("vector", em_ap,
                          reads=FS(C, 1024 + 512 * s, 512),
                          writes=BS(C, hso, 512), tag=f"{ph}sp{tt}_{gc}_{s}")

        for m in range(16):
            db = 4 + m % 2
            for half in range(2):
                wdo = O_WD + 4096 * ((2 * m + half) % 3)

                def em_wd(e, m=m, half=half, wdo=wdo):
                    return e.dma_start(
                        out=A(C, wdo, 4096),
                        in_=_dap(C.wB["wd"], m * 1048576 + 4096 * half,
                                 [[8192, 128], [1, 4096]]))
                P.add("sync", em_wd, reads=[C.RwB["wd"]],
                      writes=BS(C, wdo, 4096), ndma=1,
                      tag=f"{ph}wd{tt}_{m}_{half}")

                def em_dmm(e, wdo=wdo, db=db, half=half):
                    out = []
                    for i in range(32):
                        out.append(e.matmul(
                            C.pb[db][:, :],
                            A(C, wdo + 128 * i, 128),
                            A(C, O_HSP + 512 * (32 * half + i), 512),
                            start=(half == 0 and i == 0),
                            stop=(half == 1 and i == 31)))
                    return out
                P.add("tensor", em_dmm,
                      reads=BS(C, wdo, 4096) + BS(C, O_HSP + 16384 * half, 16384),
                      writes=[C.Rpb[db]], tag=f"{ph}dm{tt}_{m}_{half}")

            yo = O_YT + 512 * (m % 2)

            def em_ye(e, db=db, yo=yo):
                return e.tensor_copy(A(C, yo, 512), C.pb[db][:, :])
            P.add("vector", em_ye, reads=[C.Rpb[db]], writes=BS(C, yo, 512),
                  tag=f"{ph}ye{tt}_{m}")

            def em_res(e, m=m, yo=yo, tt=tt):
                xs = C.x_res[:, m, 512 * tt:512 * tt + 512]
                return e.tensor_tensor(xs, A(C, yo, 512), xs, OP.add)
            P.add("vector", em_res, reads=BS(C, yo, 512),
                  writes=[C.Rxres[tt]], tag=f"{ph}rs{tt}_{m}")



def emit_P4(C):
    P, di = C.P, C.di
    O_LNX, O_KT, O_VT, O_WM, O_TMP = 0, 8192, 16384, 24576, 26624

    P.add("vector", lambda e: e.memset(AF_(C, 0, 2048), 0.0),
          writes=FS(C, 0, 2048), tag="4bd0")
    P.add("vector", lambda e: e.memset(AF_(C, 2048, 16), 0.0),
          writes=FS(C, 2048, 512), tag="4ks0")

    for tt in range(4):
        emit_ln(C, lambda c, tt=tt: C.x_res[:, c, 512 * tt:512 * tt + 512],
                [C.Rxres[tt]], O_LNX, O_TMP, (6, 7), f"4n{tt}")

        for nt in range(4):
            for k in range(16):
                wko = O_WM + 512 * (k % 2)
                wvo = O_WM + 1024 + 512 * (k % 2)

                def em_wk(e, k=k, nt=nt, wko=wko):
                    return e.dma_start(
                        out=A(C, wko, 512),
                        in_=_dap(C.wB["wkb"], k * 262144 + 512 * nt,
                                 [[2048, 128], [1, 512]]))
                P.add("sync", em_wk, reads=[C.RwB["wkb"]],
                      writes=BS(C, wko, 512), ndma=1, tag=f"4wk{tt}_{nt}_{k}")

                def em_wv(e, k=k, nt=nt, wvo=wvo):
                    return e.dma_start(
                        out=A(C, wvo, 512),
                        in_=_dap(C.wB["wvb"], k * 262144 + 512 * nt,
                                 [[2048, 128], [1, 512]]))
                P.add("sync", em_wv, reads=[C.RwB["wvb"]],
                      writes=BS(C, wvo, 512), ndma=1, tag=f"4wv{tt}_{nt}_{k}")

                def em_kvmm(e, k=k, wko=wko, wvo=wvo):
                    out = []
                    for blk in range(4):
                        lhs = A(C, O_LNX + 512 * k + 128 * blk, 128)
                        out.append(e.matmul(C.pb[blk][:, :], lhs, A(C, wko, 512),
                                            start=(k == 0), stop=(k == 15)))
                        out.append(e.matmul(C.pb[4 + blk][:, :], lhs, A(C, wvo, 512),
                                            start=(k == 0), stop=(k == 15)))
                    return out
                P.add("tensor", em_kvmm,
                      reads=BS(C, wko, 512) + BS(C, wvo, 512) + BS(C, O_LNX, 8192),
                      writes=[C.Rpb[i] for i in range(8)],
                      tag=f"4kv{tt}_{nt}_{k}")

            for blk in range(4):
                kto = O_KT + 2048 * blk + 512 * nt
                vto = O_VT + 2048 * blk + 512 * nt

                def em_kev(e, blk=blk, kto=kto):
                    return e.tensor_relu(A(C, kto, 512), C.pb[blk][:, :])
                P.add("vector", em_kev, reads=[C.Rpb[blk]],
                      writes=BS(C, kto, 512), tag=f"4ke{tt}_{nt}_{blk}")

                def em_vev(e, blk=blk, vto=vto):
                    return e.tensor_copy(A(C, vto, 512), C.pb[4 + blk][:, :])
                P.add("vector", em_vev, reads=[C.Rpb[4 + blk]],
                      writes=BS(C, vto, 512), tag=f"4ve{tt}_{nt}_{blk}")

        for grp in range(16):
            vb = grp % 2

            def em_vk(e, grp=grp, vb=vb):
                out = []
                for blk in range(4):
                    out.append(e.matmul(
                        C.pb[vb][:, 0:128],
                        A(C, O_KT + 2048 * blk + 128 * grp, 128),
                        A(C, O_VT + 2048 * blk + 128 * grp, 128),
                        start=(blk == 0), stop=(blk == 3)))
                return out
            P.add("tensor", em_vk,
                  reads=BS(C, O_KT, 8192) + BS(C, O_VT, 8192),
                  writes=[C.Rpb[vb]], tag=f"4vk{tt}_{grp}")

            def em_vka(e, grp=grp, vb=vb):
                dst = AF_(C, 128 * grp, 128)
                return e.tensor_tensor(dst, C.pb[vb][:, 0:128], dst, OP.add)
            P.add("vector", em_vka, reads=[C.Rpb[vb]] + FS(C, 128 * grp, 128),
                  writes=FS(C, 128 * grp, 128), tag=f"4va{tt}_{grp}")

            def em_ks(e, grp=grp):
                out = []
                for blk in range(4):
                    out.append(e.matmul(
                        C.pb[2][:, grp:grp + 1],
                        A(C, O_KT + 2048 * blk + 128 * grp, 128),
                        C.ones_col[:, :],
                        start=(blk == 0), stop=(blk == 3)))
                return out
            P.add("tensor", em_ks, reads=BS(C, O_KT, 8192) + [C.Rconst],
                  writes=[C.Rpb[2]], tag=f"4ks{tt}_{grp}")

        def em_ksa(e):
            dst = AF_(C, 2048, 16)
            return e.tensor_tensor(dst, C.pb[2][:, 0:16], dst, OP.add)
        P.add("vector", em_ksa, reads=[C.Rpb[2]] + FS(C, 2048, 512),
              writes=FS(C, 2048, 512), tag=f"4ka{tt}")

    for grp in range(16):
        def em_m(e, grp=grp):
            dst = AF_(C, 128 * grp, 128)
            return e.tensor_tensor(dst, dst, C.bdf_sb[:, :], OP.mult)
        P.add("vector", em_m, reads=[C.Rconst] + FS(C, 128 * grp, 128),
              writes=FS(C, 128 * grp, 128), tag=f"4bm{grp}")

    def em_ex(g):
        with (
            g.register("p") as p,
            g.register("off") as off,
            g.register("cnt") as cnt,
            g.register("cond") as cond,
            g.register("fl") as fl,
        ):
            g.reg_load(p, C.par_sb[:1, :1])
            g.reg_mov(off, 0)
            g.reg_add(off, off, p)
            g.reg_mul(off, off, 128 * 2064)
            g.dma_start(bass.AP(C.sh_ex, off, [[2064, 128], [1, 2048]]),
                        AF_(C, 0, 2048)).then_inc(C.s_poll, 16)
            g.reg_add(off, off, 2048)
            g.dma_start(bass.AP(C.sh_ex, off, [[2064, 128], [1, 16]]),
                        AF_(C, 2048, 16)).then_inc(C.s_poll, 16)
            g.wait_ge(C.s_poll, 32)
            g.reg_mov(off, 0)
            g.reg_add(off, off, p)
            g.reg_mul(off, off, 16)
            g.dma_start(bass.AP(C.sh_fl, off, [[16, 1], [1, 16]]),
                        C.flag_sb[:, :]).then_inc(C.s_poll, 16)
            g.wait_ge(C.s_poll, 48)
            g.reg_mov(off, 1)
            g.reg_sub(off, off, p)
            g.reg_mul(off, off, 16)
            g.reg_mov(cnt, 48)
            g.reg_mov(cond, 1)
            with g.While(cond):
                g.reg_add(cnt, cnt, 16)
                g.dma_start(C.pfl_sb[:, :],
                            bass.AP(C.sh_fl, off, [[16, 1], [1, 16]])
                            ).then_inc(C.s_poll, 16)
                g.wait_ge(C.s_poll, cnt)
                g.reg_load(fl, C.pfl_sb[:1, 0:1])
                g.reg_mov(cond, fl)
                g.reg_sub(cond, cond, MAGIC1)
            g.reg_mov(cond, 1)
            with g.While(cond):
                g.reg_add(cnt, cnt, 16)
                g.dma_start(C.pfl_sb[:, :],
                            bass.AP(C.sh_fl, off, [[16, 1], [1, 16]])
                            ).then_inc(C.s_poll, 16)
                g.wait_ge(C.s_poll, cnt)
                g.reg_load(fl, C.pfl_sb[:1, 1:2])
                g.reg_mov(cond, fl)
                g.reg_sub(cond, cond, MAGIC2)
            g.reg_mov(off, 1)
            g.reg_sub(off, off, p)
            g.reg_mul(off, off, 128 * 2064)
            r1 = g.dma_start(AF_(C, 2560, 2048),
                             bass.AP(C.sh_ex, off, [[2064, 128], [1, 2048]]))
            g.reg_add(off, off, 2048)
            r2 = g.dma_start(AF_(C, 4608, 16),
                             bass.AP(C.sh_ex, off, [[2064, 128], [1, 16]]))
            return [r1, r2]
    P.add("gpsimd", em_ex,
          reads=FS(C, 0, 2560) + [C.Rconst],
          writes=FS(C, 2560, 2560), ndma=2, tag="4ex")

    O_BDT, O_KST, O_LNX2, O_QT, O_B, O_ATT2, O_WQ2, O_WP2, O_YT2, O_TMP2 = \
        0, 2048, 2560, 10752, 11776, 12800, 20992, 23040, 25088, 26112

    P.add("vector", lambda e: e.tensor_tensor(A(C, O_BDT, 2048),
                                              AF_(C, 0, 2048),
                                              AF_(C, 2560, 2048), OP.add),
          reads=FS(C, 0, 2048) + FS(C, 2560, 2048),
          writes=BS(C, O_BDT, 2048), tag="4tot")
    P.add("vector", lambda e: e.tensor_tensor(
        AF_(C, 7168, 16),
        AF_(C, 2048, 16), AF_(C, 4608, 16), OP.add),
        reads=FS(C, 2048, 512) + FS(C, 4608, 512),
        writes=FS(C, 7168, 512), tag="4kt")

    for tt in range(4):
        emit_ln(C, lambda c, tt=tt: C.x_res[:, c, 512 * tt:512 * tt + 512],
                [C.Rxres[tt]], O_LNX2, O_TMP2, (6, 7), f"4m{tt}")

        for grp in range(16):
            wqo2 = O_WQ2 + 7168 * (grp % 2)

            def em_wq(e, grp=grp, wqo2=wqo2):
                return e.dma_start(
                    out=A(C, wqo2, 2048),
                    in_=_dap(C.wB["wqb"], grp * 262144, [[2048, 128], [1, 2048]]))
            P.add("sync", em_wq, reads=[C.RwB["wqb"]],
                  writes=BS(C, wqo2, 2048), ndma=1, tag=f"4wq{tt}_{grp}")

            qb = grp % 2

            def em_qmm(e, qb=qb, wqo2=wqo2):
                out = []
                for k in range(16):
                    out.append(e.matmul(C.pb[qb][:, :],
                                        A(C, wqo2 + 128 * k, 128),
                                        A(C, O_LNX2 + 512 * k, 512),
                                        start=(k == 0), stop=(k == 15)))
                return out
            P.add("tensor", em_qmm,
                  reads=BS(C, wqo2, 2048) + BS(C, O_LNX2, 8192),
                  writes=[C.Rpb[qb]], tag=f"4qm{tt}_{grp}")

            qo = O_QT + 512 * (grp % 2)

            def em_qev(e, qb=qb, qo=qo):
                return e.tensor_relu(A(C, qo, 512), C.pb[qb][:, :])
            P.add("vector", em_qev, reads=[C.Rpb[qb]], writes=BS(C, qo, 512),
                  tag=f"4qe{tt}_{grp}")

            bo = O_B + 512 * (grp % 2)

            def em_b(e, grp=grp, qo=qo, bo=bo):
                return e.tensor_scalar(
                    A(C, bo, 512), A(C, qo, 512),
                    C.af32[:, 7168 + grp:7168 + grp + 1],
                    None, OP.mult)
            P.add("vector", em_b,
                  reads=BS(C, qo, 512) + FS(C, 7168, 512),
                  writes=BS(C, bo, 512), tag=f"4b{tt}_{grp}")

            db = 2 + grp % 2

            def em_den(e, bo=bo, db=db):
                return e.matmul(C.pb[db][:, :], C.bdb_sb[:, :], A(C, bo, 512),
                                start=True, stop=True)
            P.add("tensor", em_den, reads=BS(C, bo, 512) + [C.Rconst],
                  writes=[C.Rpb[db]], tag=f"4dn{tt}_{grp}")

            ob = 4 + grp % 2

            def em_ov(e, grp=grp, qo=qo, ob=ob):
                return e.matmul(C.pb[ob][:, :], A(C, O_BDT + 128 * grp, 128),
                                A(C, qo, 512), start=True, stop=True)
            P.add("tensor", em_ov,
                  reads=BS(C, O_BDT + 128 * grp, 128) + BS(C, qo, 512),
                  writes=[C.Rpb[ob]], tag=f"4ov{tt}_{grp}")

            do = 5120 + 512 * (grp % 2)

            def em_eps(e, db=db, do=do):
                return e.activation(AF_(C, do, 512), C.pb[db][:, :],
                                    AF.Identity, bias=f32(1e-15))
            P.add("scalar", em_eps, reads=[C.Rpb[db]], writes=FS(C, do, 512),
                  tag=f"4ep{tt}_{grp}")

            ro = 6144 + 512 * (grp % 2)

            def em_rec(e, do=do, ro=ro):
                return e.reciprocal(AF_(C, ro, 512), AF_(C, do, 512))
            P.add("vector", em_rec, reads=FS(C, do, 512),
                  writes=FS(C, ro, 512), tag=f"4rc{tt}_{grp}")

            ao = O_ATT2 + 512 * grp

            def em_att(e, ob=ob, ro=ro, ao=ao):
                return e.tensor_tensor(A(C, ao, 512), C.pb[ob][:, :],
                                       AF_(C, ro, 512), OP.mult)
            P.add("vector", em_att, reads=[C.Rpb[ob]] + FS(C, ro, 512),
                  writes=BS(C, ao, 512), tag=f"4at{tt}_{grp}")

        for oc in range(16):
            wpo2 = O_WP2 + 7168 * (oc % 2)

            def em_wp(e, oc=oc, wpo2=wpo2):
                return e.dma_start(
                    out=A(C, wpo2, 2048),
                    in_=_dap(C.wB["wpb"], oc * 262144, [[2048, 128], [1, 2048]]))
            P.add("sync", em_wp, reads=[C.RwB["wpb"]],
                  writes=BS(C, wpo2, 2048), ndma=1, tag=f"4wp{tt}_{oc}")

            ob = oc % 2

            def em_pmm(e, ob=ob, wpo2=wpo2):
                out = []
                for k in range(16):
                    out.append(e.matmul(C.pb[ob][:, :],
                                        A(C, wpo2 + 128 * k, 128),
                                        A(C, O_ATT2 + 512 * k, 512),
                                        start=(k == 0), stop=(k == 15)))
                return out
            P.add("tensor", em_pmm,
                  reads=BS(C, wpo2, 2048) + BS(C, O_ATT2, 8192),
                  writes=[C.Rpb[ob]], tag=f"4pm{tt}_{oc}")

            yo = O_YT2 + 512 * (oc % 2)

            def em_yb(e, oc=oc, ob=ob, yo=yo):
                return e.tensor_scalar(A(C, yo, 512), C.pb[ob][:, :],
                                       C.biasb_sb[:, oc:oc + 1], None, OP.add)
            P.add("vector", em_yb, reads=[C.Rpb[ob], C.Rconst],
                  writes=BS(C, yo, 512), tag=f"4yb{tt}_{oc}")

            def em_res(e, oc=oc, yo=yo, tt=tt):
                xs = C.x_res[:, oc, 512 * tt:512 * tt + 512]
                return e.tensor_tensor(xs, A(C, yo, 512), xs, OP.add)
            P.add("vector", em_res, reads=BS(C, yo, 512),
                  writes=[C.Rxres[tt]], tag=f"4rs{tt}_{oc}")



def emit_dbg(C, what):
    P, di = C.P, C.di
    O_LNX, O_TMP = 0, 47104
    emit_ln(C, lambda c: C.x_res[:, c, 0:512], [C.Rxres[0]],
            O_LNX, O_TMP, (6, 7), "dbn")
    routs = [Res(f"dbg{i}") for i in range(16)]
    if what == "lnx":
        for c in range(16):
            def em(e, c=c):
                return e.dma_start(
                    out=_dap(di["y"], c * 128 * 2048, [[2048, 128], [1, 512]]),
                    in_=A(C, O_LNX + 512 * c, 512))
            P.add("gpsimd", em, reads=BS(C, O_LNX + 512 * c, 512),
                  writes=[routs[c]], ndma=1, tag=f"dbgo{c}")
        for r in range(4):
            def em2(e, r=r):
                return e.dma_start(
                    out=_dap(di["y"], r * 2048 + 512, [[2048, 1], [1, 512]]),
                    in_=C.rows[0:1, 512 * r:512 * r + 512])
            P.add("gpsimd", em2, reads=[C.Rrows], writes=[routs[4 + r]],
                  ndma=1, tag=f"dbgr{r}")
    else:
        for s in range(4):
            def em_wg(e, s=s):
                return e.dma_start(
                    out=A(C, 8192, 2048),
                    in_=_dap(C.wB["wg"], s * 262144, [[2048, 128], [1, 2048]]))
            P.add("sync", em_wg, reads=[C.RwB["wg"]], writes=BS(C, 8192, 2048),
                  ndma=1, tag=f"dbwg{s}")
            def em_wu(e, s=s):
                return e.dma_start(
                    out=A(C, 12288, 2048),
                    in_=_dap(C.wB["wu"], s * 262144, [[2048, 128], [1, 2048]]))
            P.add("sync", em_wu, reads=[C.RwB["wu"]], writes=BS(C, 12288, 2048),
                  ndma=1, tag=f"dbwu{s}")
            def em_gmm(e):
                out = []
                for k in range(16):
                    out.append(e.matmul(C.pb[0][:, :], A(C, 8192 + 128 * k, 128),
                                        A(C, O_LNX + 512 * k, 512),
                                        start=(k == 0), stop=(k == 15)))
                return out
            P.add("tensor", em_gmm, reads=BS(C, 8192, 2048) + BS(C, O_LNX, 8192),
                  writes=[C.Rpb[0]], tag=f"dbgm{s}")
            def em_umm(e):
                out = []
                for k in range(16):
                    out.append(e.matmul(C.pb[2][:, :], A(C, 12288 + 128 * k, 128),
                                        A(C, O_LNX + 512 * k, 512),
                                        start=(k == 0), stop=(k == 15)))
                return out
            P.add("tensor", em_umm, reads=BS(C, 12288, 2048) + BS(C, O_LNX, 8192),
                  writes=[C.Rpb[2]], tag=f"dbum{s}")
            def em_sil(e):
                return e.activation(AF_(C, 0, 512), C.pb[0][:, :], AF.Silu)
            P.add("scalar", em_sil, reads=[C.Rpb[0]], writes=FS(C, 0, 512),
                  tag=f"dbsi{s}")
            def em_h(e, s=s):
                return e.tensor_tensor(AF_(C, 1024 + 512 * s, 512), AF_(C, 0, 512),
                                       C.pb[2][:, :], OP.mult)
            P.add("vector", em_h, reads=FS(C, 0, 512) + [C.Rpb[2]],
                  writes=FS(C, 1024 + 512 * s, 512), tag=f"dbh{s}")
        for s in range(4):
            def em(e, s=s):
                return e.dma_start(
                    out=_dap(di["y"], s * 128 * 2048, [[2048, 128], [1, 512]]),
                    in_=AF_(C, 1024 + 512 * s, 512))
            P.add("gpsimd", em, reads=FS(C, 1024 + 512 * s, 512),
                  writes=[routs[s]], ndma=1, tag=f"dbgo{s}")
    P.add("gpsimd", None, reads=routs, tag="dbfin")



def emit_P6(C):
    P, di = C.P, C.di
    routs = [Res(f"out{c}") for c in range(16)]
    for c in range(16):
        def em(e, c=c):
            return e.dma_start(
                out=_dap(di["y"], c * 128 * 2048, [[2048, 128], [1, 2048]]),
                in_=C.x_res[:, c, :])
        P.add("gpsimd", em, reads=list(C.Rxres), writes=[routs[c]],
              ndma=1, tag=f"o{c}")
    P.add("gpsimd", None, reads=routs, tag="fin")



_NC_CACHE = {}


def kernel(**inputs):
    if "nc" not in _NC_CACHE:
        _NC_CACHE["nc"] = build_nc()
    nc = _NC_CACHE["nc"]
    in_maps = [_prep_core(inputs, core) for core in range(8)]
    res = run_bass_kernel_spmd(nc, in_maps, core_ids=list(range(8)))
    out = np.zeros((4, 4096, 2048), dtype=np.float32)
    for core in range(8):
        seq, par = core // 2, core % 2
        own = _own_blocks(par)
        own_tok = np.concatenate([np.arange(128 * g, 128 * g + 128) for g in own])
        y = np.asarray(res.results[core]["y"])
        out[seq][own_tok, :] = y.T
    return out


# version 07
# speedup vs baseline: 1.5030x; 1.0381x over previous
import numpy as np
import ml_dtypes
import concourse.bass as bass
import concourse.mybir as mybir
from concourse.bass_utils import run_bass_kernel_spmd
from concourse.alu_op_type import AluOpType as OP

F32 = mybir.dt.float32
BF16 = mybir.dt.bfloat16
I32 = mybir.dt.int32
AF = mybir.ActivationFunctionType
f32 = np.float32

ABF_N = 50176
AF32_N = 8192
SLOT = 512
MAGIC1 = 0x12AB34CD
MAGIC2 = 0x56EF78A1
MAGIC3 = 0x3C4D5E6F
MAGIC4 = 0x718293A4
INV_SQRT_HD = f32(1.0 / np.sqrt(128.0))
EXP_SHIFT = f32(-20.0)

PHASES = (1, 2, 3, 4, 5)
SPARSIFY = True
P2_SB = (2, 3, 6, 7)
P2_LAG = 3
DBG = None

ENGINES = ("sync", "gpsimd", "scalar", "vector", "tensor")

_seq = [0]


class Ev:
    __slots__ = ("eng", "val", "waited", "seq")

    def __init__(self, eng):
        self.eng = eng
        self.val = None
        self.waited = False
        _seq[0] += 1
        self.seq = _seq[0]


class Res:
    __slots__ = ("name", "w", "rs")

    def __init__(self, name=""):
        self.name = name
        self.w = None
        self.rs = []


class Step:
    __slots__ = ("eng", "emit", "deps", "ev", "ndma", "tag", "signal")

    def __init__(self, eng, emit, deps, ev, ndma, tag):
        self.eng = eng
        self.emit = emit
        self.deps = deps
        self.ev = ev
        self.ndma = ndma
        self.tag = tag
        self.signal = False


class Prog:
    def __init__(self):
        self.steps = {e: [] for e in ENGINES}

    def add(self, eng, emit, reads=(), writes=(), ndma=0, tag=""):
        ev = Ev(eng)
        deps = {}
        for r in reads:
            if r.w is not None:
                deps[r.w.seq] = r.w
        for r in writes:
            if r.w is not None:
                deps[r.w.seq] = r.w
            for e in r.rs:
                deps[e.seq] = e
        deps.pop(ev.seq, None)
        for r in reads:
            r.rs.append(ev)
        for r in writes:
            r.w = ev
            r.rs = []
        final = []
        for d in sorted(deps.values(), key=lambda d: d.seq):
            if d.eng == eng == "tensor":
                continue
            final.append(d)
            d.waited = True
        self.steps[eng].append(Step(eng, emit, final, ev, ndma, tag))
        return ev

    def plan(self):
        for eng in ENGINES:
            cnt = 0
            for st in self.steps[eng]:
                if st.ev.waited or st.ndma:
                    st.signal = True
                    cnt += 16 * st.ndma if st.ndma else 1
                    st.ev.val = cnt

    def emit_engine(self, eng_name, eng_obj, sems):
        water = {}
        for st in self.steps[eng_name]:
            need = {}
            for d in st.deps:
                assert d.val is not None, f"dep of {st.tag} not planned"
                need[d.eng] = max(need.get(d.eng, 0), d.val)
            for deng, val in sorted(need.items()):
                if water.get(deng, 0) >= val:
                    continue
                eng_obj.wait_ge(sems[deng], val)
                water[deng] = val
            out = st.emit(eng_obj) if st.emit is not None else None
            if st.signal:
                assert out is not None, f"step {st.tag} must emit to signal"
                instrs = out if isinstance(out, (list, tuple)) else [out]
                if st.ndma:
                    assert len(instrs) == st.ndma, \
                        f"{st.tag}: ndma={st.ndma} got {len(instrs)}"
                    for i in instrs:
                        i.then_inc(sems[eng_name], 16)
                else:
                    instrs[-1].then_inc(sems[eng_name], 1)



def _own_blocks(par):
    keep = (0, 3) if par == 0 else (1, 2)
    return [g for g in range(32) if g % 4 in keep]


def _prep_core(inputs, core):
    seq, par = core // 2, core % 2
    bf = ml_dtypes.bfloat16
    x = np.asarray(inputs["L0_attn_a_x"][seq])
    xT = np.ascontiguousarray(x.T)
    own = _own_blocks(par)
    own_tok = np.concatenate([np.arange(128 * g, 128 * g + 128) for g in own])
    x_own = np.ascontiguousarray(xT[:, own_tok])

    def lhs_stream(W):
        return np.ascontiguousarray(
            W.reshape(16, 128, 16, 128).transpose(0, 3, 2, 1))

    def mov_stream(W):
        return np.ascontiguousarray(W.T.reshape(16, 128, 2048))

    Wqkv = np.asarray(inputs["L0_attn_a_W_qkv"])
    Wq, Wk, Wv = Wqkv[0:2048], Wqkv[2048:4096], Wqkv[4096:6144]
    Wo = np.asarray(inputs["L0_attn_a_W_out"])
    Wqkvb = np.asarray(inputs["L0_attn_b_w_qkv"])
    Wqb, Wkb, Wvb = Wqkvb[0:2048], Wqkvb[2048:4096], Wqkvb[4096:6144]
    Wpb = np.asarray(inputs["L0_attn_b_w_proj"])
    bpb = np.asarray(inputs["L0_attn_b_b_proj"])
    Wg = np.asarray(inputs["L0_ffn_w_gate"])
    Wu = np.asarray(inputs["L0_ffn_w_up"])
    Wd = np.asarray(inputs["L0_ffn_w_down"])

    def gate_stream(W):
        return np.ascontiguousarray(
            W.reshape(16, 128, 16, 128, 4).transpose(2, 4, 1, 0, 3))

    wd_h = np.ascontiguousarray(
        Wd.reshape(16, 128, 4, 16, 128).transpose(3, 1, 0, 2, 4)
    ).reshape(16, 128, 64, 128)

    masks = np.zeros((4, 8, 128, 512), dtype=np.float32)
    for t in range(4):
        qg = [own[4 * t + i] for i in range(4)]
        qpos = np.concatenate([128 * g + np.arange(128) for g in qg])
        for j in range(8):
            kpos = 128 * (8 * t + j) + np.arange(128)
            masks[t, j] = (kpos[:, None] <= qpos[None, :]).astype(np.float32)

    bd = np.zeros((128, 128), dtype=np.float32)
    for i in range(4):
        bd[32 * i:32 * i + 32, 32 * i:32 * i + 32] = 1.0

    return {
        "x_own": x_own,
        "x_full": xT,
        "wq": lhs_stream(Wq), "wk": lhs_stream(Wk), "wo": lhs_stream(Wo),
        "wv": mov_stream(Wv),
        "wqb": lhs_stream(Wqb), "wpb": lhs_stream(Wpb),
        "wkb": mov_stream(Wkb), "wvb": mov_stream(Wvb),
        "wg": gate_stream(Wg), "wu": gate_stream(Wu), "wd": wd_h,
        "biasb": np.ascontiguousarray(bpb.reshape(16, 128).T),
        "masks": masks.astype(bf),
        "bdones_f": bd,
        "bdones_b": bd.astype(bf),
        "parity": np.array([[par]], dtype=np.int32),
    }



class Ctx:
    pass


def _dap(t, off, dims):
    return bass.AP(t, off, dims)


def build_nc(phases=PHASES):
    nc = bass.Bass("TRN2", target_bir_lowering=False)
    C = Ctx()
    C.nc = nc
    P = Prog()
    C.P = P

    di = {}
    di["x_own"] = nc.dram_tensor("x_own", [2048, 2048], F32, kind="ExternalInput")
    di["x_full"] = nc.dram_tensor("x_full", [2048, 4096], F32, kind="ExternalInput")
    for n in ("wq", "wk", "wo", "wqb", "wpb"):
        di[n] = nc.dram_tensor(n, [16, 128, 16, 128], F32, kind="ExternalInput")
    for n in ("wv", "wkb", "wvb"):
        di[n] = nc.dram_tensor(n, [16, 128, 2048], F32, kind="ExternalInput")
    di["wg"] = nc.dram_tensor("wg", [16, 4, 128, 16, 128], F32, kind="ExternalInput")
    di["wu"] = nc.dram_tensor("wu", [16, 4, 128, 16, 128], F32, kind="ExternalInput")
    di["wd"] = nc.dram_tensor("wd", [16, 128, 64, 128], F32, kind="ExternalInput")
    di["biasb"] = nc.dram_tensor("biasb", [128, 16], F32, kind="ExternalInput")
    di["masks"] = nc.dram_tensor("masks", [4, 8, 128, 512], BF16, kind="ExternalInput")
    di["bdones_f"] = nc.dram_tensor("bdones_f", [128, 128], F32, kind="ExternalInput")
    di["bdones_b"] = nc.dram_tensor("bdones_b", [128, 128], BF16, kind="ExternalInput")
    di["parity"] = nc.dram_tensor("parity", [1, 1], I32, kind="ExternalInput")
    di["y"] = nc.dram_tensor("y", [2048, 2048], F32, kind="ExternalOutput")
    C.di = di

    C.wsize = {
        "wq": 16 * 128 * 16 * 128, "wk": 16 * 128 * 16 * 128,
        "wo": 16 * 128 * 16 * 128, "wqb": 16 * 128 * 16 * 128,
        "wpb": 16 * 128 * 16 * 128,
        "wv": 16 * 128 * 2048, "wkb": 16 * 128 * 2048, "wvb": 16 * 128 * 2048,
        "wg": 16 * 4 * 128 * 16 * 128, "wu": 16 * 4 * 128 * 16 * 128,
        "wd": 16 * 128 * 64 * 128,
    }
    C.wB = {n: nc.dram_tensor(n + "_bf", [C.wsize[n] // 128, 128], BF16)
            for n in C.wsize}
    C.RwB = {n: Res("R_" + n) for n in C.wB}

    C.Kscr = nc.dram_tensor("Kscr", [16, 128, 4096], BF16)
    C.Vscr = nc.dram_tensor("Vscr", [16, 4096, 128], BF16)
    C.RK = Res("RK")
    C.RV = Res("RV")

    C.sh_ex = nc.dram_tensor("sh_ex", [2, 128, 2064], F32, addr_space="Shared")
    C.sh_fl = nc.dram_tensor("sh_fl", [2, 1, 16], I32, addr_space="Shared")

    from contextlib import ExitStack
    with ExitStack() as _st:
        en = _st.enter_context
        x_res = en(nc.sbuf_tensor("x_res", [128, 16, 2048], BF16))
        abf = en(nc.sbuf_tensor("abf", [128, ABF_N], BF16))
        af32 = en(nc.sbuf_tensor("af32", [128, AF32_N], F32))
        rows = en(nc.sbuf_tensor("rows", [1, 2048], F32))
        rows_bf = en(nc.sbuf_tensor("rows_bf", [1, 2048], BF16))
        ones_row = en(nc.sbuf_tensor("ones_row", [1, 128], BF16))
        ones_col = en(nc.sbuf_tensor("ones_col", [128, 1], BF16))
        bdb_sb = en(nc.sbuf_tensor("bdb_sb", [128, 128], BF16))
        bdf_sb = en(nc.sbuf_tensor("bdf_sb", [128, 128], F32))
        biasb_sb = en(nc.sbuf_tensor("biasb_sb", [128, 16], F32))
        par_sb = en(nc.sbuf_tensor("par_sb", [1, 1], I32))
        flag_sb = en(nc.sbuf_tensor("flag_sb", [1, 16], I32))
        pfl_sb = en(nc.sbuf_tensor("pfl_sb", [1, 16], I32))
        pbs = [en(nc.psum_tensor(f"pb{i}", [128, 512], F32)) for i in range(8)]
        s_sync = en(nc.semaphore("s_sync"))
        s_gp = en(nc.semaphore("s_gp"))
        s_sc = en(nc.semaphore("s_sc"))
        s_ve = en(nc.semaphore("s_ve"))
        s_pe = en(nc.semaphore("s_pe"))
        s_poll = en(nc.semaphore("s_poll"))
        s_bar = en(nc.semaphore("s_bar"))
        block = en(nc.Block())

        C.x_res = x_res
        C.abf = abf
        C.af32 = af32
        C.rows = rows
        C.rows_bf = rows_bf
        C.ones_row = ones_row
        C.ones_col = ones_col
        C.bdb_sb = bdb_sb
        C.bdf_sb = bdf_sb
        C.biasb_sb = biasb_sb
        C.par_sb = par_sb
        C.flag_sb = flag_sb
        C.pfl_sb = pfl_sb
        C.pb = pbs
        C.Rpb = [Res(f"pb{i}") for i in range(8)]
        C.Rxres = [Res(f"xres{t}") for t in range(4)]
        C.Rrows = Res("rows")
        C.Rden = Res("rden")
        C.Rconst = Res("const")
        C.s_poll = s_poll
        C.s_bar = s_bar
        C.sems = {"sync": s_sync, "gpsimd": s_gp, "scalar": s_sc,
                  "vector": s_ve, "tensor": s_pe}
        C.bslots = [Res(f"b{i}") for i in range(ABF_N // SLOT)]
        C.fslots = [Res(f"f{i}") for i in range(AF32_N // SLOT)]

        emit_P0(C)
        if DBG is not None:
            emit_dbg(C, DBG)
            emit_P6_skip = True
            phases = ()
        else:
            emit_P6_skip = False
        if 1 in phases:
            emit_P1(C)
        if 2 in phases:
            emit_P2(C)
        feed_casts(C, 10 ** 6)
        if 3 in phases:
            emit_ffn(C, "P3")
        if 4 in phases:
            emit_P4(C)
        if 5 in phases:
            emit_ffn(C, "P5")
        if not emit_P6_skip:
            emit_P6(C)

        P.plan()

        @block.sync
        def _(e):
            P.emit_engine("sync", e, C.sems)

        @block.gpsimd
        def _(e):
            P.emit_engine("gpsimd", e, C.sems)

        @block.scalar
        def _(e):
            P.emit_engine("scalar", e, C.sems)

        @block.vector
        def _(e):
            P.emit_engine("vector", e, C.sems)

        @block.tensor
        def _(e):
            P.emit_engine("tensor", e, C.sems)

    return nc



def A(C, off, n):
    assert off + n <= ABF_N, (off, n)
    return C.abf[:, off:off + n]


def AF_(C, off, n):
    assert off + n <= AF32_N, (off, n)
    return C.af32[:, off:off + n]


def BS(C, off, n):
    return C.bslots[off // SLOT:(off + n + SLOT - 1) // SLOT]


def FS(C, off, n):
    return C.fslots[off // SLOT:(off + n + SLOT - 1) // SLOT]



def emit_P0(C):
    P, di = C.P, C.di

    P.add("vector", lambda e: e.memset(C.ones_row[:, :], 1.0),
          writes=[C.Rconst], tag="c_or")
    P.add("vector", lambda e: e.memset(C.ones_col[:, :], 1.0),
          writes=[C.Rconst], tag="c_oc")
    P.add("vector", lambda e: e.memset(C.flag_sb[:, 0:1], MAGIC1),
          writes=[C.Rconst], tag="c_f1")
    P.add("vector", lambda e: e.memset(C.flag_sb[:, 1:2], MAGIC2),
          writes=[C.Rconst], tag="c_f2")
    P.add("vector", lambda e: e.memset(C.flag_sb[:, 2:3], MAGIC3),
          writes=[C.Rconst], tag="c_f3")
    P.add("vector", lambda e: e.memset(C.flag_sb[:, 3:4], MAGIC4),
          writes=[C.Rconst], tag="c_f4")
    P.add("sync", lambda e: [
        e.dma_start(out=C.bdb_sb[:, :], in_=_dap(di["bdones_b"], 0, [[128, 128], [1, 128]])),
        e.dma_start(out=C.bdf_sb[:, :], in_=_dap(di["bdones_f"], 0, [[128, 128], [1, 128]])),
        e.dma_start(out=C.biasb_sb[:, :], in_=_dap(di["biasb"], 0, [[16, 128], [1, 16]])),
        e.dma_start(out=C.par_sb[:, :], in_=_dap(di["parity"], 0, [[1, 1], [1, 1]])),
    ], writes=[C.Rconst], ndma=4, tag="c_ld")

    C.Rbar = Res("bar")

    def em_bar(g):
        with (
            g.register("bp") as bp,
            g.register("bwoff") as bwoff,
            g.register("broff") as broff,
            g.register("bcnt") as bcnt,
            g.register("bcond") as bcond,
            g.register("bfl") as bfl,
        ):
            g.reg_load(bp, C.par_sb[:1, :1])
            g.reg_mov(bwoff, 0)
            g.reg_add(bwoff, bwoff, bp)
            g.reg_mul(bwoff, bwoff, 16)
            g.reg_add(bwoff, bwoff, 2)
            g.reg_mov(broff, 1)
            g.reg_sub(broff, broff, bp)
            g.reg_mul(broff, broff, 16)
            g.reg_add(broff, broff, 2)
            g.reg_mov(bcnt, 0)
            g.reg_mov(bcond, 1)
            with g.While(bcond):
                g.reg_add(bcnt, bcnt, 32)
                g.dma_start(bass.AP(C.sh_fl, bwoff, [[16, 1], [1, 2]]),
                            C.flag_sb[:, 2:4]).then_inc(C.s_bar, 16)
                g.dma_start(C.pfl_sb[:, 0:2],
                            bass.AP(C.sh_fl, broff, [[16, 1], [1, 2]])
                            ).then_inc(C.s_bar, 16)
                g.wait_ge(C.s_bar, bcnt)
                g.reg_load(bfl, C.pfl_sb[:1, 0:1])
                g.reg_mov(bcond, bfl)
                g.reg_sub(bcond, bcond, MAGIC3)
            g.reg_mov(bcond, 1)
            with g.While(bcond):
                g.reg_add(bcnt, bcnt, 32)
                g.dma_start(bass.AP(C.sh_fl, bwoff, [[16, 1], [1, 2]]),
                            C.flag_sb[:, 2:4]).then_inc(C.s_bar, 16)
                g.dma_start(C.pfl_sb[:, 0:2],
                            bass.AP(C.sh_fl, broff, [[16, 1], [1, 2]])
                            ).then_inc(C.s_bar, 16)
                g.wait_ge(C.s_bar, bcnt)
                g.reg_load(bfl, C.pfl_sb[:1, 1:2])
                g.reg_mov(bcond, bfl)
                g.reg_sub(bcond, bcond, MAGIC4)
            g.reg_add(bcnt, bcnt, 16)
            g.dma_start(bass.AP(C.sh_fl, bwoff, [[16, 1], [1, 2]]),
                        C.flag_sb[:, 2:4]).then_inc(C.s_bar, 16)
            g.wait_ge(C.s_bar, bcnt)
            return g.dma_start(C.pfl_sb[:, 0:2],
                               bass.AP(C.sh_fl, broff, [[16, 1], [1, 2]]))
    P.add("gpsimd", em_bar, reads=[C.Rconst], writes=[C.Rbar], ndma=1,
          tag="bar0")

    for c in range(16):
        def em(e, c=c):
            return e.dma_start(
                out=C.x_res[:, c, :],
                in_=_dap(di["x_own"], c * 128 * 2048, [[2048, 128], [1, 2048]]))
        P.add("gpsimd", em, reads=[C.Rbar], writes=list(C.Rxres), ndma=1,
              tag=f"xi{c}")

    C._bounce_i = 0

    def cast_jobs(name, nchunks=None):
        cols = C.wsize[name] // 128
        ch = 2048
        return [(name, i, min(ch, cols - i), cols) for i in range(0, cols, ch)]

    def emit_cast(job):
        name, i, w, cols = job
        slot = C._bounce_i % 2
        C._bounce_i += 1
        boff = slot * 2048

        def eml(e, name=name, i=i, w=w, cols=cols, boff=boff):
            return e.dma_start(out=AF_(C, boff, w),
                               in_=_dap(di[name], i, [[cols, 128], [1, w]]))

        def ems(e, name=name, i=i, w=w, cols=cols, boff=boff):
            return e.dma_start(out=_dap(C.wB[name], i, [[cols, 128], [1, w]]),
                               in_=AF_(C, boff, w))
        P.add("sync", eml, reads=[C.Rbar], writes=FS(C, boff, w), ndma=1,
              tag=f"pl{name}{i}")
        P.add("gpsimd", ems, reads=FS(C, boff, w), writes=[C.RwB[name]],
              ndma=1, tag=f"ps{name}{i}")

    C.emit_cast = emit_cast
    for job in cast_jobs("wk") + cast_jobs("wv"):
        emit_cast(job)
    C.cast_queue = []
    for n2 in ["wq", "wo", "wg", "wu", "wd", "wqb", "wkb", "wvb", "wpb"]:
        C.cast_queue.extend(cast_jobs(n2))


def feed_casts(C, n):
    for _ in range(n):
        if not C.cast_queue:
            return
        C.emit_cast(C.cast_queue.pop(0))



def emit_ln(C, get_x, reads_x, lnx_off, tmp_off, banks, tag):
    P = C.P
    W = 512
    b0, b1 = banks
    x2o = [tmp_off, tmp_off + W]
    rbco, mbco = tmp_off + 2 * W, tmp_off + 3 * W

    def em_sum(e):
        out = []
        for c in range(16):
            out.append(e.matmul(C.pb[b0][0:1, 0:W], C.ones_col[:, :], get_x(c),
                                start=(c == 0), stop=(c == 15)))
        return out
    P.add("tensor", em_sum, reads=list(reads_x) + [C.Rconst],
          writes=[C.Rpb[b0]], tag=tag + "sum")

    for c in range(16):
        o = x2o[c % 2]

        def em_sq(e, c=c, o=o):
            return e.activation(A(C, o, W), get_x(c), AF.Square)
        P.add("scalar", em_sq, reads=list(reads_x), writes=BS(C, o, W),
              tag=f"{tag}sq{c}")

        def em_mm(e, c=c, o=o):
            return e.matmul(C.pb[b1][0:1, 0:W], C.ones_col[:, :], A(C, o, W),
                            start=(c == 0), stop=(c == 15))
        P.add("tensor", em_mm, reads=BS(C, o, W) + [C.Rconst],
              writes=[C.Rpb[b1]], tag=f"{tag}sm{c}")

    mean = C.rows[0:1, 0:W]
    m2 = C.rows[0:1, 512:512 + W]
    var = C.rows[0:1, 1024:1024 + W]
    rstd = C.rows[0:1, 1536:1536 + W]
    rstd_bf = C.rows_bf[0:1, 0:W]
    nmr_bf = C.rows_bf[0:1, 512:512 + W]

    P.add("vector", lambda e: e.tensor_scalar(mean, C.pb[b0][0:1, 0:W],
                                              float(1.0 / 2048), None, OP.mult),
          reads=[C.Rpb[b0]], writes=[C.Rrows], tag=tag + "mn")
    P.add("vector", lambda e: e.tensor_tensor(m2, mean, mean, OP.mult),
          reads=[C.Rrows], writes=[C.Rrows], tag=tag + "m2")
    P.add("vector", lambda e: e.scalar_tensor_tensor(var, C.pb[b1][0:1, 0:W],
                                                     float(1.0 / 2048), m2,
                                                     OP.mult, OP.subtract),
          reads=[C.Rpb[b1], C.Rrows], writes=[C.Rrows], tag=tag + "vr")
    P.add("scalar", lambda e: e.activation(m2, var, AF.Ln, bias=f32(1e-5)),
          reads=[C.Rrows], writes=[C.Rrows], tag=tag + "ln")
    P.add("scalar", lambda e: e.activation(rstd, m2, AF.Exp, scale=f32(-0.5)),
          reads=[C.Rrows], writes=[C.Rrows], tag=tag + "rs")
    P.add("vector", lambda e: e.scalar_tensor_tensor(var, mean, float(-1.0),
                                                     rstd, OP.mult, OP.mult),
          reads=[C.Rrows], writes=[C.Rrows], tag=tag + "nm")
    P.add("vector", lambda e: e.tensor_copy(rstd_bf, rstd),
          reads=[C.Rrows], writes=[C.Rrows], tag=tag + "cr")
    P.add("vector", lambda e: e.tensor_copy(nmr_bf, var),
          reads=[C.Rrows], writes=[C.Rrows], tag=tag + "cm")

    P.add("tensor", lambda e: e.matmul(C.pb[b0][:, 0:W], C.ones_row[:, :],
                                       rstd_bf, start=True, stop=True),
          reads=[C.Rrows, C.Rconst], writes=[C.Rpb[b0]], tag=tag + "br")
    P.add("tensor", lambda e: e.matmul(C.pb[b1][:, 0:W], C.ones_row[:, :],
                                       nmr_bf, start=True, stop=True),
          reads=[C.Rrows, C.Rconst], writes=[C.Rpb[b1]], tag=tag + "bm")
    P.add("scalar", lambda e: e.copy(A(C, rbco, W), C.pb[b0][:, 0:W]),
          reads=[C.Rpb[b0]], writes=BS(C, rbco, W), tag=tag + "er")
    P.add("scalar", lambda e: e.copy(A(C, mbco, W), C.pb[b1][:, 0:W]),
          reads=[C.Rpb[b1]], writes=BS(C, mbco, W), tag=tag + "em")

    for c in range(16):
        o = x2o[c % 2]

        def em_t1(e, c=c, o=o):
            return e.tensor_tensor(A(C, o, W), get_x(c), A(C, rbco, W), OP.mult)
        P.add("vector", em_t1, reads=list(reads_x) + BS(C, rbco, W),
              writes=BS(C, o, W), tag=f"{tag}t1{c}")

        def em_t2(e, c=c, o=o):
            return e.tensor_tensor(A(C, lnx_off + 512 * c, W), A(C, o, W),
                                   A(C, mbco, W), OP.add)
        P.add("vector", em_t2, reads=BS(C, o, W) + BS(C, mbco, W),
              writes=BS(C, lnx_off + 512 * c, W), tag=f"{tag}t2{c}")



def emit_P1(C):
    P, di = C.P, C.di
    O_XBF, O_LNX, O_TMP, O_WK, O_KEV, O_WV, O_VEV = \
        0, 16384, 32768, 34816, 38912, 39936, 48128

    for sl in range(8):
        lnoff = O_LNX + (sl % 2) * 8192
        xoff = O_XBF + (sl % 2) * 8192

        def em_lx(e, sl=sl, xoff=xoff):
            out = []
            for c in range(16):
                out.append(e.dma_start(
                    out=A(C, xoff + 512 * c, 512),
                    in_=_dap(di["x_full"], c * 128 * 4096 + 512 * sl,
                             [[4096, 128], [1, 512]])))
            return out
        P.add("gpsimd", em_lx, writes=BS(C, xoff, 8192), ndma=16,
              tag=f"1lx{sl}")
        feed_casts(C, 8)

        emit_ln(C, lambda c, xoff=xoff: A(C, xoff + 512 * c, 512),
                BS(C, xoff, 8192), lnoff, O_TMP, (6, 7), f"1n{sl}")

        for oc in range(16):
            wo = O_WK + 2048 * (oc % 2)

            def em_wk(e, oc=oc, wo=wo):
                return e.dma_start(
                    out=A(C, wo, 2048),
                    in_=_dap(C.wB["wk"], oc * 262144, [[2048, 128], [1, 2048]]))
            P.add("sync", em_wk, reads=[C.RwB["wk"]], writes=BS(C, wo, 2048),
                  ndma=1, tag=f"1wk{sl}_{oc}")

            bank = oc % 4

            def em_mm(e, wo=wo, bank=bank, lnoff=lnoff):
                out = []
                for k in range(16):
                    out.append(e.matmul(C.pb[bank][:, :],
                                        A(C, wo + 128 * k, 128),
                                        A(C, lnoff + 512 * k, 512),
                                        start=(k == 0), stop=(k == 15)))
                return out
            P.add("tensor", em_mm,
                  reads=BS(C, wo, 2048) + BS(C, lnoff, 8192),
                  writes=[C.Rpb[bank]], tag=f"1km{sl}_{oc}")

            eo = O_KEV + 512 * (oc % 2)

            def em_ev(e, bank=bank, eo=eo):
                return e.tensor_copy(A(C, eo, 512), C.pb[bank][:, :])
            P.add("vector", em_ev, reads=[C.Rpb[bank]], writes=BS(C, eo, 512),
                  tag=f"1ke{sl}_{oc}")

            def em_st(e, oc=oc, eo=eo, sl=sl):
                return e.dma_start(
                    out=_dap(C.Kscr, oc * 128 * 4096 + 512 * sl,
                             [[4096, 128], [1, 512]]),
                    in_=A(C, eo, 512))
            P.add("gpsimd", em_st, reads=BS(C, eo, 512), writes=[C.RK],
                  ndma=1, tag=f"1ks{sl}_{oc}")

        for nt in range(4):
            for k in range(16):
                def em_wv(e, nt=nt, k=k):
                    return e.dma_start(
                        out=A(C, O_WV + 512 * k, 512),
                        in_=_dap(C.wB["wv"], k * 262144 + 512 * nt,
                                 [[2048, 128], [1, 512]]))
                P.add("sync", em_wv, reads=[C.RwB["wv"]],
                      writes=BS(C, O_WV + 512 * k, 512), ndma=1,
                      tag=f"1wv{sl}_{nt}_{k}")
            for blk in range(4):
                bank = 4 + blk % 2

                def em_vmm(e, blk=blk, bank=bank, lnoff=lnoff):
                    out = []
                    for k in range(16):
                        out.append(e.matmul(C.pb[bank][:, :],
                                            A(C, lnoff + 512 * k + 128 * blk, 128),
                                            A(C, O_WV + 512 * k, 512),
                                            start=(k == 0), stop=(k == 15)))
                    return out
                P.add("tensor", em_vmm,
                      reads=BS(C, O_WV, 8192) + BS(C, lnoff, 8192),
                      writes=[C.Rpb[bank]], tag=f"1vm{sl}_{nt}_{blk}")

                eo = O_VEV + 512 * (blk % 2)

                def em_vev(e, bank=bank, eo=eo):
                    return e.tensor_copy(A(C, eo, 512), C.pb[bank][:, :])
                P.add("vector", em_vev, reads=[C.Rpb[bank]],
                      writes=BS(C, eo, 512), tag=f"1ve{sl}_{nt}_{blk}")

                def em_vst(e, nt=nt, blk=blk, eo=eo, sl=sl):
                    off = (4 * nt) * 4096 * 128 + (512 * sl + 128 * blk) * 128
                    return e.dma_start(
                        out=_dap(C.Vscr, off,
                                 [[128, 128], [4096 * 128, 4], [1, 128]]),
                        in_=A(C, eo, 512))
                P.add("gpsimd", em_vst, reads=BS(C, eo, 512), writes=[C.RV],
                      ndma=1, tag=f"1vs{sl}_{nt}_{blk}")



def emit_P2(C):
    P, di = C.P, C.di
    O_MASK, O_LNX, O_QH, O_KB, O_VB, O_EXP, O_ATT, O_WQ, O_WO, O_TMP, \
        O_OUN, O_RBC, O_YT = (0, 4096, 12288, 13312, 21504, 29696, 31744,
                              39936, 41984, 44032, 46080, 47104, 48128)
    rden = C.af32[0:1, 4096:4608]
    rden_bf = C.rows_bf[0:1, 1024:1536]
    SB = P2_SB
    LAG = P2_LAG

    for t in range(4):
        PT = 8 * (t + 1)

        def em_mask(e, t=t):
            return e.dma_start(
                out=A(C, O_MASK, 4096),
                in_=_dap(di["masks"], t * 8 * 128 * 512,
                         [[512, 128], [65536, 8], [1, 512]]))
        P.add("sync", em_mask, writes=BS(C, O_MASK, 4096), ndma=1,
              tag=f"2mk{t}")

        emit_ln(C, lambda c, t=t: C.x_res[:, c, 512 * t:512 * t + 512],
                [C.Rxres[t]], O_LNX, O_TMP, (6, 7), f"2n{t}")

        for h in range(16):
            wqo = O_WQ + 2048 * (h % 2)

            def em_wq(e, h=h, wqo=wqo):
                return e.dma_start(
                    out=A(C, wqo, 2048),
                    in_=_dap(C.wB["wq"], h * 262144, [[2048, 128], [1, 2048]]))
            P.add("sync", em_wq, reads=[C.RwB["wq"]],
                  writes=BS(C, wqo, 2048), ndma=1, tag=f"2wq{t}_{h}")

            qb = h % 2

            def em_qmm(e, qb=qb, wqo=wqo):
                out = []
                for k in range(16):
                    out.append(e.matmul(C.pb[qb][:, :],
                                        A(C, wqo + 128 * k, 128),
                                        A(C, O_LNX + 512 * k, 512),
                                        start=(k == 0), stop=(k == 15)))
                return out
            P.add("tensor", em_qmm,
                  reads=BS(C, wqo, 2048) + BS(C, O_LNX, 8192),
                  writes=[C.Rpb[qb]], tag=f"2qm{t}_{h}")

            qo = O_QH + 512 * (h % 2)

            def em_qev(e, qb=qb, qo=qo):
                return e.tensor_copy(A(C, qo, 512), C.pb[qb][:, :])
            P.add("vector", em_qev, reads=[C.Rpb[qb]], writes=BS(C, qo, 512),
                  tag=f"2qe{t}_{h}")

            feed_casts(C, 3)
            ko = O_KB + 4096 * (h % 2)
            vo = O_VB + 4096 * (h % 2)

            def em_k(e, h=h, ko=ko, PT=PT):
                return e.dma_start(
                    out=A(C, ko, 128 * PT),
                    in_=_dap(C.Kscr, h * 128 * 4096, [[4096, 128], [1, 128 * PT]]))
            P.add("sync", em_k, reads=[C.RK], writes=BS(C, ko, 4096), ndma=1,
                  tag=f"2kl{t}_{h}")

            def em_v(e, h=h, vo=vo, PT=PT):
                return e.dma_start(
                    out=A(C, vo, 128 * PT),
                    in_=_dap(C.Vscr, h * 4096 * 128,
                             [[128, 128], [128 * 128, PT], [1, 128]]))
            P.add("sync", em_v, reads=[C.RV], writes=BS(C, vo, 4096), ndma=1,
                  tag=f"2vl{t}_{h}")

            tail = []

            def emit_dp(kc):
                eo, erd = tail[kc]

                def em_den(e, kc=kc, eo=eo, PT=PT):
                    return e.matmul(C.pb[4][0:1, :], C.ones_col[:, :],
                                    A(C, eo, 512),
                                    start=(kc == 0), stop=(kc == PT - 1))
                P.add("tensor", em_den, reads=erd + [C.Rconst],
                      writes=[C.Rpb[4]], tag=f"2d{t}_{h}_{kc}")

                def em_pv(e, kc=kc, eo=eo, vo=vo, PT=PT):
                    return e.matmul(C.pb[5][:, :], A(C, vo + 128 * kc, 128),
                                    A(C, eo, 512),
                                    start=(kc == 0), stop=(kc == PT - 1))
                P.add("tensor", em_pv, reads=erd + BS(C, vo + 128 * kc, 128),
                      writes=[C.Rpb[5]], tag=f"2p{t}_{h}_{kc}")

            for kc in range(PT):
                sb = SB[kc % len(SB)]

                def em_s(e, kc=kc, sb=sb, ko=ko, qo=qo):
                    return e.matmul(C.pb[sb][:, :], A(C, ko + 128 * kc, 128),
                                    A(C, qo, 512), start=True, stop=True)
                P.add("tensor", em_s,
                      reads=BS(C, ko + 128 * kc, 128) + BS(C, qo, 512),
                      writes=[C.Rpb[sb]], tag=f"2s{t}_{h}_{kc}")

                eo = O_EXP + 512 * (kc % len(SB))

                def em_exp(e, sb=sb, eo=eo):
                    return e.activation(A(C, eo, 512), C.pb[sb][:, :], AF.Exp,
                                        bias=EXP_SHIFT, scale=INV_SQRT_HD)
                P.add("scalar", em_exp, reads=[C.Rpb[sb]],
                      writes=BS(C, eo, 512), tag=f"2e{t}_{h}_{kc}")

                if kc >= 8 * t:
                    j = kc - 8 * t
                    mo = O_TMP + 512 * (kc % len(SB))

                    def em_msk(e, j=j, eo=eo, mo=mo):
                        return e.tensor_tensor(
                            A(C, mo, 512), A(C, eo, 512),
                            C.abf[:, O_MASK + 512 * j:O_MASK + 512 * j + 512],
                            OP.mult)
                    P.add("vector", em_msk,
                          reads=BS(C, eo, 512) + BS(C, O_MASK, 4096),
                          writes=BS(C, mo, 512), tag=f"2x{t}_{h}_{kc}")
                    tail.append((mo, BS(C, mo, 512)))
                else:
                    tail.append((eo, BS(C, eo, 512)))

                if kc >= LAG:
                    emit_dp(kc - LAG)
            for kc in range(max(0, PT - LAG), PT):
                emit_dp(kc)

            P.add("vector", lambda e: e.reciprocal(rden, C.pb[4][0:1, :]),
                  reads=[C.Rpb[4]], writes=[C.Rden] + FS(C, 4096, 512),
                  tag=f"2r{t}_{h}")
            P.add("vector", lambda e: e.tensor_copy(rden_bf, rden),
                  reads=[C.Rden], writes=[C.Rden], tag=f"2rc{t}_{h}")
            P.add("tensor", lambda e, qb=qb: e.matmul(C.pb[qb][:, :],
                                                      C.ones_row[:, :],
                                                      rden_bf, start=True,
                                                      stop=True),
                  reads=[C.Rden, C.Rconst], writes=[C.Rpb[qb]], tag=f"2b{t}_{h}")

            ro = O_RBC + 512 * (h % 2)
            oo = O_OUN + 512 * (h % 2)
            ao = O_ATT + 512 * h

            def em_rbc(e, ro=ro, qb=qb):
                return e.tensor_copy(A(C, ro, 512), C.pb[qb][:, :])
            P.add("vector", em_rbc, reads=[C.Rpb[qb]], writes=BS(C, ro, 512),
                  tag=f"2rb{t}_{h}")

            def em_oun(e, oo=oo):
                return e.tensor_copy(A(C, oo, 512), C.pb[5][:, :])
            P.add("vector", em_oun, reads=[C.Rpb[5]], writes=BS(C, oo, 512),
                  tag=f"2ou{t}_{h}")

            def em_att(e, ao=ao, oo=oo, ro=ro):
                return e.tensor_tensor(A(C, ao, 512), A(C, oo, 512),
                                       A(C, ro, 512), OP.mult)
            P.add("vector", em_att, reads=BS(C, oo, 512) + BS(C, ro, 512),
                  writes=BS(C, ao, 512), tag=f"2a{t}_{h}")

        for oc in range(16):
            woo = O_WO if oc % 2 == 0 else O_MASK

            def em_wo(e, oc=oc, woo=woo):
                return e.dma_start(
                    out=A(C, woo, 2048),
                    in_=_dap(C.wB["wo"], oc * 262144, [[2048, 128], [1, 2048]]))
            P.add("sync", em_wo, reads=[C.RwB["wo"]],
                  writes=BS(C, woo, 2048), ndma=1, tag=f"2wo{t}_{oc}")

            ob = oc % 2

            def em_omm(e, ob=ob, woo=woo):
                out = []
                for k in range(16):
                    out.append(e.matmul(C.pb[ob][:, :],
                                        A(C, woo + 128 * k, 128),
                                        A(C, O_ATT + 512 * k, 512),
                                        start=(k == 0), stop=(k == 15)))
                return out
            P.add("tensor", em_omm,
                  reads=BS(C, woo, 2048) + BS(C, O_ATT, 8192),
                  writes=[C.Rpb[ob]], tag=f"2om{t}_{oc}")

            yo = O_YT + 512 * (oc % 2)

            def em_yev(e, ob=ob, yo=yo):
                return e.tensor_copy(A(C, yo, 512), C.pb[ob][:, :])
            P.add("vector", em_yev, reads=[C.Rpb[ob]], writes=BS(C, yo, 512),
                  tag=f"2ye{t}_{oc}")

            def em_res(e, oc=oc, yo=yo, t=t):
                xs = C.x_res[:, oc, 512 * t:512 * t + 512]
                return e.tensor_tensor(xs, A(C, yo, 512), xs, OP.add)
            P.add("vector", em_res, reads=BS(C, yo, 512),
                  writes=[C.Rxres[t]], tag=f"2rs{t}_{oc}")



def emit_ffn(C, ph):
    P = C.P
    O_LNX, O_WG, O_WU, O_HSP, O_TMP, O_YT = 0, 8192, 12288, 14336, 47104, 49152
    O_WD = 0

    for tt in range(4):
        emit_ln(C, lambda c, tt=tt: C.x_res[:, c, 512 * tt:512 * tt + 512],
                [C.Rxres[tt]], O_LNX, O_TMP, (6, 7), f"{ph}n{tt}")

        for gc in range(16):
            for s in range(4):
                wgo = O_WG + 2048 * (s % 2)

                def em_wg(e, gc=gc, s=s, wgo=wgo):
                    return e.dma_start(
                        out=A(C, wgo, 2048),
                        in_=_dap(C.wB["wg"], (gc * 4 + s) * 262144,
                                 [[2048, 128], [1, 2048]]))
                P.add("sync", em_wg, reads=[C.RwB["wg"]],
                      writes=BS(C, wgo, 2048), ndma=1, tag=f"{ph}wg{tt}_{gc}_{s}")

                def em_wu(e, gc=gc, s=s):
                    return e.dma_start(
                        out=A(C, O_WU, 2048),
                        in_=_dap(C.wB["wu"], (gc * 4 + s) * 262144,
                                 [[2048, 128], [1, 2048]]))
                P.add("sync", em_wu, reads=[C.RwB["wu"]],
                      writes=BS(C, O_WU, 2048), ndma=1, tag=f"{ph}wu{tt}_{gc}_{s}")

                gb = s % 2
                ub = 2 + s % 2

                def em_gmm(e, wgo=wgo, gb=gb):
                    out = []
                    for k in range(16):
                        out.append(e.matmul(C.pb[gb][:, :],
                                            A(C, wgo + 128 * k, 128),
                                            A(C, O_LNX + 512 * k, 512),
                                            start=(k == 0), stop=(k == 15)))
                    return out
                P.add("tensor", em_gmm,
                      reads=BS(C, wgo, 2048) + BS(C, O_LNX, 8192),
                      writes=[C.Rpb[gb]], tag=f"{ph}gm{tt}_{gc}_{s}")

                def em_umm(e, ub=ub):
                    out = []
                    for k in range(16):
                        out.append(e.matmul(C.pb[ub][:, :],
                                            A(C, O_WU + 128 * k, 128),
                                            A(C, O_LNX + 512 * k, 512),
                                            start=(k == 0), stop=(k == 15)))
                    return out
                P.add("tensor", em_umm,
                      reads=BS(C, O_WU, 2048) + BS(C, O_LNX, 8192),
                      writes=[C.Rpb[ub]], tag=f"{ph}um{tt}_{gc}_{s}")

                so = 512 * (s % 2)

                def em_sil(e, gb=gb, so=so):
                    return e.activation(AF_(C, so, 512), C.pb[gb][:, :], AF.Silu)
                P.add("scalar", em_sil, reads=[C.Rpb[gb]],
                      writes=FS(C, so, 512), tag=f"{ph}si{tt}_{gc}_{s}")

                ho = 1024 + 512 * s

                def em_h(e, ub=ub, so=so, ho=ho):
                    return e.tensor_tensor(AF_(C, ho, 512), AF_(C, so, 512),
                                           C.pb[ub][:, :], OP.mult)
                P.add("vector", em_h, reads=FS(C, so, 512) + [C.Rpb[ub]],
                      writes=FS(C, ho, 512), tag=f"{ph}h{tt}_{gc}_{s}")

            for s in range(4):
                def em_abs(e, s=s):
                    return e.tensor_tensor(AF_(C, 3072 + 512 * s, 512),
                                           AF_(C, 1024 + 512 * s, 512),
                                           AF_(C, 1024 + 512 * s, 512),
                                           OP.mult)
                P.add("vector", em_abs, reads=FS(C, 1024 + 512 * s, 512),
                      writes=FS(C, 3072 + 512 * s, 512), tag=f"{ph}ab{tt}_{gc}_{s}")
            QO = 5120
            spec = [(QO, 3072, 3584, OP.max), (QO + 512, 3072, 3584, OP.min),
                    (QO + 1024, 4096, 4608, OP.max),
                    (QO + 1536, 4096, 4608, OP.min)]
            for do, o1, o2, op in spec:
                def em_p(e, do=do, o1=o1, o2=o2, op=op):
                    return e.tensor_tensor(AF_(C, do, 512), AF_(C, o1, 512),
                                           AF_(C, o2, 512), op)
                P.add("vector", em_p,
                      reads=FS(C, o1, 512) + FS(C, o2, 512),
                      writes=FS(C, do, 512), tag=f"{ph}mx{tt}_{gc}_{do}")
            P.add("vector", lambda e: e.tensor_tensor(
                AF_(C, QO + 2048, 512), AF_(C, QO, 512), AF_(C, QO + 1024, 512),
                OP.min),
                reads=FS(C, QO, 512) + FS(C, QO + 1024, 512),
                writes=FS(C, QO + 2048, 512), tag=f"{ph}t1{tt}_{gc}")
            P.add("vector", lambda e: e.tensor_tensor(
                AF_(C, QO, 512), AF_(C, QO + 512, 512), AF_(C, QO + 1536, 512),
                OP.max),
                reads=FS(C, QO + 512, 512) + FS(C, QO + 1536, 512),
                writes=FS(C, QO, 512), tag=f"{ph}t2{tt}_{gc}")
            P.add("vector", lambda e: e.tensor_tensor(
                AF_(C, QO + 2560, 512), AF_(C, QO + 2048, 512), AF_(C, QO, 512),
                OP.max),
                reads=FS(C, QO + 2048, 512) + FS(C, QO, 512),
                writes=FS(C, QO + 2560, 512), tag=f"{ph}m2{tt}_{gc}")
            for s in range(4):
                def em_ge(e, s=s):
                    return e.tensor_tensor(AF_(C, QO + 512, 512),
                                           AF_(C, 3072 + 512 * s, 512),
                                           AF_(C, QO + 2560, 512), OP.is_ge)
                P.add("vector", em_ge,
                      reads=FS(C, 3072 + 512 * s, 512) + FS(C, QO + 2560, 512),
                      writes=FS(C, QO + 512, 512), tag=f"{ph}ge{tt}_{gc}_{s}")

                hso = O_HSP + 512 * (4 * gc + s)

                if SPARSIFY:
                    def em_ap(e, s=s, hso=hso):
                        return e.tensor_tensor(A(C, hso, 512),
                                               AF_(C, 1024 + 512 * s, 512),
                                               AF_(C, QO + 512, 512), OP.mult)
                    P.add("vector", em_ap,
                          reads=FS(C, 1024 + 512 * s, 512) + FS(C, QO + 512, 512),
                          writes=BS(C, hso, 512), tag=f"{ph}sp{tt}_{gc}_{s}")
                else:
                    def em_ap(e, s=s, hso=hso):
                        return e.tensor_copy(A(C, hso, 512),
                                             AF_(C, 1024 + 512 * s, 512))
                    P.add("vector", em_ap,
                          reads=FS(C, 1024 + 512 * s, 512),
                          writes=BS(C, hso, 512), tag=f"{ph}sp{tt}_{gc}_{s}")

        for m in range(16):
            db = 4 + m % 2
            for half in range(2):
                wdo = O_WD + 4096 * ((2 * m + half) % 3)

                def em_wd(e, m=m, half=half, wdo=wdo):
                    return e.dma_start(
                        out=A(C, wdo, 4096),
                        in_=_dap(C.wB["wd"], m * 1048576 + 4096 * half,
                                 [[8192, 128], [1, 4096]]))
                P.add("sync", em_wd, reads=[C.RwB["wd"]],
                      writes=BS(C, wdo, 4096), ndma=1,
                      tag=f"{ph}wd{tt}_{m}_{half}")

                def em_dmm(e, wdo=wdo, db=db, half=half):
                    out = []
                    for i in range(32):
                        out.append(e.matmul(
                            C.pb[db][:, :],
                            A(C, wdo + 128 * i, 128),
                            A(C, O_HSP + 512 * (32 * half + i), 512),
                            start=(half == 0 and i == 0),
                            stop=(half == 1 and i == 31)))
                    return out
                P.add("tensor", em_dmm,
                      reads=BS(C, wdo, 4096) + BS(C, O_HSP + 16384 * half, 16384),
                      writes=[C.Rpb[db]], tag=f"{ph}dm{tt}_{m}_{half}")

            yo = O_YT + 512 * (m % 2)

            def em_ye(e, db=db, yo=yo):
                return e.tensor_copy(A(C, yo, 512), C.pb[db][:, :])
            P.add("vector", em_ye, reads=[C.Rpb[db]], writes=BS(C, yo, 512),
                  tag=f"{ph}ye{tt}_{m}")

            def em_res(e, m=m, yo=yo, tt=tt):
                xs = C.x_res[:, m, 512 * tt:512 * tt + 512]
                return e.tensor_tensor(xs, A(C, yo, 512), xs, OP.add)
            P.add("vector", em_res, reads=BS(C, yo, 512),
                  writes=[C.Rxres[tt]], tag=f"{ph}rs{tt}_{m}")



def emit_P4(C):
    P, di = C.P, C.di
    O_LNX, O_KT, O_VT, O_WM, O_TMP = 0, 8192, 16384, 24576, 26624

    P.add("vector", lambda e: e.memset(AF_(C, 0, 2048), 0.0),
          writes=FS(C, 0, 2048), tag="4bd0")
    P.add("vector", lambda e: e.memset(AF_(C, 2048, 16), 0.0),
          writes=FS(C, 2048, 512), tag="4ks0")

    for tt in range(4):
        emit_ln(C, lambda c, tt=tt: C.x_res[:, c, 512 * tt:512 * tt + 512],
                [C.Rxres[tt]], O_LNX, O_TMP, (6, 7), f"4n{tt}")

        for nt in range(4):
            for k in range(16):
                wko = O_WM + 512 * (k % 2)
                wvo = O_WM + 1024 + 512 * (k % 2)

                def em_wk(e, k=k, nt=nt, wko=wko):
                    return e.dma_start(
                        out=A(C, wko, 512),
                        in_=_dap(C.wB["wkb"], k * 262144 + 512 * nt,
                                 [[2048, 128], [1, 512]]))
                P.add("sync", em_wk, reads=[C.RwB["wkb"]],
                      writes=BS(C, wko, 512), ndma=1, tag=f"4wk{tt}_{nt}_{k}")

                def em_wv(e, k=k, nt=nt, wvo=wvo):
                    return e.dma_start(
                        out=A(C, wvo, 512),
                        in_=_dap(C.wB["wvb"], k * 262144 + 512 * nt,
                                 [[2048, 128], [1, 512]]))
                P.add("sync", em_wv, reads=[C.RwB["wvb"]],
                      writes=BS(C, wvo, 512), ndma=1, tag=f"4wv{tt}_{nt}_{k}")

                def em_kvmm(e, k=k, wko=wko, wvo=wvo):
                    out = []
                    for blk in range(4):
                        lhs = A(C, O_LNX + 512 * k + 128 * blk, 128)
                        out.append(e.matmul(C.pb[blk][:, :], lhs, A(C, wko, 512),
                                            start=(k == 0), stop=(k == 15)))
                        out.append(e.matmul(C.pb[4 + blk][:, :], lhs, A(C, wvo, 512),
                                            start=(k == 0), stop=(k == 15)))
                    return out
                P.add("tensor", em_kvmm,
                      reads=BS(C, wko, 512) + BS(C, wvo, 512) + BS(C, O_LNX, 8192),
                      writes=[C.Rpb[i] for i in range(8)],
                      tag=f"4kv{tt}_{nt}_{k}")

            for blk in range(4):
                kto = O_KT + 2048 * blk + 512 * nt
                vto = O_VT + 2048 * blk + 512 * nt

                def em_kev(e, blk=blk, kto=kto):
                    return e.tensor_relu(A(C, kto, 512), C.pb[blk][:, :])
                P.add("vector", em_kev, reads=[C.Rpb[blk]],
                      writes=BS(C, kto, 512), tag=f"4ke{tt}_{nt}_{blk}")

                def em_vev(e, blk=blk, vto=vto):
                    return e.tensor_copy(A(C, vto, 512), C.pb[4 + blk][:, :])
                P.add("vector", em_vev, reads=[C.Rpb[4 + blk]],
                      writes=BS(C, vto, 512), tag=f"4ve{tt}_{nt}_{blk}")

        for grp in range(16):
            vb = grp % 2

            def em_vk(e, grp=grp, vb=vb):
                out = []
                for blk in range(4):
                    out.append(e.matmul(
                        C.pb[vb][:, 0:128],
                        A(C, O_KT + 2048 * blk + 128 * grp, 128),
                        A(C, O_VT + 2048 * blk + 128 * grp, 128),
                        start=(blk == 0), stop=(blk == 3)))
                return out
            P.add("tensor", em_vk,
                  reads=BS(C, O_KT, 8192) + BS(C, O_VT, 8192),
                  writes=[C.Rpb[vb]], tag=f"4vk{tt}_{grp}")

            def em_vka(e, grp=grp, vb=vb):
                dst = AF_(C, 128 * grp, 128)
                return e.tensor_tensor(dst, C.pb[vb][:, 0:128], dst, OP.add)
            P.add("vector", em_vka, reads=[C.Rpb[vb]] + FS(C, 128 * grp, 128),
                  writes=FS(C, 128 * grp, 128), tag=f"4va{tt}_{grp}")

            def em_ks(e, grp=grp):
                out = []
                for blk in range(4):
                    out.append(e.matmul(
                        C.pb[2][:, grp:grp + 1],
                        A(C, O_KT + 2048 * blk + 128 * grp, 128),
                        C.ones_col[:, :],
                        start=(blk == 0), stop=(blk == 3)))
                return out
            P.add("tensor", em_ks, reads=BS(C, O_KT, 8192) + [C.Rconst],
                  writes=[C.Rpb[2]], tag=f"4ks{tt}_{grp}")

        def em_ksa(e):
            dst = AF_(C, 2048, 16)
            return e.tensor_tensor(dst, C.pb[2][:, 0:16], dst, OP.add)
        P.add("vector", em_ksa, reads=[C.Rpb[2]] + FS(C, 2048, 512),
              writes=FS(C, 2048, 512), tag=f"4ka{tt}")

    for grp in range(16):
        def em_m(e, grp=grp):
            dst = AF_(C, 128 * grp, 128)
            return e.tensor_tensor(dst, dst, C.bdf_sb[:, :], OP.mult)
        P.add("vector", em_m, reads=[C.Rconst] + FS(C, 128 * grp, 128),
              writes=FS(C, 128 * grp, 128), tag=f"4bm{grp}")

    def em_ex(g):
        with (
            g.register("p") as p,
            g.register("off") as off,
            g.register("cnt") as cnt,
            g.register("cond") as cond,
            g.register("fl") as fl,
        ):
            g.reg_load(p, C.par_sb[:1, :1])
            g.reg_mov(off, 0)
            g.reg_add(off, off, p)
            g.reg_mul(off, off, 128 * 2064)
            g.dma_start(bass.AP(C.sh_ex, off, [[2064, 128], [1, 2048]]),
                        AF_(C, 0, 2048)).then_inc(C.s_poll, 16)
            g.reg_add(off, off, 2048)
            g.dma_start(bass.AP(C.sh_ex, off, [[2064, 128], [1, 16]]),
                        AF_(C, 2048, 16)).then_inc(C.s_poll, 16)
            g.wait_ge(C.s_poll, 32)
            g.reg_mov(off, 0)
            g.reg_add(off, off, p)
            g.reg_mul(off, off, 16)
            g.dma_start(bass.AP(C.sh_fl, off, [[16, 1], [1, 16]]),
                        C.flag_sb[:, :]).then_inc(C.s_poll, 16)
            g.wait_ge(C.s_poll, 48)
            g.reg_mov(off, 1)
            g.reg_sub(off, off, p)
            g.reg_mul(off, off, 16)
            g.reg_mov(cnt, 48)
            g.reg_mov(cond, 1)
            with g.While(cond):
                g.reg_add(cnt, cnt, 16)
                g.dma_start(C.pfl_sb[:, :],
                            bass.AP(C.sh_fl, off, [[16, 1], [1, 16]])
                            ).then_inc(C.s_poll, 16)
                g.wait_ge(C.s_poll, cnt)
                g.reg_load(fl, C.pfl_sb[:1, 0:1])
                g.reg_mov(cond, fl)
                g.reg_sub(cond, cond, MAGIC1)
            g.reg_mov(cond, 1)
            with g.While(cond):
                g.reg_add(cnt, cnt, 16)
                g.dma_start(C.pfl_sb[:, :],
                            bass.AP(C.sh_fl, off, [[16, 1], [1, 16]])
                            ).then_inc(C.s_poll, 16)
                g.wait_ge(C.s_poll, cnt)
                g.reg_load(fl, C.pfl_sb[:1, 1:2])
                g.reg_mov(cond, fl)
                g.reg_sub(cond, cond, MAGIC2)
            g.reg_mov(off, 1)
            g.reg_sub(off, off, p)
            g.reg_mul(off, off, 128 * 2064)
            r1 = g.dma_start(AF_(C, 2560, 2048),
                             bass.AP(C.sh_ex, off, [[2064, 128], [1, 2048]]))
            g.reg_add(off, off, 2048)
            r2 = g.dma_start(AF_(C, 4608, 16),
                             bass.AP(C.sh_ex, off, [[2064, 128], [1, 16]]))
            return [r1, r2]
    P.add("gpsimd", em_ex,
          reads=FS(C, 0, 2560) + [C.Rconst],
          writes=FS(C, 2560, 2560), ndma=2, tag="4ex")

    O_BDT, O_KST, O_LNX2, O_QT, O_B, O_ATT2, O_WQ2, O_WP2, O_YT2, O_TMP2 = \
        0, 2048, 2560, 10752, 11776, 12800, 20992, 23040, 25088, 26112

    P.add("vector", lambda e: e.tensor_tensor(A(C, O_BDT, 2048),
                                              AF_(C, 0, 2048),
                                              AF_(C, 2560, 2048), OP.add),
          reads=FS(C, 0, 2048) + FS(C, 2560, 2048),
          writes=BS(C, O_BDT, 2048), tag="4tot")
    P.add("vector", lambda e: e.tensor_tensor(
        AF_(C, 7168, 16),
        AF_(C, 2048, 16), AF_(C, 4608, 16), OP.add),
        reads=FS(C, 2048, 512) + FS(C, 4608, 512),
        writes=FS(C, 7168, 512), tag="4kt")

    for tt in range(4):
        emit_ln(C, lambda c, tt=tt: C.x_res[:, c, 512 * tt:512 * tt + 512],
                [C.Rxres[tt]], O_LNX2, O_TMP2, (6, 7), f"4m{tt}")

        for grp in range(16):
            wqo2 = O_WQ2 + 7168 * (grp % 2)

            def em_wq(e, grp=grp, wqo2=wqo2):
                return e.dma_start(
                    out=A(C, wqo2, 2048),
                    in_=_dap(C.wB["wqb"], grp * 262144, [[2048, 128], [1, 2048]]))
            P.add("sync", em_wq, reads=[C.RwB["wqb"]],
                  writes=BS(C, wqo2, 2048), ndma=1, tag=f"4wq{tt}_{grp}")

            qb = grp % 2

            def em_qmm(e, qb=qb, wqo2=wqo2):
                out = []
                for k in range(16):
                    out.append(e.matmul(C.pb[qb][:, :],
                                        A(C, wqo2 + 128 * k, 128),
                                        A(C, O_LNX2 + 512 * k, 512),
                                        start=(k == 0), stop=(k == 15)))
                return out
            P.add("tensor", em_qmm,
                  reads=BS(C, wqo2, 2048) + BS(C, O_LNX2, 8192),
                  writes=[C.Rpb[qb]], tag=f"4qm{tt}_{grp}")

            qo = O_QT + 512 * (grp % 2)

            def em_qev(e, qb=qb, qo=qo):
                return e.tensor_relu(A(C, qo, 512), C.pb[qb][:, :])
            P.add("vector", em_qev, reads=[C.Rpb[qb]], writes=BS(C, qo, 512),
                  tag=f"4qe{tt}_{grp}")

            bo = O_B + 512 * (grp % 2)

            def em_b(e, grp=grp, qo=qo, bo=bo):
                return e.tensor_scalar(
                    A(C, bo, 512), A(C, qo, 512),
                    C.af32[:, 7168 + grp:7168 + grp + 1],
                    None, OP.mult)
            P.add("vector", em_b,
                  reads=BS(C, qo, 512) + FS(C, 7168, 512),
                  writes=BS(C, bo, 512), tag=f"4b{tt}_{grp}")

            db = 2 + grp % 2

            def em_den(e, bo=bo, db=db):
                return e.matmul(C.pb[db][:, :], C.bdb_sb[:, :], A(C, bo, 512),
                                start=True, stop=True)
            P.add("tensor", em_den, reads=BS(C, bo, 512) + [C.Rconst],
                  writes=[C.Rpb[db]], tag=f"4dn{tt}_{grp}")

            ob = 4 + grp % 2

            def em_ov(e, grp=grp, qo=qo, ob=ob):
                return e.matmul(C.pb[ob][:, :], A(C, O_BDT + 128 * grp, 128),
                                A(C, qo, 512), start=True, stop=True)
            P.add("tensor", em_ov,
                  reads=BS(C, O_BDT + 128 * grp, 128) + BS(C, qo, 512),
                  writes=[C.Rpb[ob]], tag=f"4ov{tt}_{grp}")

            do = 5120 + 512 * (grp % 2)

            def em_eps(e, db=db, do=do):
                return e.activation(AF_(C, do, 512), C.pb[db][:, :],
                                    AF.Identity, bias=f32(1e-15))
            P.add("scalar", em_eps, reads=[C.Rpb[db]], writes=FS(C, do, 512),
                  tag=f"4ep{tt}_{grp}")

            ro = 6144 + 512 * (grp % 2)

            def em_rec(e, do=do, ro=ro):
                return e.reciprocal(AF_(C, ro, 512), AF_(C, do, 512))
            P.add("vector", em_rec, reads=FS(C, do, 512),
                  writes=FS(C, ro, 512), tag=f"4rc{tt}_{grp}")

            ao = O_ATT2 + 512 * grp

            def em_att(e, ob=ob, ro=ro, ao=ao):
                return e.tensor_tensor(A(C, ao, 512), C.pb[ob][:, :],
                                       AF_(C, ro, 512), OP.mult)
            P.add("vector", em_att, reads=[C.Rpb[ob]] + FS(C, ro, 512),
                  writes=BS(C, ao, 512), tag=f"4at{tt}_{grp}")

        for oc in range(16):
            wpo2 = O_WP2 + 7168 * (oc % 2)

            def em_wp(e, oc=oc, wpo2=wpo2):
                return e.dma_start(
                    out=A(C, wpo2, 2048),
                    in_=_dap(C.wB["wpb"], oc * 262144, [[2048, 128], [1, 2048]]))
            P.add("sync", em_wp, reads=[C.RwB["wpb"]],
                  writes=BS(C, wpo2, 2048), ndma=1, tag=f"4wp{tt}_{oc}")

            ob = oc % 2

            def em_pmm(e, ob=ob, wpo2=wpo2):
                out = []
                for k in range(16):
                    out.append(e.matmul(C.pb[ob][:, :],
                                        A(C, wpo2 + 128 * k, 128),
                                        A(C, O_ATT2 + 512 * k, 512),
                                        start=(k == 0), stop=(k == 15)))
                return out
            P.add("tensor", em_pmm,
                  reads=BS(C, wpo2, 2048) + BS(C, O_ATT2, 8192),
                  writes=[C.Rpb[ob]], tag=f"4pm{tt}_{oc}")

            yo = O_YT2 + 512 * (oc % 2)

            def em_yb(e, oc=oc, ob=ob, yo=yo):
                return e.tensor_scalar(A(C, yo, 512), C.pb[ob][:, :],
                                       C.biasb_sb[:, oc:oc + 1], None, OP.add)
            P.add("vector", em_yb, reads=[C.Rpb[ob], C.Rconst],
                  writes=BS(C, yo, 512), tag=f"4yb{tt}_{oc}")

            def em_res(e, oc=oc, yo=yo, tt=tt):
                xs = C.x_res[:, oc, 512 * tt:512 * tt + 512]
                return e.tensor_tensor(xs, A(C, yo, 512), xs, OP.add)
            P.add("vector", em_res, reads=BS(C, yo, 512),
                  writes=[C.Rxres[tt]], tag=f"4rs{tt}_{oc}")



def emit_dbg(C, what):
    P, di = C.P, C.di
    O_LNX, O_TMP = 0, 47104
    emit_ln(C, lambda c: C.x_res[:, c, 0:512], [C.Rxres[0]],
            O_LNX, O_TMP, (6, 7), "dbn")
    routs = [Res(f"dbg{i}") for i in range(16)]
    if what == "lnx":
        for c in range(16):
            def em(e, c=c):
                return e.dma_start(
                    out=_dap(di["y"], c * 128 * 2048, [[2048, 128], [1, 512]]),
                    in_=A(C, O_LNX + 512 * c, 512))
            P.add("gpsimd", em, reads=BS(C, O_LNX + 512 * c, 512),
                  writes=[routs[c]], ndma=1, tag=f"dbgo{c}")
        for r in range(4):
            def em2(e, r=r):
                return e.dma_start(
                    out=_dap(di["y"], r * 2048 + 512, [[2048, 1], [1, 512]]),
                    in_=C.rows[0:1, 512 * r:512 * r + 512])
            P.add("gpsimd", em2, reads=[C.Rrows], writes=[routs[4 + r]],
                  ndma=1, tag=f"dbgr{r}")
    else:
        for s in range(4):
            def em_wg(e, s=s):
                return e.dma_start(
                    out=A(C, 8192, 2048),
                    in_=_dap(C.wB["wg"], s * 262144, [[2048, 128], [1, 2048]]))
            P.add("sync", em_wg, reads=[C.RwB["wg"]], writes=BS(C, 8192, 2048),
                  ndma=1, tag=f"dbwg{s}")
            def em_wu(e, s=s):
                return e.dma_start(
                    out=A(C, 12288, 2048),
                    in_=_dap(C.wB["wu"], s * 262144, [[2048, 128], [1, 2048]]))
            P.add("sync", em_wu, reads=[C.RwB["wu"]], writes=BS(C, 12288, 2048),
                  ndma=1, tag=f"dbwu{s}")
            def em_gmm(e):
                out = []
                for k in range(16):
                    out.append(e.matmul(C.pb[0][:, :], A(C, 8192 + 128 * k, 128),
                                        A(C, O_LNX + 512 * k, 512),
                                        start=(k == 0), stop=(k == 15)))
                return out
            P.add("tensor", em_gmm, reads=BS(C, 8192, 2048) + BS(C, O_LNX, 8192),
                  writes=[C.Rpb[0]], tag=f"dbgm{s}")
            def em_umm(e):
                out = []
                for k in range(16):
                    out.append(e.matmul(C.pb[2][:, :], A(C, 12288 + 128 * k, 128),
                                        A(C, O_LNX + 512 * k, 512),
                                        start=(k == 0), stop=(k == 15)))
                return out
            P.add("tensor", em_umm, reads=BS(C, 12288, 2048) + BS(C, O_LNX, 8192),
                  writes=[C.Rpb[2]], tag=f"dbum{s}")
            def em_sil(e):
                return e.activation(AF_(C, 0, 512), C.pb[0][:, :], AF.Silu)
            P.add("scalar", em_sil, reads=[C.Rpb[0]], writes=FS(C, 0, 512),
                  tag=f"dbsi{s}")
            def em_h(e, s=s):
                return e.tensor_tensor(AF_(C, 1024 + 512 * s, 512), AF_(C, 0, 512),
                                       C.pb[2][:, :], OP.mult)
            P.add("vector", em_h, reads=FS(C, 0, 512) + [C.Rpb[2]],
                  writes=FS(C, 1024 + 512 * s, 512), tag=f"dbh{s}")
        for s in range(4):
            def em(e, s=s):
                return e.dma_start(
                    out=_dap(di["y"], s * 128 * 2048, [[2048, 128], [1, 512]]),
                    in_=AF_(C, 1024 + 512 * s, 512))
            P.add("gpsimd", em, reads=FS(C, 1024 + 512 * s, 512),
                  writes=[routs[s]], ndma=1, tag=f"dbgo{s}")
    P.add("gpsimd", None, reads=routs, tag="dbfin")



def emit_P6(C):
    P, di = C.P, C.di
    routs = [Res(f"out{c}") for c in range(16)]
    for c in range(16):
        def em(e, c=c):
            return e.dma_start(
                out=_dap(di["y"], c * 128 * 2048, [[2048, 128], [1, 2048]]),
                in_=C.x_res[:, c, :])
        P.add("gpsimd", em, reads=list(C.Rxres), writes=[routs[c]],
              ndma=1, tag=f"o{c}")
    P.add("gpsimd", None, reads=routs, tag="fin")



_NC_CACHE = {}


def kernel(**inputs):
    if "nc" not in _NC_CACHE:
        _NC_CACHE["nc"] = build_nc()
    nc = _NC_CACHE["nc"]
    in_maps = [_prep_core(inputs, core) for core in range(8)]
    res = run_bass_kernel_spmd(nc, in_maps, core_ids=list(range(8)))
    out = np.zeros((4, 4096, 2048), dtype=np.float32)
    for core in range(8):
        seq, par = core // 2, core % 2
        own = _own_blocks(par)
        own_tok = np.concatenate([np.arange(128 * g, 128 * g + 128) for g in own])
        y = np.asarray(res.results[core]["y"])
        out[seq][own_tok, :] = y.T
    return out
